# version 18
# baseline (speedup 1.0000x reference)
import numpy as np
import ml_dtypes
from contextlib import ExitStack
from functools import reduce as _reduce

import concourse.bass as bass
import concourse.mybir as mybir
from concourse.bass_utils import run_bass_kernel_spmd

F32 = mybir.dt.float32
BF16 = mybir.dt.bfloat16
I32 = mybir.dt.int32
AF = mybir.ActivationFunctionType
ALU = mybir.AluOpType

B, T, E, H, D, INNER = 8, 2048, 1024, 16, 64, 4096
NT = T // 128
NE = E // 128
NI = INNER // 128
TSB = 4
NSB = NT // TSB
KB = 1024
WQN = float(E * INNER)


def _rope_tables():
    import jax
    with jax.default_device(jax.devices("cpu")[0]):
        import jax.numpy as jnp
        freqs = 1.0 / 10000.0 ** (jnp.arange(0, D, 2, dtype=jnp.float32) / D)
        ang = jnp.arange(T, dtype=jnp.float32)[:, None] * freqs[None, :]
        c = np.asarray(jnp.cos(ang))
        s = np.asarray(jnp.sin(ang))
    C = np.empty((128, T), np.float32)
    S = np.empty((128, T), np.float32)
    for p in range(128):
        d = p % 64
        j = d // 2
        C[p] = c[:, j]
        S[p] = (s[:, j] if (d % 2 == 1) else -s[:, j])
    return C, S


class Buf:
    __slots__ = ("w", "readers", "war_extra")

    def __init__(self):
        self.w = None
        self.readers = {}
        self.war_extra = {}

    def inherit(self, *olds):
        for o in olds:
            if o.w:
                s, v = o.w
                self.war_extra[s] = max(self.war_extra.get(s, 0), v)
            for s, v in o.readers.items():
                self.war_extra[s] = max(self.war_extra.get(s, 0), v)
        return self


class Prog:
    ENGS = ("tensor", "scalar", "vector", "gpsimd", "sync")

    def __init__(self, nc, es):
        self.nc = nc
        self.q = {e: [] for e in self.ENGS}
        self.sem = {}
        self.cnt = {}
        self.wm = {}
        for s in ("t", "s", "v", "g", "ld", "lg", "st"):
            self.sem[s] = es.enter_context(nc.semaphore("sem_" + s))
            self.cnt[s] = 0

    def emit(self, eng, builder, reads=(), writes=(), sem=None, step=1,
             extra=()):
        waits = {}

        def add(dep):
            if dep:
                s, v = dep
                waits[s] = max(waits.get(s, 0), v)

        for b in reads:
            add(b.w)
        for b in writes:
            add(b.w)
            for s, v in b.readers.items():
                add((s, v))
            for s, v in b.war_extra.items():
                add((s, v))
        for d in extra:
            add(d)
        assert sem is not None
        self.cnt[sem] += step
        dep = (sem, self.cnt[sem])
        wlist = []
        for s, v in waits.items():
            if v > self.wm.get((eng, s), 0):
                self.wm[(eng, s)] = v
                wlist.append((self.sem[s], v))
        semh = self.sem[sem]

        def f(e, wlist=wlist, builder=builder, semh=semh, step=step):
            for h, v in wlist:
                e.wait_ge(h, v)
            builder(e).then_inc(semh, step)

        self.q[eng].append(f)
        for b in reads:
            s, v = dep
            b.readers[s] = max(b.readers.get(s, 0), v)
        for b in writes:
            b.w = dep
            b.readers = {}
            b.war_extra = {}
        return dep

    def wait(self, eng, dep):
        s, v = dep
        if v > self.wm.get((eng, s), 0):
            self.wm[(eng, s)] = v
            h = self.sem[s]
            self.q[eng].append(lambda e, h=h, v=v: e.wait_ge(h, v))


class Arena:
    def __init__(self, nc, size_bytes):
        self.nc = nc
        self.size = size_bytes
        self.tensors = []

    def at(self, name, off, shape, dtype):
        per = _reduce(lambda a, b: a * b, shape[1:]) * mybir.dt.size(dtype)
        assert off % 32 == 0, (name, off)
        assert off + per <= self.size, (name, off, per, self.size)
        h = self.nc.virtual_sbuf_tensor(name, shape, dtype)
        self.tensors.append((h, off))
        return h

    def allocate(self, es):
        self.handle = es.enter_context(
            self.nc.sbuf_tensor("arena", [128, self.size], mybir.dt.uint8))
        base = self.nc.lookup_mloc(self.handle).addr
        for h, off in self.tensors:
            self.nc._tensor(h.name, h.shape, h.dtype, type="SB")
            mloc = self.nc.lookup_mloc(h)
            mloc.allocated = not self.nc.target_bir_lowering
            mloc.addr = base + off


def build(debug=False):
    nc = bass.Bass("TRN2", target_bir_lowering=False)

    x_h = nc.dram_tensor("x", [T, E], F32, kind="ExternalInput").ap()
    wq_h = nc.dram_tensor("wqT", [E, 3 * E], F32, kind="ExternalInput").ap()
    b2_h = nc.dram_tensor("b2", [128, 24], F32, kind="ExternalInput").ap()
    wg_h = nc.dram_tensor("wgT", [E, INNER], F32, kind="ExternalInput").ap()
    wu_h = nc.dram_tensor("wuT", [E, INNER], F32, kind="ExternalInput").ap()
    wd_h = nc.dram_tensor("wdT", [INNER, E], F32, kind="ExternalInput").ap()
    ct_h = nc.dram_tensor("Ct", [128, T], F32, kind="ExternalInput").ap()
    st_h = nc.dram_tensor("St", [128, T], F32, kind="ExternalInput").ap()
    pm_h = nc.dram_tensor("Pm", [128, 128], BF16, kind="ExternalInput").ap()
    idb_h = nc.dram_tensor("Idb", [128, 128], BF16, kind="ExternalInput").ap()
    idf_h = nc.dram_tensor("Idf", [128, 128], F32, kind="ExternalInput").ap()
    or_h = nc.dram_tensor("OnesRow", [1, 128], F32, kind="ExternalInput").ap()
    out_h = nc.dram_tensor("out", [T, E], F32, kind="ExternalOutput").ap()
    tk = "ExternalOutput" if debug else "Internal"
    tg_h = nc.dram_tensor("ternG", [E, INNER], BF16, kind=tk).ap()
    tu_h = nc.dram_tensor("ternU", [E, INNER], BF16, kind=tk).ap()
    td_h = nc.dram_tensor("ternD", [INNER, E], BF16, kind=tk).ap()
    dbg = {}
    if debug:
        dbg["d_qT"] = nc.dram_tensor("d_qT", [E, T], BF16, kind="ExternalOutput").ap()
        dbg["d_kT"] = nc.dram_tensor("d_kT", [E, T], BF16, kind="ExternalOutput").ap()
        dbg["d_vtok"] = nc.dram_tensor("d_vtok", [T, 1040], BF16, kind="ExternalOutput").ap()
        dbg["d_h1"] = nc.dram_tensor("d_h1", [T, E], F32, kind="ExternalOutput").ap()
        dbg["d_k1T"] = nc.dram_tensor("d_k1T", [E, T], BF16, kind="ExternalOutput").ap()
        dbg["d_sca"] = nc.dram_tensor("d_sca", [1, 16], F32, kind="ExternalOutput").ap()
        for _m in range(3):
            dbg[f"d_wacc{_m}"] = nc.dram_tensor(f"d_wacc{_m}", [128, 1], F32,
                                                kind="ExternalOutput").ap()
        dbg["d_xatt"] = nc.dram_tensor("d_xatt", [T, E], F32, kind="ExternalOutput").ap()

    ar = Arena(nc, 206 * KB)
    A = ar.at

    _o = [0]
    def sm(name, shape, dtype):
        h = A(name, _o[0], shape, dtype)
        per = _reduce(lambda a, b: a * b, shape[1:]) * mybir.dt.size(dtype)
        _o[0] += (per + 31) & ~31
        return h
    PERM = sm("PERM", [128, 128], BF16)
    IDB = sm("IDB", [128, 128], BF16)
    IDF = sm("IDF", [128, 128], F32)
    ONESC = sm("ONESC", [128, 1], BF16)
    ONESR = sm("ONESR", [1, 128], F32)
    B2 = sm("B2", [128, 24], F32)
    CH2Q = sm("CH2Q", [1, 16], F32)
    CH2K = sm("CH2K", [1, 16], F32)
    CROW = sm("CROW", [1, 16], F32)
    CROWP = sm("CROWP", [1, 8], F32)
    CBIASP = sm("CBIASP", [128, 8], F32)
    S1INV = sm("S1INV", [128, NT], F32)
    S1S = sm("S1S", [128, NT], F32)
    ALG = sm("ALG", [128, 1], F32)
    ALUp = sm("ALUp", [128, 1], F32)
    ALD = sm("ALD", [128, 1], F32)
    RABG = sm("RABG", [128, 1], F32)
    RABU = sm("RABU", [128, 1], F32)
    RABD = sm("RABD", [128, 1], F32)
    WACC = sm("WACC", [128, 1], F32)
    SCA = sm("SCA", [1, 16], F32)
    C1285 = sm("C1285", [128, 1], F32)
    C05 = sm("C05", [128, 1], F32)
    CM05 = sm("CM05", [128, 1], F32)
    SGT = sm("SGT", [128, NT], F32)
    SUT = sm("SUT", [128, TSB], F32)
    S2E = sm("S2E", [128, TSB], F32)
    SDT = sm("SDT", [128, TSB], F32)
    STAT = sm("STAT", [128, 8], F32)
    STAT2 = sm("STAT2", [128, 8], F32)
    GSTAT = sm("GSTAT", [128, 2], F32)
    assert _o[0] <= 6 * KB, _o[0]

    CBT = A("CbT", 6 * KB, [128, T], BF16)
    SBT = A("SbT", 10 * KB, [128, T], BF16)
    PT = [A(f"PT{i}", 6 * KB + i * 2 * KB, [128, 1024], BF16) for i in range(4)]
    OBUF = [A(f"OB{i}", 86 * KB + i * 2 * KB, [65, 512], F32) for i in range(2)]

    CF = A("Cf", 14 * KB, [128, T], F32)
    SF = A("Sf", 22 * KB, [128, T], F32)
    WQ = [A(f"Wq{i}", 14 * KB + i * 6 * KB, [128, 3 * E], BF16) for i in range(NE)]
    XB = [A(f"xb{t}", 14 * KB + t * 4 * KB, [128, E], F32) for t in range(NT)]
    ARAW = [A(f"araw{k}", 14 * KB + k * 16 * KB, [128, INNER], F32)
            for k in range(TSB)]
    KT2V = [A(f"k2tv{k}", 14 * KB + k * 16 * KB + 8 * KB, [128, INNER], BF16)
            for k in range(TSB)]

    XT = [A(f"xT{i}", 62 * KB + i * 4 * KB, [128, T], BF16) for i in range(NE)]
    WST = [A(f"wst{i}", 78 * KB + i * 4 * KB, [128, 1024], F32) for i in range(2)]
    CBQ = A("CBQ", 78 * KB, [1, 2048], F32)
    CBK = A("CBK", 86 * KB, [1, 2048], F32)
    SQS = A("sqs", 6 * KB, [128, 1024], BF16)
    SQS2 = A("sqs2", 8 * KB, [128, 1024], BF16)
    MST = [A(f"mst{i}", 200 * KB + i * 2 * KB, [128, 1024], BF16) for i in range(2)]
    WA_ = A("wa", 90 * KB, [128, 1024], BF16)
    WB_ = A("wb", 92 * KB, [128, 1024], BF16)
    ZSCR = A("zscr", 78 * KB, [128, E], F32)
    ZI1 = A("zi1", 82 * KB, [128, E], I32)
    M1 = A("m1", 86 * KB, [128, E], BF16)

    QT = [A(f"qT{i}", 94 * KB + i * 4 * KB, [128, T], BF16) for i in range(NE)]
    K1T = [A(f"k1T{i}", 94 * KB + i * 4 * KB, [128, T], BF16) for i in range(NE)]
    KT = [A(f"kT{i}", 126 * KB + i * 4 * KB, [128, T], BF16) for i in range(NE)]
    H1B = [A(f"h1b{t}", 126 * KB + t * 2 * KB, [128, E], BF16) for t in range(NT)]
    VTOK = [A(f"vt{t}", 158 * KB + t * 2144, [128, 1040], BF16) for t in range(NT)]
    RING = [[A(f"rg{r}_{j}", (158 * KB + r * 16 * KB if r < 2 else 78 * KB)
               + j * 2 * KB, [128, 1024], BF16) for j in range(8)]
            for r in range(3)]
    XBF = [A(f"xbf{i}", 192 * KB + i * 4 * KB, [128, E], F32) for i in range(2)]
    VSCR = A("vscr", 192 * KB, [128, T], F32)
    GB = [A(f"gb{k}", 190 * KB + k * 2 * KB, [128, 1024], BF16)
          for k in range(TSB)]
    TOUT = [A(f"tout{i}", 200 * KB + i * 2 * KB, [128, 1024], BF16) for i in range(2)]
    WAV = A("wav", 204 * KB, [128, 1024], BF16)
    OUTB = [A(f"outb{o}", 198 * KB + o * 4 * KB, [128, E], F32) for o in range(2)]
    ZVF = A("zvf", 198 * KB, [128, 1024], F32)
    ZVI = A("zvi", 202 * KB, [128, 1024], I32)
    ZGF = A("zgf", 6 * KB, [128, 1024], F32)
    ZGI = A("zgi", 10 * KB, [128, 1024], I32)

    es = ExitStack()
    PSA = es.enter_context(nc.psum_tensor("psA", [128, 1024], F32))
    PSB2 = es.enter_context(nc.psum_tensor("psB", [128, 1024], F32))
    PSC = es.enter_context(nc.psum_tensor("psC", [128, 1024], F32))
    PSD = es.enter_context(nc.psum_tensor("psD", [128, 1024], F32))
    ar.allocate(es)
    P = Prog(nc, es)

    bufs = {}
    def bf(h):
        if id(h) not in bufs:
            bufs[id(h)] = Buf()
        return bufs[id(h)]

    PSBANK = {}
    def pbank(tensor, lo):
        key = (id(tensor), lo // 512)
        if key not in PSBANK:
            PSBANK[key] = Buf()
        return PSBANK[key]

    HB = {}
    def hbuf(name):
        if name not in HB:
            HB[name] = Buf()
        return HB[name]

    def dma_ld(dst_ap, src_ap, reads=(), writes=()):
        P.emit("sync", lambda e, d=dst_ap, s=src_ap: e.dma_start(d, s),
               reads=reads, writes=writes, sem="ld", step=16)

    def dma_st(dst_ap, src_ap, reads=(), writes=()):
        P.emit("sync", lambda e, d=dst_ap, s=src_ap: e.dma_start(d, s),
               reads=reads, writes=writes, sem="st", step=16)

    def gdma(dst_ap, src_ap, reads=(), writes=()):
        P.emit("gpsimd", lambda e, d=dst_ap, s=src_ap: e.dma_start(d, s),
               reads=reads, writes=writes, sem="lg", step=16)

    for dst, src in ((PERM, pm_h), (IDB, idb_h), (IDF, idf_h),
                     (ONESR, or_h), (B2, b2_h), (CF, ct_h), (SF, st_h)):
        dma_ld(dst[:], src[:], writes=[bf(dst)])
    P.emit("vector", lambda e: e.memset(ONESC[:], 1.0), writes=[bf(ONESC)], sem="v")
    P.emit("vector", lambda e: e.memset(C1285[:], 128.5), writes=[bf(C1285)], sem="v")
    P.emit("vector", lambda e: e.memset(C05[:], 0.5), writes=[bf(C05)], sem="v")
    P.emit("vector", lambda e: e.memset(CM05[:], -0.5), writes=[bf(CM05)], sem="v")
    P.emit("vector", lambda e: e.tensor_copy(CBT[:], CF[:]),
           reads=[bf(CF)], writes=[bf(CBT)], sem="v")
    P.emit("vector", lambda e: e.tensor_copy(SBT[:], SF[:]),
           reads=[bf(SF)], writes=[bf(SBT)], sem="v")
    for t in range(NT):
        P.emit("vector", lambda e, t=t: e.memset(VTOK[t][:], 1.0),
               writes=[bf(VTOK[t])], sem="v")

    for t in range(NT):
        xb = XBF[t % 2]
        dma_ld(xb[:], x_h[t * 128:(t + 1) * 128, :], writes=[bf(xb)])
        for i in range(NE):
            pp = (t * NE + i) % 2
            pslice = PSD[:, pp * 512: pp * 512 + 128]
            pb = pbank(PSD, pp * 512)
            P.emit("tensor",
                   lambda e, p=pslice, xb=xb, i=i:
                       e.transpose(p, xb[:, i * 128:(i + 1) * 128], IDF[:]),
                   reads=[bf(xb), bf(IDF)], writes=[pb], sem="t")
            P.emit("vector",
                   lambda e, p=pslice, i=i, t=t:
                       e.tensor_copy(XT[i][:, t * 128:(t + 1) * 128], p),
                   reads=[pb], writes=[bf(XT[i])], sem="v")

    NCH = 32

    def wslice(m, ch):
        wh = (wg_h, wu_h, wd_h)[m]
        if m < 2:
            i, q = ch // 4, ch % 4
            return wh[i * 128:(i + 1) * 128, q * 1024:(q + 1) * 1024]
        return wh[ch * 128:(ch + 1) * 128, :]

    def tslice(m, ch):
        th = (tg_h, tu_h, td_h)[m]
        if m < 2:
            i, q = ch // 4, ch % 4
            return th[i * 128:(i + 1) * 128, q * 1024:(q + 1) * 1024]
        return th[ch * 128:(ch + 1) * 128, :]

    mean_wl = []

    def emit_mean(m, ch):
        w = MST[ch % 2]
        gdma(w[:], wslice(m, ch), reads=[hbuf(f"w{m}")], writes=[bf(w)])
        if ch == 0:
            P.emit("vector", lambda e: e.memset(WACC[:], 0.0),
                   writes=[bf(WACC)], sem="v")
        P.emit("vector",
               lambda e, w=w: e.reduce_sum(GSTAT[:, 0:1], w[:],
                                           axis=mybir.AxisListType.X,
                                           apply_absolute_value=True),
               reads=[bf(w)], writes=[bf(GSTAT)], sem="v")
        P.emit("vector", lambda e: e.tensor_add(WACC[:], WACC[:], GSTAT[:, 0:1]),
               reads=[bf(GSTAT), bf(WACC)], writes=[bf(WACC)], sem="v")
        if ch == NCH - 1:
            pb = pbank(PSD, 0)
            P.emit("tensor", lambda e: e.transpose(PSD[0:1, 0:128], WACC[:], IDF[:]),
                   reads=[bf(WACC), bf(IDF)], writes=[pb], sem="t")
            P.emit("vector", lambda e: e.reduce_sum(SCA[0:1, 0:1], PSD[0:1, 0:128],
                                                    axis=mybir.AxisListType.X),
                   reads=[pb], writes=[bf(SCA)], sem="v")
            P.emit("vector", lambda e, m=m: e.tensor_copy(
                       SCA[0:1, 10 + m:11 + m], SCA[0:1, 0:1]),
                   reads=[bf(SCA)], writes=[bf(SCA)], sem="v")
            if debug:
                dma_st(dbg[f"d_wacc{m}"][:], WACC[:], reads=[bf(WACC)])
            P.emit("vector", lambda e, m=m: e.tensor_scalar(
                       SCA[0:1, 1 + m:2 + m], SCA[0:1, 0:1], 1.0 / WQN, 1e-5,
                       ALU.mult, ALU.max),
                   reads=[bf(SCA)], writes=[bf(SCA)], sem="v")
            P.emit("vector", lambda e, m=m: e.reciprocal(
                       SCA[0:1, 4 + m:5 + m], SCA[0:1, 1 + m:2 + m]),
                   reads=[bf(SCA)], writes=[bf(SCA)], sem="v")
            P.emit("vector", lambda e, m=m: e.tensor_single_scalar(
                       SCA[0:1, 7 + m:8 + m], SCA[0:1, 1 + m:2 + m], 0.5,
                       ALU.mult),
                   reads=[bf(SCA)], writes=[bf(SCA)], sem="v")
            for col, dst in ((7 + m, (ALG, ALUp, ALD)[m]),
                             (4 + m, (RABG, RABU, RABD)[m])):
                pb2 = pbank(PSD, 512)
                P.emit("tensor", lambda e, col=col: e.matmul(
                           PSD[0:128, 512:513], ONESR[:], SCA[0:1, col:col + 1],
                           start=True, stop=True),
                       reads=[bf(SCA), bf(ONESR)], writes=[pb2], sem="t")
                P.emit("vector", lambda e, dst=dst: e.tensor_copy(
                           dst[:], PSD[0:128, 512:513]),
                       reads=[pb2], writes=[bf(dst)], sem="v")

    for m in range(3):
        for ch in range(NCH):
            mean_wl.append((emit_mean, m, ch))

    def emit_quant(m, ch):
        w = WST[0]
        rab = (RABG, RABU, RABD)[m]
        to = TOUT[0]
        if m == 0 and ch == 0:
            bf(WST[0]).inherit(bf(XT[4]), bf(CBQ))
            bf(WST[1]).inherit(bf(XT[5]), bf(CBQ))
            bf(WA_).inherit(bf(XT[7]), bf(CBK))
            bf(WB_).inherit(bf(XT[7]), bf(CBK))
            bf(TOUT[0]).inherit(bf(MST[0]), bf(MST[1]))
            bf(TOUT[1]).inherit(bf(MST[1]))
        dma_ld(w[:], wslice(m, ch), reads=[hbuf(f"w{m}")], writes=[bf(w)])
        P.emit("scalar", lambda e, w=w, rab=rab: e.activation(
                   WA_[:], w[:], AF.Sign, bias=C05[:], scale=rab[:]),
               reads=[bf(w), bf(rab), bf(C05)], writes=[bf(WA_)], sem="s")
        P.emit("scalar", lambda e, w=w, rab=rab: e.activation(
                   WB_[:], w[:], AF.Sign, bias=CM05[:], scale=rab[:]),
               reads=[bf(w), bf(rab), bf(CM05)], writes=[bf(WB_)], sem="s")
        P.emit("vector", lambda e, to=to: e.tensor_tensor(
                   to[:], WA_[:], WB_[:], ALU.add),
               reads=[bf(WA_), bf(WB_)], writes=[bf(to)], sem="v")
        dma_st(tslice(m, ch), to[:], reads=[bf(to)], writes=[hbuf(f"t{m}")])

    def emit_quant_v(m, ch):
        w = WST[ch % 2]
        rab = (RABG, RABU, RABD)[m]
        to = TOUT[ch % 2]
        if m == 0 and ch == 0:
            bf(WST[0]).inherit(bf(XT[4]), bf(CBQ))
            bf(WST[1]).inherit(bf(XT[5]), bf(CBQ))
            bf(WA_).inherit(bf(XT[7]), bf(CBK))
            bf(WB_).inherit(bf(XT[7]), bf(CBK))
            bf(TOUT[0]).inherit(bf(MST[0]), bf(MST[1]))
            bf(TOUT[1]).inherit(bf(MST[1]))
        dma_ld(w[:], wslice(m, ch), reads=[hbuf(f"w{m}")], writes=[bf(w)])
        P.emit("vector", lambda e, w=w, rab=rab: e.tensor_scalar(
                   WAV[:], w[:], rab[:], 0.5, ALU.mult, ALU.is_ge),
               reads=[bf(w), bf(rab)], writes=[bf(WAV)], sem="v")
        P.emit("vector", lambda e, w=w, rab=rab, to=to: e.tensor_scalar(
                   to[:], w[:], rab[:], -0.5, ALU.mult, ALU.is_ge),
               reads=[bf(w), bf(rab)], writes=[bf(to)], sem="v")
        P.emit("vector", lambda e, to=to: e.tensor_tensor(
                   to[:], to[:], WAV[:], ALU.add),
               reads=[bf(to), bf(WAV)], writes=[bf(to)], sem="v")
        P.emit("vector", lambda e, to=to: e.tensor_scalar(
                   to[:], to[:], 2.0, -2.0, ALU.mult, ALU.add),
               reads=[bf(to)], writes=[bf(to)], sem="v")
        dma_st(tslice(m, ch), to[:], reads=[bf(to)], writes=[hbuf(f"t{m}")])

    wq_s = [(emit_quant, 0, ch) for ch in range(20)]
    wq_v = ([(emit_quant_v, 0, ch) for ch in range(20, NCH)]
            + [(emit_quant_v, m, ch) for m in (1, 2) for ch in range(NCH)])

    def drain(wl, n):
        for _ in range(min(n, len(wl))):
            fn, m, ch = wl.pop(0)
            fn(m, ch)

    bf(VSCR).inherit(bf(XBF[0]), bf(XBF[1]))
    for i in range(3):
        bf(WQ[i]).inherit(bf(CF), bf(SF))
    for i in range(NE):
        gdma(WQ[i][:], wq_h[i * 128:(i + 1) * 128, :], writes=[bf(WQ[i])])

    def emit_rope(tl):
        tb = bf(tl)
        for h in range(2):
            for lo in (0, 512):
                pb = pbank(PSC, lo)
                P.emit("tensor",
                       lambda e, lo=lo, h=h, tl=tl:
                           e.matmul(PSC[:, lo:lo + 512], PERM[:],
                                    tl[:, h * 1024 + lo:h * 1024 + lo + 512],
                                    start=True, stop=True),
                       reads=[bf(PERM), tb], writes=[pb], sem="t")
                P.emit("vector",
                       lambda e, lo=lo, h=h:
                           e.tensor_tensor(
                               WAV[:, lo:lo + 512], PSC[:, lo:lo + 512],
                               SBT[:, h * 1024 + lo:h * 1024 + lo + 512],
                               ALU.mult),
                       reads=[pb, bf(SBT)], writes=[bf(WAV)], sem="v")
            P.emit("vector",
                   lambda e, h=h, tl=tl:
                       e.tensor_tensor(tl[:, h * 1024:(h + 1) * 1024],
                                       tl[:, h * 1024:(h + 1) * 1024],
                                       CBT[:, h * 1024:(h + 1) * 1024],
                                       ALU.mult),
                   reads=[tb, bf(CBT)], writes=[tb], sem="v")
            P.emit("vector",
                   lambda e, h=h, tl=tl:
                       e.tensor_tensor(tl[:, h * 1024:(h + 1) * 1024],
                                       tl[:, h * 1024:(h + 1) * 1024],
                                       WAV[:], ALU.add),
                   reads=[tb, bf(WAV)], writes=[tb], sem="v")

    QKVPS = [(PSA, 0), (PSA, 512), (PSB2, 0), (PSB2, 512)]
    for ot in range(24):
        for i in range(NE):
            for tc in range(4):
                ps, lo = QKVPS[tc]
                P.emit("tensor",
                       lambda e, ps=ps, lo=lo, i=i, ot=ot, tc=tc:
                           e.matmul(ps[:, lo:lo + 512],
                                    WQ[i][:, ot * 128:(ot + 1) * 128],
                                    XT[i][:, tc * 512:(tc + 1) * 512],
                                    start=(i == 0), stop=(i == NE - 1)),
                       reads=[bf(WQ[i]), bf(XT[i])], writes=[pbank(ps, lo)],
                       sem="t")
        for tc in range(4):
            ps, lo = QKVPS[tc]
            if ot < 8:
                dst, db = QT[ot], bf(QT[ot])
            elif ot < 16:
                dst, db = KT[ot - 8], bf(KT[ot - 8])
            else:
                dst, db = VSCR, bf(VSCR)
            P.emit("scalar",
                   lambda e, ps=ps, lo=lo, dst=dst, ot=ot, tc=tc:
                       e.activation(dst[:, tc * 512:(tc + 1) * 512],
                                    ps[:, lo:lo + 512], AF.Identity,
                                    bias=B2[:, ot:ot + 1], scale=1.0),
                   reads=[pbank(ps, lo), bf(B2)], writes=[db], sem="s")
        if ot >= 16:
            h0 = 2 * (ot - 16)
            for kt in range(NT):
                pp = kt % 2
                pslice = PSD[:, pp * 512: pp * 512 + 128]
                pb = pbank(PSD, pp * 512)
                P.emit("tensor",
                       lambda e, p=pslice, kt=kt:
                           e.transpose(p, VSCR[:, kt * 128:(kt + 1) * 128], IDF[:]),
                       reads=[bf(VSCR), bf(IDF)], writes=[pb], sem="t")
                for hh in range(2):
                    P.emit("vector",
                           lambda e, p=pslice, kt=kt, h=h0 + hh, hh=hh:
                               e.tensor_copy(VTOK[kt][:, h * 65: h * 65 + 64],
                                             p[:, hh * 64:(hh + 1) * 64]),
                           reads=[pb], writes=[bf(VTOK[kt])], sem="v")
        if 1 <= ot <= 8:
            emit_rope(QT[ot - 1])
        elif 9 <= ot <= 16:
            emit_rope(KT[ot - 9])
        drain(mean_wl, 4)

    bf(CBQ).inherit(bf(XT[4]), bf(XT[5]))
    bf(SQS).inherit(bf(CBT))
    bf(SQS2).inherit(bf(CBT))
    CPS = [(PSC, 0), (PSC, 512), (PSD, 0), (PSD, 512)]
    for tiles, chrow in ((QT, CH2Q), (KT, CH2K)):
        for hp in range(NE):
            for hh in range(2):
                sq = (SQS, SQS2)[hh]
                P.emit("scalar",
                       lambda e, tl=tiles[hp], hh=hh, sq=sq:
                           e.activation(sq[:], tl[:, hh * 1024:(hh + 1) * 1024],
                                        AF.Square),
                       reads=[bf(tiles[hp])], writes=[bf(sq)], sem="s")
                for side in range(2):
                    for cc in range(2):
                        ps, lo = CPS[side * 2 + cc]
                        P.emit("tensor",
                               lambda e, ps=ps, lo=lo, cc=cc, side=side, sq=sq:
                                   e.matmul(ps[0:1, lo:lo + 512],
                                            ONESC[side * 64:(side + 1) * 64, 0:1],
                                            sq[side * 64:(side + 1) * 64,
                                                cc * 512:(cc + 1) * 512],
                                            start=True, stop=True),
                               reads=[bf(ONESC), bf(sq)],
                               writes=[pbank(ps, lo)], sem="t")
                for side in range(2):
                    for cc in range(2):
                        ps, lo = CPS[side * 2 + cc]
                        P.emit("vector",
                               lambda e, ps=ps, lo=lo, hh=hh, cc=cc, side=side:
                                   e.tensor_copy(
                                       CBQ[0:1, (hh * 2 + cc) * 512:
                                           (hh * 2 + cc + 1) * 512],
                                       ps[0:1, lo:lo + 512])
                                   if side == 0 else
                                   e.tensor_copy(
                                       CBK[0:1, (hh * 2 + cc) * 512:
                                           (hh * 2 + cc + 1) * 512],
                                       ps[0:1, lo:lo + 512]),
                               reads=[pbank(ps, lo)],
                               writes=[bf(CBQ) if side == 0 else bf(CBK)],
                               sem="v")
            P.emit("vector", lambda e, hp=hp, chrow=chrow: e.reduce_max(
                       chrow[0:1, 2 * hp:2 * hp + 1], CBQ[:],
                       axis=mybir.AxisListType.X),
                   reads=[bf(CBQ)], writes=[bf(chrow)], sem="v")
            P.emit("vector", lambda e, hp=hp, chrow=chrow: e.reduce_max(
                       chrow[0:1, 2 * hp + 1:2 * hp + 2], CBK[:],
                       axis=mybir.AxisListType.X),
                   reads=[bf(CBK)], writes=[bf(chrow)], sem="v")
    P.emit("vector", lambda e: e.tensor_tensor(CROW[:], CH2Q[:], CH2K[:],
                                               ALU.mult),
           reads=[bf(CH2Q), bf(CH2K)], writes=[bf(CROW)], sem="v")
    P.emit("scalar", lambda e: e.activation(CROW[:], CROW[:], AF.Sqrt,
                                            scale=1.0 / 64.0),
           reads=[bf(CROW)], writes=[bf(CROW)], sem="s")
    P.emit("vector", lambda e: e.tensor_tensor(CROWP[:], CROW[0:1, 0:16:2],
                                               CROW[0:1, 1:16:2], ALU.max),
           reads=[bf(CROW)], writes=[bf(CROWP)], sem="v")
    P.emit("vector", lambda e: e.tensor_single_scalar(CROWP[:], CROWP[:], -1.0,
                                                      ALU.mult),
           reads=[bf(CROWP)], writes=[bf(CROWP)], sem="v")
    pbX = pbank(PSD, 0)
    P.emit("tensor", lambda e: e.matmul(PSD[0:128, 0:8], ONESR[:], CROWP[:],
                                        start=True, stop=True),
           reads=[bf(ONESR), bf(CROWP)], writes=[pbX], sem="t")
    P.emit("vector", lambda e: e.tensor_copy(CBIASP[:], PSD[0:128, 0:8]),
           reads=[pbX], writes=[bf(CBIASP)], sem="v")

    def emit_ln(tile, tbuf, st, stb, sq, sqb):
        P.emit("vector", lambda e: e.reduce_sum(st[:, 0:1], tile[:],
                                                axis=mybir.AxisListType.X),
               reads=[tbuf], writes=[stb], sem="v")
        P.emit("scalar", lambda e: e.activation(sq, tile[:], AF.Square,
                                                accum_out=st[:, 1:2]),
               reads=[tbuf], writes=[stb] + list(sqb), sem="s")
        P.emit("vector", lambda e: e.tensor_single_scalar(
                   st[:, 2:3], st[:, 0:1], 1.0 / float(E), ALU.mult),
               reads=[stb], writes=[stb], sem="v")
        P.emit("vector", lambda e: e.tensor_tensor(st[:, 3:4], st[:, 2:3],
                                                   st[:, 2:3], ALU.mult),
               reads=[stb], writes=[stb], sem="v")
        P.emit("vector", lambda e: e.scalar_tensor_tensor(
                   st[:, 4:5], st[:, 1:2], 1.0 / float(E), st[:, 3:4],
                   ALU.mult, ALU.subtract),
               reads=[stb], writes=[stb], sem="v")
        P.emit("vector", lambda e: e.tensor_single_scalar(
                   st[:, 5:6], st[:, 4:5], 1e-5, ALU.add),
               reads=[stb], writes=[stb], sem="v")
        P.emit("scalar", lambda e: e.activation(st[:, 6:7], st[:, 5:6], AF.Sqrt),
               reads=[stb], writes=[stb], sem="s")
        P.emit("vector", lambda e: e.reciprocal(st[:, 7:8], st[:, 6:7]),
               reads=[stb], writes=[stb], sem="v")
        P.emit("vector", lambda e: e.tensor_tensor(st[:, 3:4], st[:, 7:8],
                                                   st[:, 7:8], ALU.mult),
               reads=[stb], writes=[stb], sem="v")
        P.emit("vector", lambda e: e.scalar_tensor_tensor(
                   st[:, 4:5], st[:, 5:6], 0.5, st[:, 3:4],
                   ALU.mult, ALU.mult),
               reads=[stb], writes=[stb], sem="v")
        P.emit("vector", lambda e: e.tensor_scalar(
                   st[:, 3:4], st[:, 4:5], -1.0, 1.5, ALU.mult, ALU.add),
               reads=[stb], writes=[stb], sem="v")
        P.emit("vector", lambda e: e.tensor_tensor(st[:, 6:7], st[:, 7:8],
                                                   st[:, 3:4], ALU.mult),
               reads=[stb], writes=[stb], sem="v")
        P.emit("vector", lambda e: e.tensor_scalar(
                   tile[:], tile[:], st[:, 2:3], st[:, 6:7],
                   ALU.subtract, ALU.mult),
               reads=[stb, tbuf], writes=[tbuf], sem="v")

    for t in range(NT):
        bf(XB[t]).inherit(*[bf(WQ[i]) for i in range(NE)],
                          *[bf(XT[i]) for i in range(4)])
        dma_ld(XB[t][:], x_h[t * 128:(t + 1) * 128, :], writes=[bf(XB[t])])
    for pt in PT:
        bf(pt).inherit(bf(CBT), bf(SBT), bf(SQS), bf(SQS2))
    for ob in OBUF:
        bf(ob).inherit(bf(XT[6]), bf(CBK))

    for hp in range(NE):
        h0, h1 = 2 * hp, 2 * hp + 1
        for qc in range(4):
            def emit_S(kt):
                ps = PSA if kt % 2 == 0 else PSB2
                for side in range(2):
                    lo = side * 512
                    P.emit("tensor",
                           lambda e, ps=ps, lo=lo, side=side, kt=kt, qc=qc, hp=hp:
                               e.matmul(ps[:, lo:lo + 512],
                                        KT[hp][side * 64:(side + 1) * 64,
                                               kt * 128:(kt + 1) * 128],
                                        QT[hp][side * 64:(side + 1) * 64,
                                               qc * 512:(qc + 1) * 512],
                                        start=True, stop=True,
                                        tile_position=(side * 64, 0)),
                           reads=[bf(KT[hp]), bf(QT[hp])],
                           writes=[pbank(ps, lo)], sem="t")

            def emit_exp(kt):
                ps = PSA if kt % 2 == 0 else PSB2
                pt = PT[kt % 4]
                P.emit("scalar",
                       lambda e, ps=ps, pt=pt, hp=hp:
                           e.activation(pt[:], ps[:], AF.Exp,
                                        bias=CBIASP[:, hp:hp + 1], scale=0.125),
                       reads=[pbank(ps, 0), pbank(ps, 512), bf(CBIASP)],
                       writes=[bf(pt)], sem="s")

            def emit_PV(kt):
                pt = PT[kt % 4]
                for side, h in ((0, h0), (1, h1)):
                    lo = side * 512
                    P.emit("tensor",
                           lambda e, pt=pt, lo=lo, h=h, kt=kt:
                               e.matmul(PSC[0:65, lo:lo + 512],
                                        VTOK[kt][:, h * 65: h * 65 + 65],
                                        pt[:, lo:lo + 512],
                                        start=(kt == 0), stop=(kt == NT - 1)),
                           reads=[bf(VTOK[kt]), bf(pt)],
                           writes=[pbank(PSC, lo)], sem="t")

            emit_S(0)
            emit_exp(0)
            for kt in range(1, NT):
                emit_S(kt)
                emit_PV(kt - 1)
                emit_exp(kt)
            emit_PV(NT - 1)

            for side, h in ((0, h0), (1, h1)):
                lo = side * 512
                ob = OBUF[side]
                P.emit("vector",
                       lambda e, ob=ob, lo=lo:
                           e.tensor_copy(ob[:], PSC[0:65, lo:lo + 512]),
                       reads=[pbank(PSC, lo)], writes=[bf(ob)], sem="v")
                for j in range(4):
                    lo2 = (j % 2) * 512
                    pb = pbank(PSD, lo2)
                    P.emit("tensor",
                           lambda e, ob=ob, j=j, lo2=lo2:
                               e.transpose(PSD[:, lo2:lo2 + 65],
                                           ob[0:65, j * 128:(j + 1) * 128],
                                           IDF[0:65, 0:65]),
                           reads=[bf(ob), bf(IDF)], writes=[pb], sem="t")
                    P.emit("vector",
                           lambda e, lo2=lo2:
                               e.reciprocal(STAT[:, 0:1], PSD[:, lo2 + 64:lo2 + 65]),
                           reads=[pb], writes=[bf(STAT)], sem="v")
                    tb = XB[qc * 4 + j]
                    P.emit("vector",
                           lambda e, lo2=lo2, tb=tb, h=h:
                               e.scalar_tensor_tensor(
                                   tb[:, h * 64:(h + 1) * 64],
                                   PSD[:, lo2:lo2 + 64], STAT[:, 0:1],
                                   tb[:, h * 64:(h + 1) * 64],
                                   ALU.mult, ALU.add),
                           reads=[pb, bf(STAT), bf(tb)], writes=[bf(tb)],
                           sem="v")
            drain(wq_s, 1)
            drain(wq_v, 3)
            if hp == NE - 1:
                for t in range(qc * 4, qc * 4 + 4):
                    emit_ln(XB[t], bf(XB[t]), STAT2, bf(STAT2),
                            SQS[:], [bf(SQS)])
    drain(wq_s, len(wq_s))
    drain(wq_v, len(wq_v))

    if debug:
        for i in range(NE):
            dma_st(dbg["d_qT"][i * 128:(i + 1) * 128, :], QT[i][:], reads=[bf(QT[i])])
            dma_st(dbg["d_kT"][i * 128:(i + 1) * 128, :], KT[i][:], reads=[bf(KT[i])])
        for t in range(NT):
            dma_st(dbg["d_vtok"][t * 128:(t + 1) * 128, :], VTOK[t][:],
                   reads=[bf(VTOK[t])])
            dma_st(dbg["d_xatt"][t * 128:(t + 1) * 128, :], XB[t][:],
                   reads=[bf(XB[t])])
        dma_st(dbg["d_sca"][:], SCA[:], reads=[bf(SCA)])

    if debug:
        for t in range(NT):
            dma_st(dbg["d_h1"][t * 128:(t + 1) * 128, :], XB[t][:],
                   reads=[bf(XB[t])])

    bf(ZSCR).inherit(bf(WST[0]))
    bf(ZI1).inherit(bf(WST[1]))
    bf(M1).inherit(bf(CBK), bf(OBUF[0]), bf(OBUF[1]))
    for i in range(NE):
        bf(K1T[i]).inherit(bf(QT[i]))
    for t in range(NT):
        bf(H1B[t]).inherit(bf(KT[t // 2]))

    for t in range(NT):
        P.emit("vector", lambda e, t=t: e.reduce_max(
                   STAT[:, 0:1], XB[t][:], axis=mybir.AxisListType.X,
                   apply_absolute_value=True),
               reads=[bf(XB[t])], writes=[bf(STAT)], sem="v")
        P.emit("vector", lambda e: e.tensor_single_scalar(
                   STAT[:, 1:2], STAT[:, 0:1], 1e-5, ALU.max),
               reads=[bf(STAT)], writes=[bf(STAT)], sem="v")
        P.emit("vector", lambda e: e.reciprocal(STAT[:, 2:3], STAT[:, 1:2]),
               reads=[bf(STAT)], writes=[bf(STAT)], sem="v")
        P.emit("vector", lambda e, t=t: e.tensor_single_scalar(
                   S1S[:, t:t + 1], STAT[:, 2:3], 127.0, ALU.mult),
               reads=[bf(STAT)], writes=[bf(S1S)], sem="v")
        P.emit("vector", lambda e, t=t: e.tensor_single_scalar(
                   S1INV[:, t:t + 1], STAT[:, 1:2], 1.0 / 127.0, ALU.mult),
               reads=[bf(STAT)], writes=[bf(S1INV)], sem="v")
        P.emit("scalar", lambda e, t=t: e.activation(
                   ZSCR[:], XB[t][:], AF.Identity, bias=C1285[:],
                   scale=S1S[:, t:t + 1]),
               reads=[bf(XB[t]), bf(S1S), bf(C1285)], writes=[bf(ZSCR)], sem="s")
        P.emit("vector", lambda e: e.tensor_copy(ZI1[:], ZSCR[:]),
               reads=[bf(ZSCR)], writes=[bf(ZI1)], sem="v")
        P.emit("vector", lambda e: e.scalar_tensor_tensor(
                   M1[:], ZSCR[:], 0.0, ZI1[:], ALU.add, ALU.is_lt),
               reads=[bf(ZSCR), bf(ZI1)], writes=[bf(M1)], sem="v")
        P.emit("vector", lambda e: e.scalar_tensor_tensor(
                   ZSCR[:], ZI1[:], -128.0, M1[:], ALU.add, ALU.subtract),
               reads=[bf(ZI1), bf(M1)], writes=[bf(ZSCR)], sem="v")
        for i in range(NE):
            lo = ((t * NE + i) % 2) * 512
            pb = pbank(PSD, lo)
            P.emit("tensor", lambda e, lo=lo, i=i: e.transpose(
                       PSD[:, lo:lo + 128], ZSCR[:, i * 128:(i + 1) * 128], IDF[:]),
                   reads=[bf(ZSCR), bf(IDF)], writes=[pb], sem="t")
            P.emit("vector", lambda e, lo=lo, i=i, t=t: e.tensor_copy(
                       K1T[i][:, t * 128:(t + 1) * 128], PSD[:, lo:lo + 128]),
                   reads=[pb], writes=[bf(K1T[i])], sem="v")
        P.emit("vector", lambda e, t=t: e.tensor_copy(H1B[t][:], XB[t][:]),
               reads=[bf(XB[t])], writes=[bf(H1B[t])], sem="v")

    if debug:
        for i in range(NE):
            dma_st(dbg["d_k1T"][i * 128:(i + 1) * 128, :], K1T[i][:],
                   reads=[bf(K1T[i])])

    for k in range(TSB):
        for t in range(4 * k, 4 * k + 4):
            bf(ARAW[k]).inherit(bf(XB[t]))
    bf(ARAW[0]).inherit(bf(CF), bf(SF))
    bf(ARAW[3]).inherit(bf(XT[0]), bf(XT[1]), bf(XT[2]), bf(XT[3]))
    for k in range(TSB):
        bufs[id(KT2V[k])] = bf(ARAW[k])
    for zg in (ZGF, ZGI):
        bf(zg).inherit(bf(CBT), bf(SBT), bf(PT[0]), bf(PT[1]),
                       bf(PT[2]), bf(PT[3]), bf(SQS), bf(SQS2))
    for j in range(8):
        for t in range(0, 9):
            bf(RING[0][j]).inherit(bf(VTOK[t]))
        for t in range(7, NT):
            bf(RING[1][j]).inherit(bf(VTOK[t]))
        bf(RING[2][j]).inherit(bf(ZSCR), bf(ZI1), bf(M1), bf(WA_), bf(WB_),
                               bf(WST[0]), bf(WST[1]), bf(OBUF[0]), bf(OBUF[1]),
                               bf(CBQ), bf(CBK),
                               *[bf(XT[i]) for i in range(4, NE)])
    for k in range(TSB):
        bf(GB[k]).inherit(bf(VTOK[14]), bf(VTOK[15]), bf(XBF[0]),
                          bf(XBF[1]), bf(VSCR))
    bf(OUTB[0]).inherit(bf(XBF[1]), bf(TOUT[0]), bf(MST[0]))
    bf(OUTB[1]).inherit(bf(TOUT[0]), bf(TOUT[1]), bf(WAV), bf(MST[1]))

    sched = []
    for sb in range(NSB):
        sched.append(("g", sb, 0))
        if sb > 0:
            for q in range(4):
                sched.append(("dA", sb - 1, q))
            for q in range(4):
                sched.append(("dB", sb - 1, q))
        sched.append(("u", sb, 0))
        for q in range(1, 4):
            sched.append(("g", sb, q))
            sched.append(("u", sb, q))
    for q in range(4):
        sched.append(("dA", NSB - 1, q))
    for q in range(4):
        sched.append(("dB", NSB - 1, q))

    def load_chunk(ci):
        kind, sb, q = sched[ci]
        slot = RING[ci % 3]
        hb = hbuf({"g": "t0", "u": "t1", "dA": "t2", "dB": "t2"}[kind])
        for j in range(8):
            if kind in ("dA", "dB"):
                src = td_h[(q * 8 + j) * 128:(q * 8 + j + 1) * 128, :]
            else:
                wh = tg_h if kind == "g" else tu_h
                src = wh[j * 128:(j + 1) * 128, q * 1024:(q + 1) * 1024]
            dma_ld(slot[j][:], src, reads=[hb], writes=[bf(slot[j])])

    def emit_scales(sb, k):
        gti = sb * TSB + k
        P.emit("vector", lambda e, k=k: e.reduce_max(
                   STAT2[:, 0:1], ARAW[k][:], axis=mybir.AxisListType.X,
                   apply_absolute_value=True),
               reads=[bf(ARAW[k])], writes=[bf(STAT2)], sem="v")
        P.emit("vector", lambda e, gti=gti, k=k: e.tensor_tensor(
                   SUT[:, k:k + 1], ALUp[:], S1INV[:, gti:gti + 1], ALU.mult),
               reads=[bf(ALUp), bf(S1INV)], writes=[bf(SUT)], sem="v")
        P.emit("vector", lambda e, k=k: e.tensor_tensor(
                   STAT2[:, 1:2], STAT2[:, 0:1], SUT[:, k:k + 1], ALU.mult),
               reads=[bf(STAT2), bf(SUT)], writes=[bf(STAT2)], sem="v")
        P.emit("vector", lambda e: e.tensor_single_scalar(
                   STAT2[:, 2:3], STAT2[:, 1:2], 1e-5, ALU.max),
               reads=[bf(STAT2)], writes=[bf(STAT2)], sem="v")
        P.emit("vector", lambda e: e.reciprocal(STAT2[:, 3:4], STAT2[:, 2:3]),
               reads=[bf(STAT2)], writes=[bf(STAT2)], sem="v")
        P.emit("vector", lambda e: e.tensor_single_scalar(
                   STAT2[:, 4:5], STAT2[:, 3:4], 127.0, ALU.mult),
               reads=[bf(STAT2)], writes=[bf(STAT2)], sem="v")
        P.emit("vector", lambda e, k=k: e.tensor_tensor(
                   S2E[:, k:k + 1], SUT[:, k:k + 1], STAT2[:, 4:5], ALU.mult),
               reads=[bf(SUT), bf(STAT2)], writes=[bf(S2E)], sem="v")
        P.emit("vector", lambda e: e.tensor_single_scalar(
                   STAT2[:, 5:6], STAT2[:, 2:3], 1.0 / 127.0, ALU.mult),
               reads=[bf(STAT2)], writes=[bf(STAT2)], sem="v")
        P.emit("vector", lambda e, k=k: e.tensor_tensor(
                   SDT[:, k:k + 1], ALD[:], STAT2[:, 5:6], ALU.mult),
               reads=[bf(ALD), bf(STAT2)], writes=[bf(SDT)], sem="v")

    def emit_rounds(sb, k):
        ab = bf(ARAW[k])
        for c in range(4):
            ar = ARAW[k][:, c * 1024:(c + 1) * 1024]
            zf, zi = (ZVF, ZVI) if c % 2 == 0 else (ZGF, ZGI)
            P.emit("vector", lambda e, ar=ar, k=k, zf=zf: e.tensor_scalar(
                       zf[:], ar, S2E[:, k:k + 1], 128.5, ALU.mult, ALU.add),
                   reads=[bf(S2E), ab], writes=[bf(zf)], sem="v")
            P.emit("vector", lambda e, zf=zf, zi=zi: e.tensor_copy(
                       zi[:], zf[:]),
                   reads=[bf(zf)], writes=[bf(zi)], sem="v")
            P.emit("vector", lambda e, zf=zf, zi=zi: e.tensor_tensor(
                       zf[:], zf[:], zi[:], ALU.is_lt),
                   reads=[bf(zf), bf(zi)], writes=[bf(zf)], sem="v")
            P.emit("vector", lambda e, k=k, c=c, zf=zf, zi=zi:
                       e.scalar_tensor_tensor(
                           ARAW[k][:, c * 1024:(c + 1) * 1024], zi[:],
                           -128.0, zf[:], ALU.add, ALU.subtract),
                   reads=[bf(zi), bf(zf)], writes=[ab], sem="v")

    def emit_tr(sb, k):
        ps = (PSC, PSD)[k % 2]
        ab = bf(ARAW[k])
        for g in range(7, -1, -1):
            lo = (g % 2) * 512
            pb = pbank(ps, lo)
            for w in range(4):
                oc = 4 * g + w
                P.emit("tensor",
                       lambda e, ps=ps, lo=lo, w=w, oc=oc, k=k: e.transpose(
                           ps[:, lo + w * 128:lo + (w + 1) * 128],
                           ARAW[k][:, oc * 128:(oc + 1) * 128], IDF[:]),
                       reads=[ab, bf(IDF)], writes=[pb], sem="t")
            P.emit("scalar", lambda e, ps=ps, lo=lo, g=g, k=k: e.copy(
                       KT2V[k][:, g * 512:(g + 1) * 512], ps[:, lo:lo + 512]),
                   reads=[pb], writes=[ab], sem="s")

    loaded = [False] * len(sched)

    def ensure_load(ci):
        if ci < len(sched) and not loaded[ci]:
            load_chunk(ci)
            loaded[ci] = True

    P.emit("vector", lambda e: e.tensor_scalar(
               SGT[:, 0:NT], S1INV[:, 0:NT], ALG[:], None, ALU.mult),
           reads=[bf(ALG), bf(S1INV)], writes=[bf(SGT)], sem="v")

    pend_tr = []

    for ci, (kind, sb, q) in enumerate(sched):
        for la in range(3):
            ensure_load(ci + la)
        slot = RING[ci % 3]
        if kind in ("g", "u"):
            def emit_gu_mm(k):
                gti = sb * TSB + k
                ps = (PSA, PSB2)[k % 2]
                for i in range(NE):
                    for oc2 in range(2):
                        lo = oc2 * 512
                        P.emit("tensor",
                               lambda e, ps=ps, lo=lo, i=i, gti=gti, slot=slot:
                                   e.matmul(ps[:, lo:lo + 512],
                                            K1T[i][:, gti * 128:(gti + 1) * 128],
                                            slot[i][:, lo:lo + 512],
                                            start=(i == 0), stop=(i == NE - 1)),
                               reads=[bf(K1T[i]), bf(slot[i])],
                               writes=[pbank(ps, lo)], sem="t")

            def emit_gu_epi(k):
                ps = (PSA, PSB2)[k % 2]
                if kind == "g":
                    gti2 = sb * TSB + k
                    P.emit("scalar", lambda e, ps=ps, k=k, gti2=gti2:
                               e.activation(GB[k][:], ps[:], AF.Silu,
                                            scale=SGT[:, gti2:gti2 + 1]),
                           reads=[pbank(ps, 0), pbank(ps, 512), bf(SGT)],
                           writes=[bf(GB[k])], sem="s")
                else:
                    P.emit("vector", lambda e, ps=ps, k=k, q=q: e.tensor_tensor(
                               ARAW[k][:, q * 1024:(q + 1) * 1024], GB[k][:],
                               ps[:], ALU.mult),
                           reads=[bf(GB[k]), pbank(ps, 0), pbank(ps, 512)],
                           writes=[bf(ARAW[k])], sem="v")

            if kind == "u" and q == 3:
                emit_gu_mm(0)
                emit_gu_mm(1)
                emit_gu_epi(0)
                emit_gu_epi(1)
                emit_gu_mm(2)
                emit_gu_mm(3)
                emit_gu_epi(2)
                emit_gu_epi(3)
                bf(ZVF).inherit(bf(OUTB[0]))
                bf(ZVI).inherit(bf(OUTB[1]))
                for k in range(TSB):
                    emit_scales(sb, k)
                    emit_rounds(sb, k)
                    pend_tr.append((sb, k))
            else:
                for k in range(TSB):
                    emit_gu_mm(k)
                    emit_gu_epi(k)
        else:
            if q == 0:
                for _ in range(2):
                    if pend_tr:
                        emit_tr(*pend_tr.pop(0))
            ks = (0, 1) if kind == "dA" else (2, 3)
            for k in ks:
                ps = (PSC, PSD)[k % 2]
                ab = bf(ARAW[k])
                for j in range(8):
                    oc = q * 8 + j
                    for c in range(2):
                        lo = c * 512
                        P.emit("tensor",
                               lambda e, ps=ps, lo=lo, oc=oc, k=k, slot=slot,
                                       j=j, q=q:
                                   e.matmul(ps[:, lo:lo + 512],
                                            KT2V[k][:, oc * 128:(oc + 1) * 128],
                                            slot[j][:, lo:lo + 512],
                                            start=(q == 0 and j == 0),
                                            stop=(q == 3 and j == 7)),
                               reads=[ab, bf(slot[j])],
                               writes=[pbank(ps, lo)], sem="t")
            if q == 3:
                for k in ks:
                    gti = sb * TSB + k
                    ps = (PSC, PSD)[k % 2]
                    ob_ = OUTB[k % 2]
                    if kind == "dA" and k == 0:
                        bf(OUTB[0]).inherit(bf(ZVF))
                        bf(OUTB[1]).inherit(bf(ZVI))
                    for c in range(2):
                        lo = c * 512
                        P.emit("vector",
                               lambda e, ps=ps, lo=lo, ob_=ob_, k=k, gti=gti:
                                   e.scalar_tensor_tensor(
                                       ob_[:, lo:lo + 512], ps[:, lo:lo + 512],
                                       SDT[:, k:k + 1],
                                       H1B[gti][:, lo:lo + 512],
                                       ALU.mult, ALU.add),
                               reads=[pbank(ps, lo), bf(SDT), bf(H1B[gti])],
                               writes=[bf(ob_)], sem="v")
                    emit_ln(ob_, bf(ob_), STAT2, bf(STAT2), ps[:, 0:1024],
                            [pbank(ps, 0), pbank(ps, 512)])
                    P.emit("gpsimd",
                           lambda e, ob_=ob_, gti=gti:
                               e.dma_start(out_h[gti * 128:(gti + 1) * 128, :],
                                           ob_[:]),
                           reads=[bf(ob_)], sem="st", step=16)

    P.q["sync"].append(lambda e, v=P.cnt["st"]: e.wait_ge(P.sem["st"], v))

    with nc.Block() as blk:
        @blk.tensor
        def _(eng):
            for f in P.q["tensor"]:
                f(eng)

        @blk.scalar
        def _(eng):
            for f in P.q["scalar"]:
                f(eng)

        @blk.vector
        def _(eng):
            for f in P.q["vector"]:
                f(eng)

        @blk.gpsimd
        def _(eng):
            for f in P.q["gpsimd"]:
                f(eng)

        @blk.sync
        def _(eng):
            for f in P.q["sync"]:
                f(eng)

    es.close()
    return nc


_CACHE = {}


def _get_nc(debug=False):
    if debug not in _CACHE:
        _CACHE[debug] = build(debug)
    return _CACHE[debug]


def make_inputs(L0_attn_x, L0_attn_W_qkv, L0_attn_b_qkv, L0_ffn_w_gate,
                L0_ffn_w_up, L0_ffn_w_down):
    C, S = _rope_tables()
    perm = np.zeros((128, 128), ml_dtypes.bfloat16)
    for i in range(128):
        perm[i ^ 1, i] = 1.0
    shared = {
        "wqT": np.ascontiguousarray(np.asarray(L0_attn_W_qkv).T),
        "b2": np.ascontiguousarray(np.asarray(L0_attn_b_qkv).reshape(24, 128).T),
        "wgT": np.ascontiguousarray(np.asarray(L0_ffn_w_gate).T),
        "wuT": np.ascontiguousarray(np.asarray(L0_ffn_w_up).T),
        "wdT": np.ascontiguousarray(np.asarray(L0_ffn_w_down).T),
        "Ct": C, "St": S,
        "Pm": perm,
        "Idb": np.eye(128, dtype=ml_dtypes.bfloat16),
        "Idf": np.eye(128, dtype=np.float32),
        "OnesRow": np.ones((1, 128), np.float32),
    }
    x = np.asarray(L0_attn_x)
    return [dict(shared, x=np.ascontiguousarray(x[c])) for c in range(B)]


def kernel(L0_attn_x, L0_attn_W_qkv, L0_attn_b_qkv, L0_ffn_w_gate,
           L0_ffn_w_up, L0_ffn_w_down):
    nc = _get_nc(False)
    in_maps = make_inputs(L0_attn_x, L0_attn_W_qkv, L0_attn_b_qkv,
                          L0_ffn_w_gate, L0_ffn_w_up, L0_ffn_w_down)
    res = run_bass_kernel_spmd(nc, in_maps, core_ids=list(range(B)))
    return np.stack([res.results[c]["out"] for c in range(B)], axis=0)


# version 19
# speedup vs baseline: 1.0020x; 1.0020x over previous
import numpy as np
import ml_dtypes
from contextlib import ExitStack
from functools import reduce as _reduce

import concourse.bass as bass
import concourse.mybir as mybir
from concourse.bass_utils import run_bass_kernel_spmd

F32 = mybir.dt.float32
BF16 = mybir.dt.bfloat16
I32 = mybir.dt.int32
AF = mybir.ActivationFunctionType
ALU = mybir.AluOpType

B, T, E, H, D, INNER = 8, 2048, 1024, 16, 64, 4096
NT = T // 128
NE = E // 128
NI = INNER // 128
TSB = 4
NSB = NT // TSB
KB = 1024
WQN = float(E * INNER)


def _rope_tables():
    import jax
    with jax.default_device(jax.devices("cpu")[0]):
        import jax.numpy as jnp
        freqs = 1.0 / 10000.0 ** (jnp.arange(0, D, 2, dtype=jnp.float32) / D)
        ang = jnp.arange(T, dtype=jnp.float32)[:, None] * freqs[None, :]
        c = np.asarray(jnp.cos(ang))
        s = np.asarray(jnp.sin(ang))
    C = np.empty((128, T), np.float32)
    S = np.empty((128, T), np.float32)
    for p in range(128):
        d = p % 64
        j = d // 2
        C[p] = c[:, j]
        S[p] = (s[:, j] if (d % 2 == 1) else -s[:, j])
    return C, S


class Buf:
    __slots__ = ("w", "readers", "war_extra")

    def __init__(self):
        self.w = None
        self.readers = {}
        self.war_extra = {}

    def inherit(self, *olds):
        for o in olds:
            if o.w:
                s, v = o.w
                self.war_extra[s] = max(self.war_extra.get(s, 0), v)
            for s, v in o.readers.items():
                self.war_extra[s] = max(self.war_extra.get(s, 0), v)
        return self


class Prog:
    ENGS = ("tensor", "scalar", "vector", "gpsimd", "sync")

    def __init__(self, nc, es):
        self.nc = nc
        self.q = {e: [] for e in self.ENGS}
        self.sem = {}
        self.cnt = {}
        self.wm = {}
        for s in ("t", "s", "v", "g", "ld", "lg", "st"):
            self.sem[s] = es.enter_context(nc.semaphore("sem_" + s))
            self.cnt[s] = 0

    def emit(self, eng, builder, reads=(), writes=(), sem=None, step=1,
             extra=()):
        waits = {}

        def add(dep):
            if dep:
                s, v = dep
                waits[s] = max(waits.get(s, 0), v)

        for b in reads:
            add(b.w)
        for b in writes:
            add(b.w)
            for s, v in b.readers.items():
                add((s, v))
            for s, v in b.war_extra.items():
                add((s, v))
        for d in extra:
            add(d)
        assert sem is not None
        self.cnt[sem] += step
        dep = (sem, self.cnt[sem])
        wlist = []
        for s, v in waits.items():
            if v > self.wm.get((eng, s), 0):
                self.wm[(eng, s)] = v
                wlist.append((self.sem[s], v))
        semh = self.sem[sem]

        def f(e, wlist=wlist, builder=builder, semh=semh, step=step):
            for h, v in wlist:
                e.wait_ge(h, v)
            builder(e).then_inc(semh, step)

        self.q[eng].append(f)
        for b in reads:
            s, v = dep
            b.readers[s] = max(b.readers.get(s, 0), v)
        for b in writes:
            b.w = dep
            b.readers = {}
            b.war_extra = {}
        return dep

    def wait(self, eng, dep):
        s, v = dep
        if v > self.wm.get((eng, s), 0):
            self.wm[(eng, s)] = v
            h = self.sem[s]
            self.q[eng].append(lambda e, h=h, v=v: e.wait_ge(h, v))


class Arena:
    def __init__(self, nc, size_bytes):
        self.nc = nc
        self.size = size_bytes
        self.tensors = []

    def at(self, name, off, shape, dtype):
        per = _reduce(lambda a, b: a * b, shape[1:]) * mybir.dt.size(dtype)
        assert off % 32 == 0, (name, off)
        assert off + per <= self.size, (name, off, per, self.size)
        h = self.nc.virtual_sbuf_tensor(name, shape, dtype)
        self.tensors.append((h, off))
        return h

    def allocate(self, es):
        self.handle = es.enter_context(
            self.nc.sbuf_tensor("arena", [128, self.size], mybir.dt.uint8))
        base = self.nc.lookup_mloc(self.handle).addr
        for h, off in self.tensors:
            self.nc._tensor(h.name, h.shape, h.dtype, type="SB")
            mloc = self.nc.lookup_mloc(h)
            mloc.allocated = not self.nc.target_bir_lowering
            mloc.addr = base + off


def build(debug=False):
    nc = bass.Bass("TRN2", target_bir_lowering=False)

    x_h = nc.dram_tensor("x", [T, E], F32, kind="ExternalInput").ap()
    wq_h = nc.dram_tensor("wqT", [E, 3 * E], F32, kind="ExternalInput").ap()
    b2_h = nc.dram_tensor("b2", [128, 24], F32, kind="ExternalInput").ap()
    wg_h = nc.dram_tensor("wgT", [E, INNER], F32, kind="ExternalInput").ap()
    wu_h = nc.dram_tensor("wuT", [E, INNER], F32, kind="ExternalInput").ap()
    wd_h = nc.dram_tensor("wdT", [INNER, E], F32, kind="ExternalInput").ap()
    ct_h = nc.dram_tensor("Ct", [128, T], F32, kind="ExternalInput").ap()
    st_h = nc.dram_tensor("St", [128, T], F32, kind="ExternalInput").ap()
    pm_h = nc.dram_tensor("Pm", [128, 128], BF16, kind="ExternalInput").ap()
    idb_h = nc.dram_tensor("Idb", [128, 128], BF16, kind="ExternalInput").ap()
    idf_h = nc.dram_tensor("Idf", [128, 128], F32, kind="ExternalInput").ap()
    or_h = nc.dram_tensor("OnesRow", [1, 128], F32, kind="ExternalInput").ap()
    out_h = nc.dram_tensor("out", [T, E], F32, kind="ExternalOutput").ap()
    tk = "ExternalOutput" if debug else "Internal"
    tg_h = nc.dram_tensor("ternG", [E, INNER], BF16, kind=tk).ap()
    tu_h = nc.dram_tensor("ternU", [E, INNER], BF16, kind=tk).ap()
    td_h = nc.dram_tensor("ternD", [INNER, E], BF16, kind=tk).ap()
    dbg = {}
    if debug:
        dbg["d_qT"] = nc.dram_tensor("d_qT", [E, T], BF16, kind="ExternalOutput").ap()
        dbg["d_kT"] = nc.dram_tensor("d_kT", [E, T], BF16, kind="ExternalOutput").ap()
        dbg["d_vtok"] = nc.dram_tensor("d_vtok", [T, 1040], BF16, kind="ExternalOutput").ap()
        dbg["d_h1"] = nc.dram_tensor("d_h1", [T, E], F32, kind="ExternalOutput").ap()
        dbg["d_k1T"] = nc.dram_tensor("d_k1T", [E, T], BF16, kind="ExternalOutput").ap()
        dbg["d_sca"] = nc.dram_tensor("d_sca", [1, 16], F32, kind="ExternalOutput").ap()
        for _m in range(3):
            dbg[f"d_wacc{_m}"] = nc.dram_tensor(f"d_wacc{_m}", [128, 1], F32,
                                                kind="ExternalOutput").ap()
        dbg["d_xatt"] = nc.dram_tensor("d_xatt", [T, E], F32, kind="ExternalOutput").ap()

    ar = Arena(nc, 206 * KB)
    A = ar.at

    _o = [0]
    def sm(name, shape, dtype):
        h = A(name, _o[0], shape, dtype)
        per = _reduce(lambda a, b: a * b, shape[1:]) * mybir.dt.size(dtype)
        _o[0] += (per + 31) & ~31
        return h
    PERM = sm("PERM", [128, 128], BF16)
    IDB = sm("IDB", [128, 128], BF16)
    IDF = sm("IDF", [128, 128], F32)
    ONESC = sm("ONESC", [128, 1], BF16)
    ONESR = sm("ONESR", [1, 128], F32)
    B2 = sm("B2", [128, 24], F32)
    CH2Q = sm("CH2Q", [1, 16], F32)
    CH2K = sm("CH2K", [1, 16], F32)
    CROW = sm("CROW", [1, 16], F32)
    CROWP = sm("CROWP", [1, 8], F32)
    CBIASP = sm("CBIASP", [128, 8], F32)
    S1INV = sm("S1INV", [128, NT], F32)
    S1S = sm("S1S", [128, NT], F32)
    ALG = sm("ALG", [128, 1], F32)
    ALUp = sm("ALUp", [128, 1], F32)
    ALD = sm("ALD", [128, 1], F32)
    RABG = sm("RABG", [128, 1], F32)
    RABU = sm("RABU", [128, 1], F32)
    RABD = sm("RABD", [128, 1], F32)
    WACC = sm("WACC", [128, 1], F32)
    SCA = sm("SCA", [1, 16], F32)
    C1285 = sm("C1285", [128, 1], F32)
    C05 = sm("C05", [128, 1], F32)
    CM05 = sm("CM05", [128, 1], F32)
    SGT = sm("SGT", [128, NT], F32)
    SUT = sm("SUT", [128, TSB], F32)
    S2E = sm("S2E", [128, TSB], F32)
    SDT = sm("SDT", [128, TSB], F32)
    STAT = sm("STAT", [128, 8], F32)
    STAT2 = sm("STAT2", [128, 8], F32)
    GSTAT = sm("GSTAT", [128, 2], F32)
    assert _o[0] <= 6 * KB, _o[0]

    CBT = A("CbT", 6 * KB, [128, T], BF16)
    SBT = A("SbT", 10 * KB, [128, T], BF16)
    PT = [A(f"PT{i}", 6 * KB + i * 2 * KB, [128, 1024], BF16) for i in range(4)]
    OBUF = [A(f"OB{i}", 86 * KB + i * 2 * KB, [65, 512], F32) for i in range(2)]

    CF = A("Cf", 14 * KB, [128, T], F32)
    SF = A("Sf", 22 * KB, [128, T], F32)
    WQ = [A(f"Wq{i}", 14 * KB + i * 6 * KB, [128, 3 * E], BF16) for i in range(NE)]
    XB = [A(f"xb{t}", 14 * KB + t * 4 * KB, [128, E], F32) for t in range(NT)]
    ARAW = [A(f"araw{k}", 14 * KB + k * 16 * KB, [128, INNER], F32)
            for k in range(TSB)]
    KT2V = [A(f"k2tv{k}", 14 * KB + k * 16 * KB + 8 * KB, [128, INNER], BF16)
            for k in range(TSB)]

    XT = [A(f"xT{i}", 62 * KB + i * 4 * KB, [128, T], BF16) for i in range(NE)]
    WST = [A(f"wst{i}", 78 * KB + i * 4 * KB, [128, 1024], F32) for i in range(2)]
    CBQ = A("CBQ", 78 * KB, [1, 2048], F32)
    CBK = A("CBK", 86 * KB, [1, 2048], F32)
    SQS = A("sqs", 6 * KB, [128, 1024], BF16)
    SQS2 = A("sqs2", 8 * KB, [128, 1024], BF16)
    MST = [A(f"mst{i}", 200 * KB + i * 2 * KB, [128, 1024], BF16) for i in range(2)]
    WA_ = A("wa", 90 * KB, [128, 1024], BF16)
    WB_ = A("wb", 92 * KB, [128, 1024], BF16)
    ZSCR = A("zscr", 78 * KB, [128, E], F32)
    ZI1 = A("zi1", 82 * KB, [128, E], I32)
    M1 = A("m1", 86 * KB, [128, E], BF16)

    QT = [A(f"qT{i}", 94 * KB + i * 4 * KB, [128, T], BF16) for i in range(NE)]
    K1T = [A(f"k1T{i}", 94 * KB + i * 4 * KB, [128, T], BF16) for i in range(NE)]
    KT = [A(f"kT{i}", 126 * KB + i * 4 * KB, [128, T], BF16) for i in range(NE)]
    H1B = [A(f"h1b{t}", 126 * KB + t * 2 * KB, [128, E], BF16) for t in range(NT)]
    VTOK = [A(f"vt{t}", 158 * KB + t * 2144, [128, 1040], BF16) for t in range(NT)]
    RING = [[A(f"rg{r}_{j}", (158 * KB + r * 16 * KB if r < 2 else 78 * KB)
               + j * 2 * KB, [128, 1024], BF16) for j in range(8)]
            for r in range(3)]
    XBF = [A(f"xbf{i}", 192 * KB + i * 4 * KB, [128, E], F32) for i in range(2)]
    VSCR = A("vscr", 192 * KB, [128, T], F32)
    GB = [A(f"gb{k}", 190 * KB + k * 2 * KB, [128, 1024], BF16)
          for k in range(TSB)]
    TOUT = [A(f"tout{i}", 200 * KB + i * 2 * KB, [128, 1024], BF16) for i in range(2)]
    WAV = A("wav", 204 * KB, [128, 1024], BF16)
    OUTB = [A(f"outb{o}", 198 * KB + o * 4 * KB, [128, E], F32) for o in range(2)]
    ZVF = A("zvf", 198 * KB, [128, 1024], F32)
    ZVI = A("zvi", 202 * KB, [128, 1024], I32)
    ZGF = A("zgf", 6 * KB, [128, 1024], F32)
    ZGI = A("zgi", 10 * KB, [128, 1024], I32)

    es = ExitStack()
    PSA = es.enter_context(nc.psum_tensor("psA", [128, 1024], F32))
    PSB2 = es.enter_context(nc.psum_tensor("psB", [128, 1024], F32))
    PSC = es.enter_context(nc.psum_tensor("psC", [128, 1024], F32))
    PSD = es.enter_context(nc.psum_tensor("psD", [128, 1024], F32))
    ar.allocate(es)
    P = Prog(nc, es)

    bufs = {}
    def bf(h):
        if id(h) not in bufs:
            bufs[id(h)] = Buf()
        return bufs[id(h)]

    PSBANK = {}
    def pbank(tensor, lo):
        key = (id(tensor), lo // 512)
        if key not in PSBANK:
            PSBANK[key] = Buf()
        return PSBANK[key]

    HB = {}
    def hbuf(name):
        if name not in HB:
            HB[name] = Buf()
        return HB[name]

    def dma_ld(dst_ap, src_ap, reads=(), writes=()):
        P.emit("sync", lambda e, d=dst_ap, s=src_ap: e.dma_start(d, s),
               reads=reads, writes=writes, sem="ld", step=16)

    def dma_st(dst_ap, src_ap, reads=(), writes=()):
        P.emit("sync", lambda e, d=dst_ap, s=src_ap: e.dma_start(d, s),
               reads=reads, writes=writes, sem="st", step=16)

    def gdma(dst_ap, src_ap, reads=(), writes=()):
        P.emit("gpsimd", lambda e, d=dst_ap, s=src_ap: e.dma_start(d, s),
               reads=reads, writes=writes, sem="lg", step=16)

    for dst, src in ((PERM, pm_h), (IDB, idb_h), (IDF, idf_h),
                     (ONESR, or_h), (B2, b2_h), (CF, ct_h), (SF, st_h)):
        dma_ld(dst[:], src[:], writes=[bf(dst)])
    P.emit("vector", lambda e: e.memset(ONESC[:], 1.0), writes=[bf(ONESC)], sem="v")
    P.emit("vector", lambda e: e.memset(C1285[:], 128.5), writes=[bf(C1285)], sem="v")
    P.emit("vector", lambda e: e.memset(C05[:], 0.5), writes=[bf(C05)], sem="v")
    P.emit("vector", lambda e: e.memset(CM05[:], -0.5), writes=[bf(CM05)], sem="v")
    P.emit("vector", lambda e: e.tensor_copy(CBT[:], CF[:]),
           reads=[bf(CF)], writes=[bf(CBT)], sem="v")
    P.emit("vector", lambda e: e.tensor_copy(SBT[:], SF[:]),
           reads=[bf(SF)], writes=[bf(SBT)], sem="v")
    for t in range(NT):
        P.emit("vector", lambda e, t=t: e.memset(VTOK[t][:], 1.0),
               writes=[bf(VTOK[t])], sem="v")

    for t in range(NT):
        xb = XBF[t % 2]
        dma_ld(xb[:], x_h[t * 128:(t + 1) * 128, :], writes=[bf(xb)])
        for i in range(NE):
            pp = (t * NE + i) % 2
            pslice = PSD[:, pp * 512: pp * 512 + 128]
            pb = pbank(PSD, pp * 512)
            P.emit("tensor",
                   lambda e, p=pslice, xb=xb, i=i:
                       e.transpose(p, xb[:, i * 128:(i + 1) * 128], IDF[:]),
                   reads=[bf(xb), bf(IDF)], writes=[pb], sem="t")
            P.emit("vector",
                   lambda e, p=pslice, i=i, t=t:
                       e.tensor_copy(XT[i][:, t * 128:(t + 1) * 128], p),
                   reads=[pb], writes=[bf(XT[i])], sem="v")

    NCH = 32

    def wslice(m, ch):
        wh = (wg_h, wu_h, wd_h)[m]
        if m < 2:
            i, q = ch // 4, ch % 4
            return wh[i * 128:(i + 1) * 128, q * 1024:(q + 1) * 1024]
        return wh[ch * 128:(ch + 1) * 128, :]

    def tslice(m, ch):
        th = (tg_h, tu_h, td_h)[m]
        if m < 2:
            i, q = ch // 4, ch % 4
            return th[i * 128:(i + 1) * 128, q * 1024:(q + 1) * 1024]
        return th[ch * 128:(ch + 1) * 128, :]

    mean_wl = []

    def emit_mean(m, ch):
        w = MST[ch % 2]
        gdma(w[:], wslice(m, ch), reads=[hbuf(f"w{m}")], writes=[bf(w)])
        if ch == 0:
            P.emit("vector", lambda e: e.memset(WACC[:], 0.0),
                   writes=[bf(WACC)], sem="v")
        P.emit("vector",
               lambda e, w=w: e.reduce_sum(GSTAT[:, 0:1], w[:],
                                           axis=mybir.AxisListType.X,
                                           apply_absolute_value=True),
               reads=[bf(w)], writes=[bf(GSTAT)], sem="v")
        P.emit("vector", lambda e: e.tensor_add(WACC[:], WACC[:], GSTAT[:, 0:1]),
               reads=[bf(GSTAT), bf(WACC)], writes=[bf(WACC)], sem="v")
        if ch == NCH - 1:
            pb = pbank(PSD, 0)
            P.emit("tensor", lambda e: e.transpose(PSD[0:1, 0:128], WACC[:], IDF[:]),
                   reads=[bf(WACC), bf(IDF)], writes=[pb], sem="t")
            P.emit("vector", lambda e: e.reduce_sum(SCA[0:1, 0:1], PSD[0:1, 0:128],
                                                    axis=mybir.AxisListType.X),
                   reads=[pb], writes=[bf(SCA)], sem="v")
            P.emit("vector", lambda e, m=m: e.tensor_copy(
                       SCA[0:1, 10 + m:11 + m], SCA[0:1, 0:1]),
                   reads=[bf(SCA)], writes=[bf(SCA)], sem="v")
            if debug:
                dma_st(dbg[f"d_wacc{m}"][:], WACC[:], reads=[bf(WACC)])
            P.emit("vector", lambda e, m=m: e.tensor_scalar(
                       SCA[0:1, 1 + m:2 + m], SCA[0:1, 0:1], 1.0 / WQN, 1e-5,
                       ALU.mult, ALU.max),
                   reads=[bf(SCA)], writes=[bf(SCA)], sem="v")
            P.emit("vector", lambda e, m=m: e.reciprocal(
                       SCA[0:1, 4 + m:5 + m], SCA[0:1, 1 + m:2 + m]),
                   reads=[bf(SCA)], writes=[bf(SCA)], sem="v")
            P.emit("vector", lambda e, m=m: e.tensor_single_scalar(
                       SCA[0:1, 7 + m:8 + m], SCA[0:1, 1 + m:2 + m], 0.5,
                       ALU.mult),
                   reads=[bf(SCA)], writes=[bf(SCA)], sem="v")
            for col, dst in ((7 + m, (ALG, ALUp, ALD)[m]),
                             (4 + m, (RABG, RABU, RABD)[m])):
                pb2 = pbank(PSD, 512)
                P.emit("tensor", lambda e, col=col: e.matmul(
                           PSD[0:128, 512:513], ONESR[:], SCA[0:1, col:col + 1],
                           start=True, stop=True),
                       reads=[bf(SCA), bf(ONESR)], writes=[pb2], sem="t")
                P.emit("vector", lambda e, dst=dst: e.tensor_copy(
                           dst[:], PSD[0:128, 512:513]),
                       reads=[pb2], writes=[bf(dst)], sem="v")

    for m in range(3):
        for ch in range(NCH):
            mean_wl.append((emit_mean, m, ch))

    def emit_quant(m, ch):
        w = WST[0]
        rab = (RABG, RABU, RABD)[m]
        to = TOUT[0]
        if m == 0 and ch == 0:
            bf(WST[0]).inherit(bf(XT[4]), bf(CBQ))
            bf(WST[1]).inherit(bf(XT[5]), bf(CBQ))
            bf(WA_).inherit(bf(XT[7]), bf(CBK))
            bf(WB_).inherit(bf(XT[7]), bf(CBK))
            bf(TOUT[0]).inherit(bf(MST[0]), bf(MST[1]))
            bf(TOUT[1]).inherit(bf(MST[1]))
        dma_ld(w[:], wslice(m, ch), reads=[hbuf(f"w{m}")], writes=[bf(w)])
        P.emit("scalar", lambda e, w=w, rab=rab: e.activation(
                   WA_[:], w[:], AF.Sign, bias=C05[:], scale=rab[:]),
               reads=[bf(w), bf(rab), bf(C05)], writes=[bf(WA_)], sem="s")
        P.emit("scalar", lambda e, w=w, rab=rab: e.activation(
                   WB_[:], w[:], AF.Sign, bias=CM05[:], scale=rab[:]),
               reads=[bf(w), bf(rab), bf(CM05)], writes=[bf(WB_)], sem="s")
        P.emit("vector", lambda e, to=to: e.tensor_tensor(
                   to[:], WA_[:], WB_[:], ALU.add),
               reads=[bf(WA_), bf(WB_)], writes=[bf(to)], sem="v")
        dma_st(tslice(m, ch), to[:], reads=[bf(to)], writes=[hbuf(f"t{m}")])

    def emit_quant_v(m, ch):
        w = WST[ch % 2]
        rab = (RABG, RABU, RABD)[m]
        to = TOUT[ch % 2]
        if m == 0 and ch == 0:
            bf(WST[0]).inherit(bf(XT[4]), bf(CBQ))
            bf(WST[1]).inherit(bf(XT[5]), bf(CBQ))
            bf(WA_).inherit(bf(XT[7]), bf(CBK))
            bf(WB_).inherit(bf(XT[7]), bf(CBK))
            bf(TOUT[0]).inherit(bf(MST[0]), bf(MST[1]))
            bf(TOUT[1]).inherit(bf(MST[1]))
        dma_ld(w[:], wslice(m, ch), reads=[hbuf(f"w{m}")], writes=[bf(w)])
        P.emit("vector", lambda e, w=w, rab=rab: e.tensor_scalar(
                   WAV[:], w[:], rab[:], 0.5, ALU.mult, ALU.is_ge),
               reads=[bf(w), bf(rab)], writes=[bf(WAV)], sem="v")
        P.emit("vector", lambda e, w=w, rab=rab, to=to: e.tensor_scalar(
                   to[:], w[:], rab[:], -0.5, ALU.mult, ALU.is_ge),
               reads=[bf(w), bf(rab)], writes=[bf(to)], sem="v")
        P.emit("vector", lambda e, to=to: e.tensor_tensor(
                   to[:], to[:], WAV[:], ALU.add),
               reads=[bf(to), bf(WAV)], writes=[bf(to)], sem="v")
        P.emit("vector", lambda e, to=to: e.tensor_scalar(
                   to[:], to[:], 2.0, -2.0, ALU.mult, ALU.add),
               reads=[bf(to)], writes=[bf(to)], sem="v")
        dma_st(tslice(m, ch), to[:], reads=[bf(to)], writes=[hbuf(f"t{m}")])

    wq_s = []
    wq_v = [(emit_quant_v, m, ch) for m in (0, 1, 2) for ch in range(NCH)]

    def drain(wl, n):
        for _ in range(min(n, len(wl))):
            fn, m, ch = wl.pop(0)
            fn(m, ch)

    bf(VSCR).inherit(bf(XBF[0]), bf(XBF[1]))
    for i in range(3):
        bf(WQ[i]).inherit(bf(CF), bf(SF))
    for i in range(NE):
        gdma(WQ[i][:], wq_h[i * 128:(i + 1) * 128, :], writes=[bf(WQ[i])])

    def emit_rope(tl):
        tb = bf(tl)
        for h in range(2):
            for lo in (0, 512):
                pb = pbank(PSC, lo)
                P.emit("tensor",
                       lambda e, lo=lo, h=h, tl=tl:
                           e.matmul(PSC[:, lo:lo + 512], PERM[:],
                                    tl[:, h * 1024 + lo:h * 1024 + lo + 512],
                                    start=True, stop=True),
                       reads=[bf(PERM), tb], writes=[pb], sem="t")
                P.emit("vector",
                       lambda e, lo=lo, h=h:
                           e.tensor_tensor(
                               WAV[:, lo:lo + 512], PSC[:, lo:lo + 512],
                               SBT[:, h * 1024 + lo:h * 1024 + lo + 512],
                               ALU.mult),
                       reads=[pb, bf(SBT)], writes=[bf(WAV)], sem="v")
            P.emit("vector",
                   lambda e, h=h, tl=tl:
                       e.tensor_tensor(tl[:, h * 1024:(h + 1) * 1024],
                                       tl[:, h * 1024:(h + 1) * 1024],
                                       CBT[:, h * 1024:(h + 1) * 1024],
                                       ALU.mult),
                   reads=[tb, bf(CBT)], writes=[tb], sem="v")
            P.emit("vector",
                   lambda e, h=h, tl=tl:
                       e.tensor_tensor(tl[:, h * 1024:(h + 1) * 1024],
                                       tl[:, h * 1024:(h + 1) * 1024],
                                       WAV[:], ALU.add),
                   reads=[tb, bf(WAV)], writes=[tb], sem="v")

    QKVPS = [(PSA, 0), (PSA, 512), (PSB2, 0), (PSB2, 512)]
    for ot in range(24):
        for i in range(NE):
            for tc in range(4):
                ps, lo = QKVPS[tc]
                P.emit("tensor",
                       lambda e, ps=ps, lo=lo, i=i, ot=ot, tc=tc:
                           e.matmul(ps[:, lo:lo + 512],
                                    WQ[i][:, ot * 128:(ot + 1) * 128],
                                    XT[i][:, tc * 512:(tc + 1) * 512],
                                    start=(i == 0), stop=(i == NE - 1)),
                       reads=[bf(WQ[i]), bf(XT[i])], writes=[pbank(ps, lo)],
                       sem="t")
        for tc in range(4):
            ps, lo = QKVPS[tc]
            if ot < 8:
                dst, db = QT[ot], bf(QT[ot])
            elif ot < 16:
                dst, db = KT[ot - 8], bf(KT[ot - 8])
            else:
                dst, db = VSCR, bf(VSCR)
            P.emit("scalar",
                   lambda e, ps=ps, lo=lo, dst=dst, ot=ot, tc=tc:
                       e.activation(dst[:, tc * 512:(tc + 1) * 512],
                                    ps[:, lo:lo + 512], AF.Identity,
                                    bias=B2[:, ot:ot + 1], scale=1.0),
                   reads=[pbank(ps, lo), bf(B2)], writes=[db], sem="s")
        if ot >= 16:
            h0 = 2 * (ot - 16)
            for kt in range(NT):
                pp = kt % 2
                pslice = PSD[:, pp * 512: pp * 512 + 128]
                pb = pbank(PSD, pp * 512)
                P.emit("tensor",
                       lambda e, p=pslice, kt=kt:
                           e.transpose(p, VSCR[:, kt * 128:(kt + 1) * 128], IDF[:]),
                       reads=[bf(VSCR), bf(IDF)], writes=[pb], sem="t")
                for hh in range(2):
                    P.emit("vector",
                           lambda e, p=pslice, kt=kt, h=h0 + hh, hh=hh:
                               e.tensor_copy(VTOK[kt][:, h * 65: h * 65 + 64],
                                             p[:, hh * 64:(hh + 1) * 64]),
                           reads=[pb], writes=[bf(VTOK[kt])], sem="v")
        if 1 <= ot <= 8:
            emit_rope(QT[ot - 1])
        elif 9 <= ot <= 16:
            emit_rope(KT[ot - 9])
        drain(mean_wl, 4)

    bf(CBQ).inherit(bf(XT[4]), bf(XT[5]))
    bf(SQS).inherit(bf(CBT))
    bf(SQS2).inherit(bf(CBT))
    CPS = [(PSC, 0), (PSC, 512), (PSD, 0), (PSD, 512)]
    for tiles, chrow in ((QT, CH2Q), (KT, CH2K)):
        for hp in range(NE):
            for hh in range(2):
                sq = (SQS, SQS2)[hh]
                P.emit("scalar",
                       lambda e, tl=tiles[hp], hh=hh, sq=sq:
                           e.activation(sq[:], tl[:, hh * 1024:(hh + 1) * 1024],
                                        AF.Square),
                       reads=[bf(tiles[hp])], writes=[bf(sq)], sem="s")
                for side in range(2):
                    for cc in range(2):
                        ps, lo = CPS[side * 2 + cc]
                        P.emit("tensor",
                               lambda e, ps=ps, lo=lo, cc=cc, side=side, sq=sq:
                                   e.matmul(ps[0:1, lo:lo + 512],
                                            ONESC[side * 64:(side + 1) * 64, 0:1],
                                            sq[side * 64:(side + 1) * 64,
                                                cc * 512:(cc + 1) * 512],
                                            start=True, stop=True),
                               reads=[bf(ONESC), bf(sq)],
                               writes=[pbank(ps, lo)], sem="t")
                for side in range(2):
                    for cc in range(2):
                        ps, lo = CPS[side * 2 + cc]
                        P.emit("vector",
                               lambda e, ps=ps, lo=lo, hh=hh, cc=cc, side=side:
                                   e.tensor_copy(
                                       CBQ[0:1, (hh * 2 + cc) * 512:
                                           (hh * 2 + cc + 1) * 512],
                                       ps[0:1, lo:lo + 512])
                                   if side == 0 else
                                   e.tensor_copy(
                                       CBK[0:1, (hh * 2 + cc) * 512:
                                           (hh * 2 + cc + 1) * 512],
                                       ps[0:1, lo:lo + 512]),
                               reads=[pbank(ps, lo)],
                               writes=[bf(CBQ) if side == 0 else bf(CBK)],
                               sem="v")
            P.emit("vector", lambda e, hp=hp, chrow=chrow: e.reduce_max(
                       chrow[0:1, 2 * hp:2 * hp + 1], CBQ[:],
                       axis=mybir.AxisListType.X),
                   reads=[bf(CBQ)], writes=[bf(chrow)], sem="v")
            P.emit("vector", lambda e, hp=hp, chrow=chrow: e.reduce_max(
                       chrow[0:1, 2 * hp + 1:2 * hp + 2], CBK[:],
                       axis=mybir.AxisListType.X),
                   reads=[bf(CBK)], writes=[bf(chrow)], sem="v")
    P.emit("vector", lambda e: e.tensor_tensor(CROW[:], CH2Q[:], CH2K[:],
                                               ALU.mult),
           reads=[bf(CH2Q), bf(CH2K)], writes=[bf(CROW)], sem="v")
    P.emit("scalar", lambda e: e.activation(CROW[:], CROW[:], AF.Sqrt,
                                            scale=1.0 / 64.0),
           reads=[bf(CROW)], writes=[bf(CROW)], sem="s")
    P.emit("vector", lambda e: e.tensor_tensor(CROWP[:], CROW[0:1, 0:16:2],
                                               CROW[0:1, 1:16:2], ALU.max),
           reads=[bf(CROW)], writes=[bf(CROWP)], sem="v")
    P.emit("vector", lambda e: e.tensor_single_scalar(CROWP[:], CROWP[:], -1.0,
                                                      ALU.mult),
           reads=[bf(CROWP)], writes=[bf(CROWP)], sem="v")
    pbX = pbank(PSD, 0)
    P.emit("tensor", lambda e: e.matmul(PSD[0:128, 0:8], ONESR[:], CROWP[:],
                                        start=True, stop=True),
           reads=[bf(ONESR), bf(CROWP)], writes=[pbX], sem="t")
    P.emit("vector", lambda e: e.tensor_copy(CBIASP[:], PSD[0:128, 0:8]),
           reads=[pbX], writes=[bf(CBIASP)], sem="v")

    def emit_ln(tile, tbuf, st, stb, sq, sqb):
        P.emit("vector", lambda e: e.reduce_sum(st[:, 0:1], tile[:],
                                                axis=mybir.AxisListType.X),
               reads=[tbuf], writes=[stb], sem="v")
        P.emit("scalar", lambda e: e.activation(sq, tile[:], AF.Square,
                                                accum_out=st[:, 1:2]),
               reads=[tbuf], writes=[stb] + list(sqb), sem="s")
        P.emit("vector", lambda e: e.tensor_single_scalar(
                   st[:, 2:3], st[:, 0:1], 1.0 / float(E), ALU.mult),
               reads=[stb], writes=[stb], sem="v")
        P.emit("vector", lambda e: e.tensor_tensor(st[:, 3:4], st[:, 2:3],
                                                   st[:, 2:3], ALU.mult),
               reads=[stb], writes=[stb], sem="v")
        P.emit("vector", lambda e: e.scalar_tensor_tensor(
                   st[:, 4:5], st[:, 1:2], 1.0 / float(E), st[:, 3:4],
                   ALU.mult, ALU.subtract),
               reads=[stb], writes=[stb], sem="v")
        P.emit("vector", lambda e: e.tensor_single_scalar(
                   st[:, 5:6], st[:, 4:5], 1e-5, ALU.add),
               reads=[stb], writes=[stb], sem="v")
        P.emit("scalar", lambda e: e.activation(st[:, 6:7], st[:, 5:6], AF.Sqrt),
               reads=[stb], writes=[stb], sem="s")
        P.emit("vector", lambda e: e.reciprocal(st[:, 7:8], st[:, 6:7]),
               reads=[stb], writes=[stb], sem="v")
        P.emit("vector", lambda e: e.tensor_tensor(st[:, 3:4], st[:, 7:8],
                                                   st[:, 7:8], ALU.mult),
               reads=[stb], writes=[stb], sem="v")
        P.emit("vector", lambda e: e.scalar_tensor_tensor(
                   st[:, 4:5], st[:, 5:6], 0.5, st[:, 3:4],
                   ALU.mult, ALU.mult),
               reads=[stb], writes=[stb], sem="v")
        P.emit("vector", lambda e: e.tensor_scalar(
                   st[:, 3:4], st[:, 4:5], -1.0, 1.5, ALU.mult, ALU.add),
               reads=[stb], writes=[stb], sem="v")
        P.emit("vector", lambda e: e.tensor_tensor(st[:, 6:7], st[:, 7:8],
                                                   st[:, 3:4], ALU.mult),
               reads=[stb], writes=[stb], sem="v")
        P.emit("vector", lambda e: e.tensor_scalar(
                   tile[:], tile[:], st[:, 2:3], st[:, 6:7],
                   ALU.subtract, ALU.mult),
               reads=[stb, tbuf], writes=[tbuf], sem="v")

    for t in range(NT):
        bf(XB[t]).inherit(*[bf(WQ[i]) for i in range(NE)],
                          *[bf(XT[i]) for i in range(4)])
        dma_ld(XB[t][:], x_h[t * 128:(t + 1) * 128, :], writes=[bf(XB[t])])
    for pt in PT:
        bf(pt).inherit(bf(CBT), bf(SBT), bf(SQS), bf(SQS2))
    for ob in OBUF:
        bf(ob).inherit(bf(XT[6]), bf(CBK))

    for hp in range(NE):
        h0, h1 = 2 * hp, 2 * hp + 1
        for qc in range(4):
            def emit_S(kt):
                ps = PSA if kt % 2 == 0 else PSB2
                for side in range(2):
                    lo = side * 512
                    P.emit("tensor",
                           lambda e, ps=ps, lo=lo, side=side, kt=kt, qc=qc, hp=hp:
                               e.matmul(ps[:, lo:lo + 512],
                                        KT[hp][side * 64:(side + 1) * 64,
                                               kt * 128:(kt + 1) * 128],
                                        QT[hp][side * 64:(side + 1) * 64,
                                               qc * 512:(qc + 1) * 512],
                                        start=True, stop=True,
                                        tile_position=(side * 64, 0)),
                           reads=[bf(KT[hp]), bf(QT[hp])],
                           writes=[pbank(ps, lo)], sem="t")

            def emit_exp(kt):
                ps = PSA if kt % 2 == 0 else PSB2
                pt = PT[kt % 4]
                P.emit("scalar",
                       lambda e, ps=ps, pt=pt, hp=hp:
                           e.activation(pt[:], ps[:], AF.Exp,
                                        bias=CBIASP[:, hp:hp + 1], scale=0.125),
                       reads=[pbank(ps, 0), pbank(ps, 512), bf(CBIASP)],
                       writes=[bf(pt)], sem="s")

            def emit_PV(kt):
                pt = PT[kt % 4]
                for side, h in ((0, h0), (1, h1)):
                    lo = side * 512
                    P.emit("tensor",
                           lambda e, pt=pt, lo=lo, h=h, kt=kt:
                               e.matmul(PSC[0:65, lo:lo + 512],
                                        VTOK[kt][:, h * 65: h * 65 + 65],
                                        pt[:, lo:lo + 512],
                                        start=(kt == 0), stop=(kt == NT - 1)),
                           reads=[bf(VTOK[kt]), bf(pt)],
                           writes=[pbank(PSC, lo)], sem="t")

            emit_S(0)
            emit_exp(0)
            for kt in range(1, NT):
                emit_S(kt)
                emit_PV(kt - 1)
                emit_exp(kt)
            emit_PV(NT - 1)

            for side, h in ((0, h0), (1, h1)):
                lo = side * 512
                ob = OBUF[side]
                P.emit("vector",
                       lambda e, ob=ob, lo=lo:
                           e.tensor_copy(ob[:], PSC[0:65, lo:lo + 512]),
                       reads=[pbank(PSC, lo)], writes=[bf(ob)], sem="v")
                for j in range(4):
                    lo2 = (j % 2) * 512
                    pb = pbank(PSD, lo2)
                    P.emit("tensor",
                           lambda e, ob=ob, j=j, lo2=lo2:
                               e.transpose(PSD[:, lo2:lo2 + 65],
                                           ob[0:65, j * 128:(j + 1) * 128],
                                           IDF[0:65, 0:65]),
                           reads=[bf(ob), bf(IDF)], writes=[pb], sem="t")
                    P.emit("vector",
                           lambda e, lo2=lo2:
                               e.reciprocal(STAT[:, 0:1], PSD[:, lo2 + 64:lo2 + 65]),
                           reads=[pb], writes=[bf(STAT)], sem="v")
                    tb = XB[qc * 4 + j]
                    P.emit("vector",
                           lambda e, lo2=lo2, tb=tb, h=h:
                               e.scalar_tensor_tensor(
                                   tb[:, h * 64:(h + 1) * 64],
                                   PSD[:, lo2:lo2 + 64], STAT[:, 0:1],
                                   tb[:, h * 64:(h + 1) * 64],
                                   ALU.mult, ALU.add),
                           reads=[pb, bf(STAT), bf(tb)], writes=[bf(tb)],
                           sem="v")
            drain(wq_v, 3)
            if hp == NE - 1:
                for t in range(qc * 4, qc * 4 + 4):
                    emit_ln(XB[t], bf(XB[t]), STAT2, bf(STAT2),
                            SQS[:], [bf(SQS)])
    drain(wq_s, len(wq_s))
    drain(wq_v, len(wq_v))

    if debug:
        for i in range(NE):
            dma_st(dbg["d_qT"][i * 128:(i + 1) * 128, :], QT[i][:], reads=[bf(QT[i])])
            dma_st(dbg["d_kT"][i * 128:(i + 1) * 128, :], KT[i][:], reads=[bf(KT[i])])
        for t in range(NT):
            dma_st(dbg["d_vtok"][t * 128:(t + 1) * 128, :], VTOK[t][:],
                   reads=[bf(VTOK[t])])
            dma_st(dbg["d_xatt"][t * 128:(t + 1) * 128, :], XB[t][:],
                   reads=[bf(XB[t])])
        dma_st(dbg["d_sca"][:], SCA[:], reads=[bf(SCA)])

    if debug:
        for t in range(NT):
            dma_st(dbg["d_h1"][t * 128:(t + 1) * 128, :], XB[t][:],
                   reads=[bf(XB[t])])

    bf(ZSCR).inherit(bf(WST[0]))
    bf(ZI1).inherit(bf(WST[1]))
    bf(M1).inherit(bf(CBK), bf(OBUF[0]), bf(OBUF[1]))
    for i in range(NE):
        bf(K1T[i]).inherit(bf(QT[i]))
    for t in range(NT):
        bf(H1B[t]).inherit(bf(KT[t // 2]))

    for t in range(NT):
        P.emit("vector", lambda e, t=t: e.reduce_max(
                   STAT[:, 0:1], XB[t][:], axis=mybir.AxisListType.X,
                   apply_absolute_value=True),
               reads=[bf(XB[t])], writes=[bf(STAT)], sem="v")
        P.emit("vector", lambda e: e.tensor_single_scalar(
                   STAT[:, 1:2], STAT[:, 0:1], 1e-5, ALU.max),
               reads=[bf(STAT)], writes=[bf(STAT)], sem="v")
        P.emit("vector", lambda e: e.reciprocal(STAT[:, 2:3], STAT[:, 1:2]),
               reads=[bf(STAT)], writes=[bf(STAT)], sem="v")
        P.emit("vector", lambda e, t=t: e.tensor_single_scalar(
                   S1S[:, t:t + 1], STAT[:, 2:3], 127.0, ALU.mult),
               reads=[bf(STAT)], writes=[bf(S1S)], sem="v")
        P.emit("vector", lambda e, t=t: e.tensor_single_scalar(
                   S1INV[:, t:t + 1], STAT[:, 1:2], 1.0 / 127.0, ALU.mult),
               reads=[bf(STAT)], writes=[bf(S1INV)], sem="v")
        P.emit("scalar", lambda e, t=t: e.activation(
                   ZSCR[:], XB[t][:], AF.Identity, bias=C1285[:],
                   scale=S1S[:, t:t + 1]),
               reads=[bf(XB[t]), bf(S1S), bf(C1285)], writes=[bf(ZSCR)], sem="s")
        P.emit("vector", lambda e: e.tensor_copy(ZI1[:], ZSCR[:]),
               reads=[bf(ZSCR)], writes=[bf(ZI1)], sem="v")
        P.emit("vector", lambda e: e.scalar_tensor_tensor(
                   M1[:], ZSCR[:], 0.0, ZI1[:], ALU.add, ALU.is_lt),
               reads=[bf(ZSCR), bf(ZI1)], writes=[bf(M1)], sem="v")
        P.emit("vector", lambda e: e.scalar_tensor_tensor(
                   ZSCR[:], ZI1[:], -128.0, M1[:], ALU.add, ALU.subtract),
               reads=[bf(ZI1), bf(M1)], writes=[bf(ZSCR)], sem="v")
        for i in range(NE):
            lo = ((t * NE + i) % 2) * 512
            pb = pbank(PSD, lo)
            P.emit("tensor", lambda e, lo=lo, i=i: e.transpose(
                       PSD[:, lo:lo + 128], ZSCR[:, i * 128:(i + 1) * 128], IDF[:]),
                   reads=[bf(ZSCR), bf(IDF)], writes=[pb], sem="t")
            P.emit("vector", lambda e, lo=lo, i=i, t=t: e.tensor_copy(
                       K1T[i][:, t * 128:(t + 1) * 128], PSD[:, lo:lo + 128]),
                   reads=[pb], writes=[bf(K1T[i])], sem="v")
        P.emit("vector", lambda e, t=t: e.tensor_copy(H1B[t][:], XB[t][:]),
               reads=[bf(XB[t])], writes=[bf(H1B[t])], sem="v")

    if debug:
        for i in range(NE):
            dma_st(dbg["d_k1T"][i * 128:(i + 1) * 128, :], K1T[i][:],
                   reads=[bf(K1T[i])])

    for k in range(TSB):
        for t in range(4 * k, 4 * k + 4):
            bf(ARAW[k]).inherit(bf(XB[t]))
    bf(ARAW[0]).inherit(bf(CF), bf(SF))
    bf(ARAW[3]).inherit(bf(XT[0]), bf(XT[1]), bf(XT[2]), bf(XT[3]))
    for k in range(TSB):
        bufs[id(KT2V[k])] = bf(ARAW[k])
    for zg in (ZGF, ZGI):
        bf(zg).inherit(bf(CBT), bf(SBT), bf(PT[0]), bf(PT[1]),
                       bf(PT[2]), bf(PT[3]), bf(SQS), bf(SQS2))
    for j in range(8):
        for t in range(0, 9):
            bf(RING[0][j]).inherit(bf(VTOK[t]))
        for t in range(7, NT):
            bf(RING[1][j]).inherit(bf(VTOK[t]))
        bf(RING[2][j]).inherit(bf(ZSCR), bf(ZI1), bf(M1), bf(WA_), bf(WB_),
                               bf(WST[0]), bf(WST[1]), bf(OBUF[0]), bf(OBUF[1]),
                               bf(CBQ), bf(CBK),
                               *[bf(XT[i]) for i in range(4, NE)])
    for k in range(TSB):
        bf(GB[k]).inherit(bf(VTOK[14]), bf(VTOK[15]), bf(XBF[0]),
                          bf(XBF[1]), bf(VSCR))
    bf(OUTB[0]).inherit(bf(XBF[1]), bf(TOUT[0]), bf(MST[0]))
    bf(OUTB[1]).inherit(bf(TOUT[0]), bf(TOUT[1]), bf(WAV), bf(MST[1]))

    sched = []
    for sb in range(NSB):
        sched.append(("g", sb, 0))
        if sb > 0:
            for q in range(4):
                sched.append(("dA", sb - 1, q))
            for q in range(4):
                sched.append(("dB", sb - 1, q))
        sched.append(("u", sb, 0))
        for q in range(1, 4):
            sched.append(("g", sb, q))
            sched.append(("u", sb, q))
    for q in range(4):
        sched.append(("dA", NSB - 1, q))
    for q in range(4):
        sched.append(("dB", NSB - 1, q))

    def load_chunk(ci):
        kind, sb, q = sched[ci]
        slot = RING[ci % 3]
        hb = hbuf({"g": "t0", "u": "t1", "dA": "t2", "dB": "t2"}[kind])
        for j in range(8):
            if kind in ("dA", "dB"):
                src = td_h[(q * 8 + j) * 128:(q * 8 + j + 1) * 128, :]
            else:
                wh = tg_h if kind == "g" else tu_h
                src = wh[j * 128:(j + 1) * 128, q * 1024:(q + 1) * 1024]
            dma_ld(slot[j][:], src, reads=[hb], writes=[bf(slot[j])])

    def emit_scales(sb, k):
        gti = sb * TSB + k
        P.emit("vector", lambda e, k=k: e.reduce_max(
                   STAT2[:, 0:1], ARAW[k][:], axis=mybir.AxisListType.X,
                   apply_absolute_value=True),
               reads=[bf(ARAW[k])], writes=[bf(STAT2)], sem="v")
        P.emit("vector", lambda e, gti=gti, k=k: e.tensor_tensor(
                   SUT[:, k:k + 1], ALUp[:], S1INV[:, gti:gti + 1], ALU.mult),
               reads=[bf(ALUp), bf(S1INV)], writes=[bf(SUT)], sem="v")
        P.emit("vector", lambda e, k=k: e.tensor_tensor(
                   STAT2[:, 1:2], STAT2[:, 0:1], SUT[:, k:k + 1], ALU.mult),
               reads=[bf(STAT2), bf(SUT)], writes=[bf(STAT2)], sem="v")
        P.emit("vector", lambda e: e.tensor_single_scalar(
                   STAT2[:, 2:3], STAT2[:, 1:2], 1e-5, ALU.max),
               reads=[bf(STAT2)], writes=[bf(STAT2)], sem="v")
        P.emit("vector", lambda e: e.reciprocal(STAT2[:, 3:4], STAT2[:, 2:3]),
               reads=[bf(STAT2)], writes=[bf(STAT2)], sem="v")
        P.emit("vector", lambda e: e.tensor_single_scalar(
                   STAT2[:, 4:5], STAT2[:, 3:4], 127.0, ALU.mult),
               reads=[bf(STAT2)], writes=[bf(STAT2)], sem="v")
        P.emit("vector", lambda e, k=k: e.tensor_tensor(
                   S2E[:, k:k + 1], SUT[:, k:k + 1], STAT2[:, 4:5], ALU.mult),
               reads=[bf(SUT), bf(STAT2)], writes=[bf(S2E)], sem="v")
        P.emit("vector", lambda e: e.tensor_single_scalar(
                   STAT2[:, 5:6], STAT2[:, 2:3], 1.0 / 127.0, ALU.mult),
               reads=[bf(STAT2)], writes=[bf(STAT2)], sem="v")
        P.emit("vector", lambda e, k=k: e.tensor_tensor(
                   SDT[:, k:k + 1], ALD[:], STAT2[:, 5:6], ALU.mult),
               reads=[bf(ALD), bf(STAT2)], writes=[bf(SDT)], sem="v")

    def emit_rounds(sb, k):
        ab = bf(ARAW[k])
        for c in range(4):
            ar = ARAW[k][:, c * 1024:(c + 1) * 1024]
            zf, zi = (ZVF, ZVI) if c % 2 == 0 else (ZGF, ZGI)
            P.emit("vector", lambda e, ar=ar, k=k, zf=zf: e.tensor_scalar(
                       zf[:], ar, S2E[:, k:k + 1], 128.5, ALU.mult, ALU.add),
                   reads=[bf(S2E), ab], writes=[bf(zf)], sem="v")
            P.emit("vector", lambda e, zf=zf, zi=zi: e.tensor_copy(
                       zi[:], zf[:]),
                   reads=[bf(zf)], writes=[bf(zi)], sem="v")
            P.emit("vector", lambda e, zf=zf, zi=zi: e.tensor_tensor(
                       zf[:], zf[:], zi[:], ALU.is_lt),
                   reads=[bf(zf), bf(zi)], writes=[bf(zf)], sem="v")
            P.emit("vector", lambda e, k=k, c=c, zf=zf, zi=zi:
                       e.scalar_tensor_tensor(
                           ARAW[k][:, c * 1024:(c + 1) * 1024], zi[:],
                           -128.0, zf[:], ALU.add, ALU.subtract),
                   reads=[bf(zi), bf(zf)], writes=[ab], sem="v")

    def emit_tr(sb, k):
        ps = (PSC, PSD)[k % 2]
        ab = bf(ARAW[k])
        for g in range(7, -1, -1):
            lo = (g % 2) * 512
            pb = pbank(ps, lo)
            for w in range(4):
                oc = 4 * g + w
                P.emit("tensor",
                       lambda e, ps=ps, lo=lo, w=w, oc=oc, k=k: e.transpose(
                           ps[:, lo + w * 128:lo + (w + 1) * 128],
                           ARAW[k][:, oc * 128:(oc + 1) * 128], IDF[:]),
                       reads=[ab, bf(IDF)], writes=[pb], sem="t")
            P.emit("scalar", lambda e, ps=ps, lo=lo, g=g, k=k: e.copy(
                       KT2V[k][:, g * 512:(g + 1) * 512], ps[:, lo:lo + 512]),
                   reads=[pb], writes=[ab], sem="s")

    loaded = [False] * len(sched)

    def ensure_load(ci):
        if ci < len(sched) and not loaded[ci]:
            load_chunk(ci)
            loaded[ci] = True

    P.emit("vector", lambda e: e.tensor_scalar(
               SGT[:, 0:NT], S1INV[:, 0:NT], ALG[:], None, ALU.mult),
           reads=[bf(ALG), bf(S1INV)], writes=[bf(SGT)], sem="v")

    pend_tr = []

    for ci, (kind, sb, q) in enumerate(sched):
        for la in range(3):
            ensure_load(ci + la)
        slot = RING[ci % 3]
        if kind in ("g", "u"):
            def emit_gu_mm(k):
                gti = sb * TSB + k
                ps = (PSA, PSB2)[k % 2]
                for i in range(NE):
                    for oc2 in range(2):
                        lo = oc2 * 512
                        P.emit("tensor",
                               lambda e, ps=ps, lo=lo, i=i, gti=gti, slot=slot:
                                   e.matmul(ps[:, lo:lo + 512],
                                            K1T[i][:, gti * 128:(gti + 1) * 128],
                                            slot[i][:, lo:lo + 512],
                                            start=(i == 0), stop=(i == NE - 1)),
                               reads=[bf(K1T[i]), bf(slot[i])],
                               writes=[pbank(ps, lo)], sem="t")

            def emit_gu_epi(k):
                ps = (PSA, PSB2)[k % 2]
                if kind == "g":
                    gti2 = sb * TSB + k
                    P.emit("scalar", lambda e, ps=ps, k=k, gti2=gti2:
                               e.activation(GB[k][:], ps[:], AF.Silu,
                                            scale=SGT[:, gti2:gti2 + 1]),
                           reads=[pbank(ps, 0), pbank(ps, 512), bf(SGT)],
                           writes=[bf(GB[k])], sem="s")
                else:
                    P.emit("vector", lambda e, ps=ps, k=k, q=q: e.tensor_tensor(
                               ARAW[k][:, q * 1024:(q + 1) * 1024], GB[k][:],
                               ps[:], ALU.mult),
                           reads=[bf(GB[k]), pbank(ps, 0), pbank(ps, 512)],
                           writes=[bf(ARAW[k])], sem="v")

            if kind == "u" and q == 3:
                emit_gu_mm(0)
                emit_gu_mm(1)
                emit_gu_epi(0)
                emit_gu_epi(1)
                emit_gu_mm(2)
                emit_gu_mm(3)
                emit_gu_epi(2)
                emit_gu_epi(3)
                bf(ZVF).inherit(bf(OUTB[0]))
                bf(ZVI).inherit(bf(OUTB[1]))
                for k in range(TSB):
                    emit_scales(sb, k)
                    emit_rounds(sb, k)
                    pend_tr.append((sb, k))
            else:
                for k in range(TSB):
                    emit_gu_mm(k)
                    emit_gu_epi(k)
        else:
            if q == 0:
                for _ in range(2):
                    if pend_tr:
                        emit_tr(*pend_tr.pop(0))
            ks = (0, 1) if kind == "dA" else (2, 3)
            for k in ks:
                ps = (PSC, PSD)[k % 2]
                ab = bf(ARAW[k])
                for j in range(8):
                    oc = q * 8 + j
                    for c in range(2):
                        lo = c * 512
                        P.emit("tensor",
                               lambda e, ps=ps, lo=lo, oc=oc, k=k, slot=slot,
                                       j=j, q=q:
                                   e.matmul(ps[:, lo:lo + 512],
                                            KT2V[k][:, oc * 128:(oc + 1) * 128],
                                            slot[j][:, lo:lo + 512],
                                            start=(q == 0 and j == 0),
                                            stop=(q == 3 and j == 7)),
                               reads=[ab, bf(slot[j])],
                               writes=[pbank(ps, lo)], sem="t")
            if q == 3:
                for k in ks:
                    gti = sb * TSB + k
                    ps = (PSC, PSD)[k % 2]
                    ob_ = OUTB[k % 2]
                    if kind == "dA" and k == 0:
                        bf(OUTB[0]).inherit(bf(ZVF))
                        bf(OUTB[1]).inherit(bf(ZVI))
                    for c in range(2):
                        lo = c * 512
                        P.emit("vector",
                               lambda e, ps=ps, lo=lo, ob_=ob_, k=k, gti=gti:
                                   e.scalar_tensor_tensor(
                                       ob_[:, lo:lo + 512], ps[:, lo:lo + 512],
                                       SDT[:, k:k + 1],
                                       H1B[gti][:, lo:lo + 512],
                                       ALU.mult, ALU.add),
                               reads=[pbank(ps, lo), bf(SDT), bf(H1B[gti])],
                               writes=[bf(ob_)], sem="v")
                    emit_ln(ob_, bf(ob_), STAT2, bf(STAT2), ps[:, 0:1024],
                            [pbank(ps, 0), pbank(ps, 512)])
                    P.emit("gpsimd",
                           lambda e, ob_=ob_, gti=gti:
                               e.dma_start(out_h[gti * 128:(gti + 1) * 128, :],
                                           ob_[:]),
                           reads=[bf(ob_)], sem="st", step=16)

    P.q["sync"].append(lambda e, v=P.cnt["st"]: e.wait_ge(P.sem["st"], v))

    with nc.Block() as blk:
        @blk.tensor
        def _(eng):
            for f in P.q["tensor"]:
                f(eng)

        @blk.scalar
        def _(eng):
            for f in P.q["scalar"]:
                f(eng)

        @blk.vector
        def _(eng):
            for f in P.q["vector"]:
                f(eng)

        @blk.gpsimd
        def _(eng):
            for f in P.q["gpsimd"]:
                f(eng)

        @blk.sync
        def _(eng):
            for f in P.q["sync"]:
                f(eng)

    es.close()
    return nc


_CACHE = {}


def _get_nc(debug=False):
    if debug not in _CACHE:
        _CACHE[debug] = build(debug)
    return _CACHE[debug]


def make_inputs(L0_attn_x, L0_attn_W_qkv, L0_attn_b_qkv, L0_ffn_w_gate,
                L0_ffn_w_up, L0_ffn_w_down):
    C, S = _rope_tables()
    perm = np.zeros((128, 128), ml_dtypes.bfloat16)
    for i in range(128):
        perm[i ^ 1, i] = 1.0
    shared = {
        "wqT": np.ascontiguousarray(np.asarray(L0_attn_W_qkv).T),
        "b2": np.ascontiguousarray(np.asarray(L0_attn_b_qkv).reshape(24, 128).T),
        "wgT": np.ascontiguousarray(np.asarray(L0_ffn_w_gate).T),
        "wuT": np.ascontiguousarray(np.asarray(L0_ffn_w_up).T),
        "wdT": np.ascontiguousarray(np.asarray(L0_ffn_w_down).T),
        "Ct": C, "St": S,
        "Pm": perm,
        "Idb": np.eye(128, dtype=ml_dtypes.bfloat16),
        "Idf": np.eye(128, dtype=np.float32),
        "OnesRow": np.ones((1, 128), np.float32),
    }
    x = np.asarray(L0_attn_x)
    return [dict(shared, x=np.ascontiguousarray(x[c])) for c in range(B)]


def kernel(L0_attn_x, L0_attn_W_qkv, L0_attn_b_qkv, L0_ffn_w_gate,
           L0_ffn_w_up, L0_ffn_w_down):
    nc = _get_nc(False)
    in_maps = make_inputs(L0_attn_x, L0_attn_W_qkv, L0_attn_b_qkv,
                          L0_ffn_w_gate, L0_ffn_w_up, L0_ffn_w_down)
    res = run_bass_kernel_spmd(nc, in_maps, core_ids=list(range(B)))
    return np.stack([res.results[c]["out"] for c in range(B)], axis=0)


# version 20
# speedup vs baseline: 1.0145x; 1.0125x over previous
import numpy as np
import ml_dtypes
from contextlib import ExitStack
from functools import reduce as _reduce

import concourse.bass as bass
import concourse.mybir as mybir
from concourse.bass_utils import run_bass_kernel_spmd

F32 = mybir.dt.float32
BF16 = mybir.dt.bfloat16
I32 = mybir.dt.int32
AF = mybir.ActivationFunctionType
ALU = mybir.AluOpType

B, T, E, H, D, INNER = 8, 2048, 1024, 16, 64, 4096
NT = T // 128
NE = E // 128
NI = INNER // 128
TSB = 4
NSB = NT // TSB
KB = 1024
WQN = float(E * INNER)


def _rope_tables():
    import jax
    with jax.default_device(jax.devices("cpu")[0]):
        import jax.numpy as jnp
        freqs = 1.0 / 10000.0 ** (jnp.arange(0, D, 2, dtype=jnp.float32) / D)
        ang = jnp.arange(T, dtype=jnp.float32)[:, None] * freqs[None, :]
        c = np.asarray(jnp.cos(ang))
        s = np.asarray(jnp.sin(ang))
    C = np.empty((128, T), np.float32)
    S = np.empty((128, T), np.float32)
    for p in range(128):
        d = p % 64
        j = d // 2
        C[p] = c[:, j]
        S[p] = (s[:, j] if (d % 2 == 1) else -s[:, j])
    return C, S


class Buf:
    __slots__ = ("w", "readers", "war_extra")

    def __init__(self):
        self.w = None
        self.readers = {}
        self.war_extra = {}

    def inherit(self, *olds):
        for o in olds:
            if o.w:
                s, v = o.w
                self.war_extra[s] = max(self.war_extra.get(s, 0), v)
            for s, v in o.readers.items():
                self.war_extra[s] = max(self.war_extra.get(s, 0), v)
        return self


class Prog:
    ENGS = ("tensor", "scalar", "vector", "gpsimd", "sync")

    def __init__(self, nc, es):
        self.nc = nc
        self.q = {e: [] for e in self.ENGS}
        self.sem = {}
        self.cnt = {}
        self.wm = {}
        for s in ("t", "s", "v", "g", "ld", "lg", "st"):
            self.sem[s] = es.enter_context(nc.semaphore("sem_" + s))
            self.cnt[s] = 0

    def emit(self, eng, builder, reads=(), writes=(), sem=None, step=1,
             extra=()):
        waits = {}

        def add(dep):
            if dep:
                s, v = dep
                waits[s] = max(waits.get(s, 0), v)

        for b in reads:
            add(b.w)
        for b in writes:
            add(b.w)
            for s, v in b.readers.items():
                add((s, v))
            for s, v in b.war_extra.items():
                add((s, v))
        for d in extra:
            add(d)
        assert sem is not None
        self.cnt[sem] += step
        dep = (sem, self.cnt[sem])
        wlist = []
        for s, v in waits.items():
            if v > self.wm.get((eng, s), 0):
                self.wm[(eng, s)] = v
                wlist.append((self.sem[s], v))
        semh = self.sem[sem]

        def f(e, wlist=wlist, builder=builder, semh=semh, step=step):
            for h, v in wlist:
                e.wait_ge(h, v)
            builder(e).then_inc(semh, step)

        self.q[eng].append(f)
        for b in reads:
            s, v = dep
            b.readers[s] = max(b.readers.get(s, 0), v)
        for b in writes:
            b.w = dep
            b.readers = {}
            b.war_extra = {}
        return dep

    def wait(self, eng, dep):
        s, v = dep
        if v > self.wm.get((eng, s), 0):
            self.wm[(eng, s)] = v
            h = self.sem[s]
            self.q[eng].append(lambda e, h=h, v=v: e.wait_ge(h, v))


class Arena:
    def __init__(self, nc, size_bytes):
        self.nc = nc
        self.size = size_bytes
        self.tensors = []

    def at(self, name, off, shape, dtype):
        per = _reduce(lambda a, b: a * b, shape[1:]) * mybir.dt.size(dtype)
        assert off % 32 == 0, (name, off)
        assert off + per <= self.size, (name, off, per, self.size)
        h = self.nc.virtual_sbuf_tensor(name, shape, dtype)
        self.tensors.append((h, off))
        return h

    def allocate(self, es):
        self.handle = es.enter_context(
            self.nc.sbuf_tensor("arena", [128, self.size], mybir.dt.uint8))
        base = self.nc.lookup_mloc(self.handle).addr
        for h, off in self.tensors:
            self.nc._tensor(h.name, h.shape, h.dtype, type="SB")
            mloc = self.nc.lookup_mloc(h)
            mloc.allocated = not self.nc.target_bir_lowering
            mloc.addr = base + off


def build(debug=False):
    nc = bass.Bass("TRN2", target_bir_lowering=False)

    x_h = nc.dram_tensor("x", [T, E], F32, kind="ExternalInput").ap()
    wq_h = nc.dram_tensor("wqT", [E, 3 * E], F32, kind="ExternalInput").ap()
    b2_h = nc.dram_tensor("b2", [128, 24], F32, kind="ExternalInput").ap()
    wg_h = nc.dram_tensor("wgT", [E, INNER], F32, kind="ExternalInput").ap()
    wu_h = nc.dram_tensor("wuT", [E, INNER], F32, kind="ExternalInput").ap()
    wd_h = nc.dram_tensor("wdT", [INNER, E], F32, kind="ExternalInput").ap()
    ct_h = nc.dram_tensor("Ct", [128, T], F32, kind="ExternalInput").ap()
    st_h = nc.dram_tensor("St", [128, T], F32, kind="ExternalInput").ap()
    pm_h = nc.dram_tensor("Pm", [128, 128], BF16, kind="ExternalInput").ap()
    idb_h = nc.dram_tensor("Idb", [128, 128], BF16, kind="ExternalInput").ap()
    idf_h = nc.dram_tensor("Idf", [128, 128], F32, kind="ExternalInput").ap()
    or_h = nc.dram_tensor("OnesRow", [1, 128], F32, kind="ExternalInput").ap()
    out_h = nc.dram_tensor("out", [T, E], F32, kind="ExternalOutput").ap()
    tk = "ExternalOutput" if debug else "Internal"
    tg_h = nc.dram_tensor("ternG", [E, INNER], BF16, kind=tk).ap()
    tu_h = nc.dram_tensor("ternU", [E, INNER], BF16, kind=tk).ap()
    td_h = nc.dram_tensor("ternD", [INNER, E], BF16, kind=tk).ap()
    dbg = {}
    if debug:
        dbg["d_qT"] = nc.dram_tensor("d_qT", [E, T], BF16, kind="ExternalOutput").ap()
        dbg["d_kT"] = nc.dram_tensor("d_kT", [E, T], BF16, kind="ExternalOutput").ap()
        dbg["d_vtok"] = nc.dram_tensor("d_vtok", [T, 1040], BF16, kind="ExternalOutput").ap()
        dbg["d_h1"] = nc.dram_tensor("d_h1", [T, E], F32, kind="ExternalOutput").ap()
        dbg["d_k1T"] = nc.dram_tensor("d_k1T", [E, T], BF16, kind="ExternalOutput").ap()
        dbg["d_sca"] = nc.dram_tensor("d_sca", [1, 16], F32, kind="ExternalOutput").ap()
        for _m in range(3):
            dbg[f"d_wacc{_m}"] = nc.dram_tensor(f"d_wacc{_m}", [128, 1], F32,
                                                kind="ExternalOutput").ap()
        dbg["d_xatt"] = nc.dram_tensor("d_xatt", [T, E], F32, kind="ExternalOutput").ap()

    ar = Arena(nc, 206 * KB)
    A = ar.at

    _o = [0]
    def sm(name, shape, dtype):
        h = A(name, _o[0], shape, dtype)
        per = _reduce(lambda a, b: a * b, shape[1:]) * mybir.dt.size(dtype)
        _o[0] += (per + 31) & ~31
        return h
    PERM = sm("PERM", [128, 128], BF16)
    IDB = sm("IDB", [128, 128], BF16)
    IDF = sm("IDF", [128, 128], F32)
    ONESC = sm("ONESC", [128, 1], BF16)
    ONESR = sm("ONESR", [1, 128], F32)
    B2 = sm("B2", [128, 24], F32)
    CH2Q = sm("CH2Q", [1, 16], F32)
    CH2K = sm("CH2K", [1, 16], F32)
    CROW = sm("CROW", [1, 16], F32)
    CROWP = sm("CROWP", [1, 8], F32)
    CBIASP = sm("CBIASP", [128, 8], F32)
    S1INV = sm("S1INV", [128, NT], F32)
    S1S = sm("S1S", [128, NT], F32)
    ALG = sm("ALG", [128, 1], F32)
    ALUp = sm("ALUp", [128, 1], F32)
    ALD = sm("ALD", [128, 1], F32)
    RABG = sm("RABG", [128, 1], F32)
    RABU = sm("RABU", [128, 1], F32)
    RABD = sm("RABD", [128, 1], F32)
    WACC = sm("WACC", [128, 1], F32)
    SCA = sm("SCA", [1, 16], F32)
    C1285 = sm("C1285", [128, 1], F32)
    C05 = sm("C05", [128, 1], F32)
    CM05 = sm("CM05", [128, 1], F32)
    SGT = sm("SGT", [128, NT], F32)
    SUT = sm("SUT", [128, TSB], F32)
    S2E = sm("S2E", [128, TSB], F32)
    SDT = sm("SDT", [128, TSB], F32)
    STAT = sm("STAT", [128, 8], F32)
    STAT2 = sm("STAT2", [128, 8], F32)
    GSTAT = sm("GSTAT", [128, 2], F32)
    assert _o[0] <= 6 * KB, _o[0]

    CBT = A("CbT", 6 * KB, [128, T], BF16)
    SBT = A("SbT", 10 * KB, [128, T], BF16)
    PT = [A(f"PT{i}", 6 * KB + i * 2 * KB, [128, 1024], BF16) for i in range(4)]
    OBUF = [A(f"OB{i}", 86 * KB + i * 2 * KB, [65, 512], F32) for i in range(2)]

    CF = A("Cf", 14 * KB, [128, T], F32)
    SF = A("Sf", 22 * KB, [128, T], F32)
    WQ = [A(f"Wq{i}", 14 * KB + i * 6 * KB, [128, 3 * E], BF16) for i in range(NE)]
    XB = [A(f"xb{t}", 14 * KB + t * 4 * KB, [128, E], F32) for t in range(NT)]
    ARAW = [A(f"araw{k}", 14 * KB + k * 16 * KB, [128, INNER], F32)
            for k in range(TSB)]
    KT2V = [A(f"k2tv{k}", 14 * KB + k * 16 * KB + 8 * KB, [128, INNER], BF16)
            for k in range(TSB)]

    XT = [A(f"xT{i}", 62 * KB + i * 4 * KB, [128, T], BF16) for i in range(NE)]
    WST = [A(f"wst{i}", 78 * KB + i * 4 * KB, [128, 1024], F32) for i in range(2)]
    CBQ = A("CBQ", 78 * KB, [1, 2048], F32)
    CBK = A("CBK", 86 * KB, [1, 2048], F32)
    SQS = A("sqs", 6 * KB, [128, 1024], BF16)
    SQS2 = A("sqs2", 8 * KB, [128, 1024], BF16)
    MST = [A(f"mst{i}", 200 * KB + i * 2 * KB, [128, 1024], BF16) for i in range(2)]
    WA_ = A("wa", 90 * KB, [128, 1024], BF16)
    WB_ = A("wb", 92 * KB, [128, 1024], BF16)
    ZSCR = A("zscr", 78 * KB, [128, E], F32)
    ZI1 = A("zi1", 82 * KB, [128, E], I32)
    M1 = A("m1", 86 * KB, [128, E], BF16)

    QT = [A(f"qT{i}", 94 * KB + i * 4 * KB, [128, T], BF16) for i in range(NE)]
    K1T = [A(f"k1T{i}", 94 * KB + i * 4 * KB, [128, T], BF16) for i in range(NE)]
    KT = [A(f"kT{i}", 126 * KB + i * 4 * KB, [128, T], BF16) for i in range(NE)]
    H1B = [A(f"h1b{t}", 126 * KB + t * 2 * KB, [128, E], BF16) for t in range(NT)]
    VTOK = [A(f"vt{t}", 158 * KB + t * 2144, [128, 1040], BF16) for t in range(NT)]
    RING = [[A(f"rg{r}_{j}", (158 * KB + r * 16 * KB if r < 2 else 78 * KB)
               + j * 2 * KB, [128, 1024], BF16) for j in range(8)]
            for r in range(3)]
    XBF = [A(f"xbf{i}", 192 * KB + i * 4 * KB, [128, E], F32) for i in range(2)]
    VSCR = A("vscr", 192 * KB, [128, T], F32)
    GB = [A(f"gb{k}", 190 * KB + k * 2 * KB, [128, 1024], BF16)
          for k in range(TSB)]
    TOUT = [A(f"tout{i}", 200 * KB + i * 2 * KB, [128, 1024], BF16) for i in range(2)]
    WAV = A("wav", 204 * KB, [128, 1024], BF16)
    OUTB = [A(f"outb{o}", 198 * KB + o * 4 * KB, [128, E], F32) for o in range(2)]
    ZVF = A("zvf", 198 * KB, [128, 1024], F32)
    ZVI = A("zvi", 202 * KB, [128, 1024], I32)
    ZGF = A("zgf", 6 * KB, [128, 1024], F32)
    ZGI = A("zgi", 10 * KB, [128, 1024], I32)

    es = ExitStack()
    PSA = es.enter_context(nc.psum_tensor("psA", [128, 1024], F32))
    PSB2 = es.enter_context(nc.psum_tensor("psB", [128, 1024], F32))
    PSC = es.enter_context(nc.psum_tensor("psC", [128, 1024], F32))
    PSD = es.enter_context(nc.psum_tensor("psD", [128, 1024], F32))
    ar.allocate(es)
    P = Prog(nc, es)

    bufs = {}
    def bf(h):
        if id(h) not in bufs:
            bufs[id(h)] = Buf()
        return bufs[id(h)]

    PSBANK = {}
    def pbank(tensor, lo):
        key = (id(tensor), lo // 512)
        if key not in PSBANK:
            PSBANK[key] = Buf()
        return PSBANK[key]

    HB = {}
    def hbuf(name):
        if name not in HB:
            HB[name] = Buf()
        return HB[name]

    def dma_ld(dst_ap, src_ap, reads=(), writes=()):
        P.emit("sync", lambda e, d=dst_ap, s=src_ap: e.dma_start(d, s),
               reads=reads, writes=writes, sem="ld", step=16)

    def dma_st(dst_ap, src_ap, reads=(), writes=()):
        P.emit("sync", lambda e, d=dst_ap, s=src_ap: e.dma_start(d, s),
               reads=reads, writes=writes, sem="st", step=16)

    def gdma(dst_ap, src_ap, reads=(), writes=()):
        P.emit("gpsimd", lambda e, d=dst_ap, s=src_ap: e.dma_start(d, s),
               reads=reads, writes=writes, sem="lg", step=16)

    for dst, src in ((PERM, pm_h), (IDB, idb_h), (IDF, idf_h),
                     (ONESR, or_h), (B2, b2_h), (CF, ct_h), (SF, st_h)):
        dma_ld(dst[:], src[:], writes=[bf(dst)])
    P.emit("vector", lambda e: e.memset(ONESC[:], 1.0), writes=[bf(ONESC)], sem="v")
    P.emit("vector", lambda e: e.memset(C1285[:], 128.5), writes=[bf(C1285)], sem="v")
    P.emit("vector", lambda e: e.memset(C05[:], 0.5), writes=[bf(C05)], sem="v")
    P.emit("vector", lambda e: e.memset(CM05[:], -0.5), writes=[bf(CM05)], sem="v")
    P.emit("vector", lambda e: e.tensor_copy(CBT[:], CF[:]),
           reads=[bf(CF)], writes=[bf(CBT)], sem="v")
    P.emit("vector", lambda e: e.tensor_copy(SBT[:], SF[:]),
           reads=[bf(SF)], writes=[bf(SBT)], sem="v")
    for t in range(NT):
        P.emit("vector", lambda e, t=t: e.memset(VTOK[t][:], 1.0),
               writes=[bf(VTOK[t])], sem="v")

    for t in range(NT):
        xb = XBF[t % 2]
        dma_ld(xb[:], x_h[t * 128:(t + 1) * 128, :], writes=[bf(xb)])
        for i in range(NE):
            pp = (t * NE + i) % 2
            pslice = PSD[:, pp * 512: pp * 512 + 128]
            pb = pbank(PSD, pp * 512)
            P.emit("tensor",
                   lambda e, p=pslice, xb=xb, i=i:
                       e.transpose(p, xb[:, i * 128:(i + 1) * 128], IDF[:]),
                   reads=[bf(xb), bf(IDF)], writes=[pb], sem="t")
            P.emit("vector",
                   lambda e, p=pslice, i=i, t=t:
                       e.tensor_copy(XT[i][:, t * 128:(t + 1) * 128], p),
                   reads=[pb], writes=[bf(XT[i])], sem="v")

    NCH = 32

    def wslice(m, ch):
        wh = (wg_h, wu_h, wd_h)[m]
        if m < 2:
            i, q = ch // 4, ch % 4
            return wh[i * 128:(i + 1) * 128, q * 1024:(q + 1) * 1024]
        return wh[ch * 128:(ch + 1) * 128, :]

    def tslice(m, ch):
        th = (tg_h, tu_h, td_h)[m]
        if m < 2:
            i, q = ch // 4, ch % 4
            return th[i * 128:(i + 1) * 128, q * 1024:(q + 1) * 1024]
        return th[ch * 128:(ch + 1) * 128, :]

    mean_wl = []

    def emit_mean(m, ch):
        w = MST[ch % 2]
        gdma(w[:], wslice(m, ch), reads=[hbuf(f"w{m}")], writes=[bf(w)])
        if ch == 0:
            P.emit("vector", lambda e: e.memset(WACC[:], 0.0),
                   writes=[bf(WACC)], sem="v")
        P.emit("vector",
               lambda e, w=w: e.reduce_sum(GSTAT[:, 0:1], w[:],
                                           axis=mybir.AxisListType.X,
                                           apply_absolute_value=True),
               reads=[bf(w)], writes=[bf(GSTAT)], sem="v")
        P.emit("vector", lambda e: e.tensor_add(WACC[:], WACC[:], GSTAT[:, 0:1]),
               reads=[bf(GSTAT), bf(WACC)], writes=[bf(WACC)], sem="v")
        if ch == NCH - 1:
            pb = pbank(PSD, 0)
            P.emit("tensor", lambda e: e.transpose(PSD[0:1, 0:128], WACC[:], IDF[:]),
                   reads=[bf(WACC), bf(IDF)], writes=[pb], sem="t")
            P.emit("vector", lambda e: e.reduce_sum(SCA[0:1, 0:1], PSD[0:1, 0:128],
                                                    axis=mybir.AxisListType.X),
                   reads=[pb], writes=[bf(SCA)], sem="v")
            P.emit("vector", lambda e, m=m: e.tensor_copy(
                       SCA[0:1, 10 + m:11 + m], SCA[0:1, 0:1]),
                   reads=[bf(SCA)], writes=[bf(SCA)], sem="v")
            if debug:
                dma_st(dbg[f"d_wacc{m}"][:], WACC[:], reads=[bf(WACC)])
            P.emit("vector", lambda e, m=m: e.tensor_scalar(
                       SCA[0:1, 1 + m:2 + m], SCA[0:1, 0:1], 1.0 / WQN, 1e-5,
                       ALU.mult, ALU.max),
                   reads=[bf(SCA)], writes=[bf(SCA)], sem="v")
            P.emit("vector", lambda e, m=m: e.reciprocal(
                       SCA[0:1, 4 + m:5 + m], SCA[0:1, 1 + m:2 + m]),
                   reads=[bf(SCA)], writes=[bf(SCA)], sem="v")
            P.emit("vector", lambda e, m=m: e.tensor_single_scalar(
                       SCA[0:1, 7 + m:8 + m], SCA[0:1, 1 + m:2 + m], 0.5,
                       ALU.mult),
                   reads=[bf(SCA)], writes=[bf(SCA)], sem="v")
            for col, dst in ((7 + m, (ALG, ALUp, ALD)[m]),
                             (4 + m, (RABG, RABU, RABD)[m])):
                pb2 = pbank(PSD, 512)
                P.emit("tensor", lambda e, col=col: e.matmul(
                           PSD[0:128, 512:513], ONESR[:], SCA[0:1, col:col + 1],
                           start=True, stop=True),
                       reads=[bf(SCA), bf(ONESR)], writes=[pb2], sem="t")
                P.emit("vector", lambda e, dst=dst: e.tensor_copy(
                           dst[:], PSD[0:128, 512:513]),
                       reads=[pb2], writes=[bf(dst)], sem="v")

    for m in range(3):
        for ch in range(NCH):
            mean_wl.append((emit_mean, m, ch))

    def emit_quant(m, ch):
        w = WST[0]
        rab = (RABG, RABU, RABD)[m]
        to = TOUT[0]
        if m == 0 and ch == 0:
            bf(WST[0]).inherit(bf(XT[4]), bf(CBQ))
            bf(WST[1]).inherit(bf(XT[5]), bf(CBQ))
            bf(WA_).inherit(bf(XT[7]), bf(CBK))
            bf(WB_).inherit(bf(XT[7]), bf(CBK))
            bf(TOUT[0]).inherit(bf(MST[0]), bf(MST[1]))
            bf(TOUT[1]).inherit(bf(MST[1]))
        dma_ld(w[:], wslice(m, ch), reads=[hbuf(f"w{m}")], writes=[bf(w)])
        P.emit("scalar", lambda e, w=w, rab=rab: e.activation(
                   WA_[:], w[:], AF.Sign, bias=C05[:], scale=rab[:]),
               reads=[bf(w), bf(rab), bf(C05)], writes=[bf(WA_)], sem="s")
        P.emit("scalar", lambda e, w=w, rab=rab: e.activation(
                   WB_[:], w[:], AF.Sign, bias=CM05[:], scale=rab[:]),
               reads=[bf(w), bf(rab), bf(CM05)], writes=[bf(WB_)], sem="s")
        P.emit("vector", lambda e, to=to: e.tensor_tensor(
                   to[:], WA_[:], WB_[:], ALU.add),
               reads=[bf(WA_), bf(WB_)], writes=[bf(to)], sem="v")
        dma_st(tslice(m, ch), to[:], reads=[bf(to)], writes=[hbuf(f"t{m}")])

    def emit_quant_v(m, ch):
        w = WST[ch % 2]
        rab = (RABG, RABU, RABD)[m]
        to = TOUT[ch % 2]
        if m == 0 and ch == 0:
            bf(WST[0]).inherit(bf(XT[4]), bf(CBQ))
            bf(WST[1]).inherit(bf(XT[5]), bf(CBQ))
            bf(WA_).inherit(bf(XT[7]), bf(CBK))
            bf(WB_).inherit(bf(XT[7]), bf(CBK))
            bf(TOUT[0]).inherit(bf(MST[0]), bf(MST[1]))
            bf(TOUT[1]).inherit(bf(MST[1]))
        dma_ld(w[:], wslice(m, ch), reads=[hbuf(f"w{m}")], writes=[bf(w)])
        P.emit("vector", lambda e, w=w, rab=rab: e.tensor_scalar(
                   WAV[:], w[:], rab[:], 0.5, ALU.mult, ALU.is_ge),
               reads=[bf(w), bf(rab)], writes=[bf(WAV)], sem="v")
        P.emit("vector", lambda e, w=w, rab=rab, to=to: e.tensor_scalar(
                   to[:], w[:], rab[:], -0.5, ALU.mult, ALU.is_ge),
               reads=[bf(w), bf(rab)], writes=[bf(to)], sem="v")
        P.emit("vector", lambda e, to=to: e.tensor_tensor(
                   to[:], to[:], WAV[:], ALU.add),
               reads=[bf(to), bf(WAV)], writes=[bf(to)], sem="v")
        P.emit("vector", lambda e, to=to: e.tensor_scalar(
                   to[:], to[:], 2.0, -2.0, ALU.mult, ALU.add),
               reads=[bf(to)], writes=[bf(to)], sem="v")
        dma_st(tslice(m, ch), to[:], reads=[bf(to)], writes=[hbuf(f"t{m}")])

    wq_s = []
    wq_v = [(emit_quant_v, m, ch) for m in (0, 1, 2) for ch in range(NCH)]

    def drain(wl, n):
        for _ in range(min(n, len(wl))):
            fn, m, ch = wl.pop(0)
            fn(m, ch)

    bf(VSCR).inherit(bf(XBF[0]), bf(XBF[1]))
    for i in range(3):
        bf(WQ[i]).inherit(bf(CF), bf(SF))
    for i in range(NE):
        gdma(WQ[i][:], wq_h[i * 128:(i + 1) * 128, :], writes=[bf(WQ[i])])

    def emit_rope(tl):
        tb = bf(tl)
        for h in range(2):
            for lo in (0, 512):
                pb = pbank(PSC, lo)
                P.emit("tensor",
                       lambda e, lo=lo, h=h, tl=tl:
                           e.matmul(PSC[:, lo:lo + 512], PERM[:],
                                    tl[:, h * 1024 + lo:h * 1024 + lo + 512],
                                    start=True, stop=True),
                       reads=[bf(PERM), tb], writes=[pb], sem="t")
                P.emit("vector",
                       lambda e, lo=lo, h=h:
                           e.tensor_tensor(
                               WAV[:, lo:lo + 512], PSC[:, lo:lo + 512],
                               SBT[:, h * 1024 + lo:h * 1024 + lo + 512],
                               ALU.mult),
                       reads=[pb, bf(SBT)], writes=[bf(WAV)], sem="v")
            P.emit("vector",
                   lambda e, h=h, tl=tl:
                       e.tensor_tensor(tl[:, h * 1024:(h + 1) * 1024],
                                       tl[:, h * 1024:(h + 1) * 1024],
                                       CBT[:, h * 1024:(h + 1) * 1024],
                                       ALU.mult),
                   reads=[tb, bf(CBT)], writes=[tb], sem="v")
            P.emit("vector",
                   lambda e, h=h, tl=tl:
                       e.tensor_tensor(tl[:, h * 1024:(h + 1) * 1024],
                                       tl[:, h * 1024:(h + 1) * 1024],
                                       WAV[:], ALU.add),
                   reads=[tb, bf(WAV)], writes=[tb], sem="v")

    QKVPS = [(PSA, 0), (PSA, 512), (PSB2, 0), (PSB2, 512)]
    for ot in range(24):
        for i in range(NE):
            for tc in range(4):
                ps, lo = QKVPS[tc]
                P.emit("tensor",
                       lambda e, ps=ps, lo=lo, i=i, ot=ot, tc=tc:
                           e.matmul(ps[:, lo:lo + 512],
                                    WQ[i][:, ot * 128:(ot + 1) * 128],
                                    XT[i][:, tc * 512:(tc + 1) * 512],
                                    start=(i == 0), stop=(i == NE - 1)),
                       reads=[bf(WQ[i]), bf(XT[i])], writes=[pbank(ps, lo)],
                       sem="t")
        for tc in range(4):
            ps, lo = QKVPS[tc]
            if ot < 8:
                dst, db = QT[ot], bf(QT[ot])
            elif ot < 16:
                dst, db = KT[ot - 8], bf(KT[ot - 8])
            else:
                dst, db = VSCR, bf(VSCR)
            P.emit("scalar",
                   lambda e, ps=ps, lo=lo, dst=dst, ot=ot, tc=tc:
                       e.activation(dst[:, tc * 512:(tc + 1) * 512],
                                    ps[:, lo:lo + 512], AF.Identity,
                                    bias=B2[:, ot:ot + 1], scale=1.0),
                   reads=[pbank(ps, lo), bf(B2)], writes=[db], sem="s")
        if ot >= 16:
            h0 = 2 * (ot - 16)
            for kt in range(NT):
                pp = kt % 2
                pslice = PSD[:, pp * 512: pp * 512 + 128]
                pb = pbank(PSD, pp * 512)
                P.emit("tensor",
                       lambda e, p=pslice, kt=kt:
                           e.transpose(p, VSCR[:, kt * 128:(kt + 1) * 128], IDF[:]),
                       reads=[bf(VSCR), bf(IDF)], writes=[pb], sem="t")
                for hh in range(2):
                    P.emit("vector",
                           lambda e, p=pslice, kt=kt, h=h0 + hh, hh=hh:
                               e.tensor_copy(VTOK[kt][:, h * 65: h * 65 + 64],
                                             p[:, hh * 64:(hh + 1) * 64]),
                           reads=[pb], writes=[bf(VTOK[kt])], sem="v")
        if 1 <= ot <= 8:
            emit_rope(QT[ot - 1])
        elif 9 <= ot <= 16:
            emit_rope(KT[ot - 9])
        drain(mean_wl, 4)

    bf(CBQ).inherit(bf(XT[4]), bf(XT[5]))
    bf(SQS).inherit(bf(CBT))
    bf(SQS2).inherit(bf(CBT))
    CPS = [(PSC, 0), (PSC, 512), (PSD, 0), (PSD, 512)]
    for tiles, chrow in ((QT, CH2Q), (KT, CH2K)):
        for hp in range(NE):
            for hh in range(2):
                sq = (SQS, SQS2)[hh]
                P.emit("scalar",
                       lambda e, tl=tiles[hp], hh=hh, sq=sq:
                           e.activation(sq[:], tl[:, hh * 1024:(hh + 1) * 1024],
                                        AF.Square),
                       reads=[bf(tiles[hp])], writes=[bf(sq)], sem="s")
                for side in range(2):
                    for cc in range(2):
                        ps, lo = CPS[side * 2 + cc]
                        P.emit("tensor",
                               lambda e, ps=ps, lo=lo, cc=cc, side=side, sq=sq:
                                   e.matmul(ps[0:1, lo:lo + 512],
                                            ONESC[side * 64:(side + 1) * 64, 0:1],
                                            sq[side * 64:(side + 1) * 64,
                                                cc * 512:(cc + 1) * 512],
                                            start=True, stop=True),
                               reads=[bf(ONESC), bf(sq)],
                               writes=[pbank(ps, lo)], sem="t")
                for side in range(2):
                    for cc in range(2):
                        ps, lo = CPS[side * 2 + cc]
                        P.emit("vector",
                               lambda e, ps=ps, lo=lo, hh=hh, cc=cc, side=side:
                                   e.tensor_copy(
                                       CBQ[0:1, (hh * 2 + cc) * 512:
                                           (hh * 2 + cc + 1) * 512],
                                       ps[0:1, lo:lo + 512])
                                   if side == 0 else
                                   e.tensor_copy(
                                       CBK[0:1, (hh * 2 + cc) * 512:
                                           (hh * 2 + cc + 1) * 512],
                                       ps[0:1, lo:lo + 512]),
                               reads=[pbank(ps, lo)],
                               writes=[bf(CBQ) if side == 0 else bf(CBK)],
                               sem="v")
            P.emit("vector", lambda e, hp=hp, chrow=chrow: e.reduce_max(
                       chrow[0:1, 2 * hp:2 * hp + 1], CBQ[:],
                       axis=mybir.AxisListType.X),
                   reads=[bf(CBQ)], writes=[bf(chrow)], sem="v")
            P.emit("vector", lambda e, hp=hp, chrow=chrow: e.reduce_max(
                       chrow[0:1, 2 * hp + 1:2 * hp + 2], CBK[:],
                       axis=mybir.AxisListType.X),
                   reads=[bf(CBK)], writes=[bf(chrow)], sem="v")
    P.emit("vector", lambda e: e.tensor_tensor(CROW[:], CH2Q[:], CH2K[:],
                                               ALU.mult),
           reads=[bf(CH2Q), bf(CH2K)], writes=[bf(CROW)], sem="v")
    P.emit("scalar", lambda e: e.activation(CROW[:], CROW[:], AF.Sqrt,
                                            scale=1.0 / 64.0),
           reads=[bf(CROW)], writes=[bf(CROW)], sem="s")
    P.emit("vector", lambda e: e.tensor_tensor(CROWP[:], CROW[0:1, 0:16:2],
                                               CROW[0:1, 1:16:2], ALU.max),
           reads=[bf(CROW)], writes=[bf(CROWP)], sem="v")
    P.emit("vector", lambda e: e.tensor_single_scalar(CROWP[:], CROWP[:], -1.0,
                                                      ALU.mult),
           reads=[bf(CROWP)], writes=[bf(CROWP)], sem="v")
    pbX = pbank(PSD, 0)
    P.emit("tensor", lambda e: e.matmul(PSD[0:128, 0:8], ONESR[:], CROWP[:],
                                        start=True, stop=True),
           reads=[bf(ONESR), bf(CROWP)], writes=[pbX], sem="t")
    P.emit("vector", lambda e: e.tensor_copy(CBIASP[:], PSD[0:128, 0:8]),
           reads=[pbX], writes=[bf(CBIASP)], sem="v")

    for t in range(NT):
        bf(XB[t]).inherit(*[bf(WQ[i]) for i in range(NE)],
                          *[bf(XT[i]) for i in range(4)])
        dma_ld(XB[t][:], x_h[t * 128:(t + 1) * 128, :], writes=[bf(XB[t])])
    for pt in PT:
        bf(pt).inherit(bf(CBT), bf(SBT), bf(SQS), bf(SQS2))
    for ob in OBUF:
        bf(ob).inherit(bf(XT[6]), bf(CBK))

    for hp in range(NE):
        h0, h1 = 2 * hp, 2 * hp + 1
        for qc in range(4):
            def emit_S(kt):
                ps = PSA if kt % 2 == 0 else PSB2
                for side in range(2):
                    lo = side * 512
                    P.emit("tensor",
                           lambda e, ps=ps, lo=lo, side=side, kt=kt, qc=qc, hp=hp:
                               e.matmul(ps[:, lo:lo + 512],
                                        KT[hp][side * 64:(side + 1) * 64,
                                               kt * 128:(kt + 1) * 128],
                                        QT[hp][side * 64:(side + 1) * 64,
                                               qc * 512:(qc + 1) * 512],
                                        start=True, stop=True,
                                        tile_position=(side * 64, 0)),
                           reads=[bf(KT[hp]), bf(QT[hp])],
                           writes=[pbank(ps, lo)], sem="t")

            def emit_exp(kt):
                ps = PSA if kt % 2 == 0 else PSB2
                pt = PT[kt % 4]
                P.emit("scalar",
                       lambda e, ps=ps, pt=pt, hp=hp:
                           e.activation(pt[:], ps[:], AF.Exp,
                                        bias=CBIASP[:, hp:hp + 1], scale=0.125),
                       reads=[pbank(ps, 0), pbank(ps, 512), bf(CBIASP)],
                       writes=[bf(pt)], sem="s")

            def emit_PV(kt):
                pt = PT[kt % 4]
                for side, h in ((0, h0), (1, h1)):
                    lo = side * 512
                    P.emit("tensor",
                           lambda e, pt=pt, lo=lo, h=h, kt=kt:
                               e.matmul(PSC[0:65, lo:lo + 512],
                                        VTOK[kt][:, h * 65: h * 65 + 65],
                                        pt[:, lo:lo + 512],
                                        start=(kt == 0), stop=(kt == NT - 1)),
                           reads=[bf(VTOK[kt]), bf(pt)],
                           writes=[pbank(PSC, lo)], sem="t")

            emit_S(0)
            emit_exp(0)
            for kt in range(1, NT):
                emit_S(kt)
                emit_PV(kt - 1)
                emit_exp(kt)
            emit_PV(NT - 1)

            for side, h in ((0, h0), (1, h1)):
                lo = side * 512
                ob = OBUF[side]
                P.emit("vector",
                       lambda e, ob=ob, lo=lo:
                           e.tensor_copy(ob[:], PSC[0:65, lo:lo + 512]),
                       reads=[pbank(PSC, lo)], writes=[bf(ob)], sem="v")
                for j in range(4):
                    lo2 = (j % 2) * 512
                    pb = pbank(PSD, lo2)
                    P.emit("tensor",
                           lambda e, ob=ob, j=j, lo2=lo2:
                               e.transpose(PSD[:, lo2:lo2 + 65],
                                           ob[0:65, j * 128:(j + 1) * 128],
                                           IDF[0:65, 0:65]),
                           reads=[bf(ob), bf(IDF)], writes=[pb], sem="t")
                    P.emit("vector",
                           lambda e, lo2=lo2:
                               e.reciprocal(STAT[:, 0:1], PSD[:, lo2 + 64:lo2 + 65]),
                           reads=[pb], writes=[bf(STAT)], sem="v")
                    tb = XB[qc * 4 + j]
                    P.emit("vector",
                           lambda e, lo2=lo2, tb=tb, h=h:
                               e.scalar_tensor_tensor(
                                   tb[:, h * 64:(h + 1) * 64],
                                   PSD[:, lo2:lo2 + 64], STAT[:, 0:1],
                                   tb[:, h * 64:(h + 1) * 64],
                                   ALU.mult, ALU.add),
                           reads=[pb, bf(STAT), bf(tb)], writes=[bf(tb)],
                           sem="v")
            drain(wq_v, 3)
    drain(wq_s, len(wq_s))
    drain(wq_v, len(wq_v))

    if debug:
        for i in range(NE):
            dma_st(dbg["d_qT"][i * 128:(i + 1) * 128, :], QT[i][:], reads=[bf(QT[i])])
            dma_st(dbg["d_kT"][i * 128:(i + 1) * 128, :], KT[i][:], reads=[bf(KT[i])])
        for t in range(NT):
            dma_st(dbg["d_vtok"][t * 128:(t + 1) * 128, :], VTOK[t][:],
                   reads=[bf(VTOK[t])])
            dma_st(dbg["d_xatt"][t * 128:(t + 1) * 128, :], XB[t][:],
                   reads=[bf(XB[t])])
        dma_st(dbg["d_sca"][:], SCA[:], reads=[bf(SCA)])

    def emit_ln(tile, tbuf, st, stb, sq, sqb):
        P.emit("vector", lambda e: e.reduce_sum(st[:, 0:1], tile[:],
                                                axis=mybir.AxisListType.X),
               reads=[tbuf], writes=[stb], sem="v")
        P.emit("scalar", lambda e: e.activation(sq, tile[:], AF.Square,
                                                accum_out=st[:, 1:2]),
               reads=[tbuf], writes=[stb] + list(sqb), sem="s")
        P.emit("vector", lambda e: e.tensor_single_scalar(
                   st[:, 2:3], st[:, 0:1], 1.0 / float(E), ALU.mult),
               reads=[stb], writes=[stb], sem="v")
        P.emit("vector", lambda e: e.tensor_tensor(st[:, 3:4], st[:, 2:3],
                                                   st[:, 2:3], ALU.mult),
               reads=[stb], writes=[stb], sem="v")
        P.emit("vector", lambda e: e.scalar_tensor_tensor(
                   st[:, 4:5], st[:, 1:2], 1.0 / float(E), st[:, 3:4],
                   ALU.mult, ALU.subtract),
               reads=[stb], writes=[stb], sem="v")
        P.emit("vector", lambda e: e.tensor_single_scalar(
                   st[:, 5:6], st[:, 4:5], 1e-5, ALU.add),
               reads=[stb], writes=[stb], sem="v")
        P.emit("scalar", lambda e: e.activation(st[:, 6:7], st[:, 5:6], AF.Sqrt),
               reads=[stb], writes=[stb], sem="s")
        P.emit("vector", lambda e: e.reciprocal(st[:, 7:8], st[:, 6:7]),
               reads=[stb], writes=[stb], sem="v")
        P.emit("vector", lambda e: e.tensor_tensor(st[:, 3:4], st[:, 7:8],
                                                   st[:, 7:8], ALU.mult),
               reads=[stb], writes=[stb], sem="v")
        P.emit("vector", lambda e: e.scalar_tensor_tensor(
                   st[:, 4:5], st[:, 5:6], 0.5, st[:, 3:4],
                   ALU.mult, ALU.mult),
               reads=[stb], writes=[stb], sem="v")
        P.emit("vector", lambda e: e.tensor_scalar(
                   st[:, 3:4], st[:, 4:5], -1.0, 1.5, ALU.mult, ALU.add),
               reads=[stb], writes=[stb], sem="v")
        P.emit("vector", lambda e: e.tensor_tensor(st[:, 6:7], st[:, 7:8],
                                                   st[:, 3:4], ALU.mult),
               reads=[stb], writes=[stb], sem="v")
        P.emit("vector", lambda e: e.tensor_scalar(
                   tile[:], tile[:], st[:, 2:3], st[:, 6:7],
                   ALU.subtract, ALU.mult),
               reads=[stb, tbuf], writes=[tbuf], sem="v")

    for t in range(NT):
        emit_ln(XB[t], bf(XB[t]), STAT, bf(STAT),
                PSC[:, 0:1024], [pbank(PSC, 0), pbank(PSC, 512)])

    if debug:
        for t in range(NT):
            dma_st(dbg["d_h1"][t * 128:(t + 1) * 128, :], XB[t][:],
                   reads=[bf(XB[t])])

    bf(ZSCR).inherit(bf(WST[0]))
    bf(ZI1).inherit(bf(WST[1]))
    bf(M1).inherit(bf(CBK), bf(OBUF[0]), bf(OBUF[1]))
    for i in range(NE):
        bf(K1T[i]).inherit(bf(QT[i]))
    for t in range(NT):
        bf(H1B[t]).inherit(bf(KT[t // 2]))

    for t in range(NT):
        P.emit("vector", lambda e, t=t: e.reduce_max(
                   STAT[:, 0:1], XB[t][:], axis=mybir.AxisListType.X,
                   apply_absolute_value=True),
               reads=[bf(XB[t])], writes=[bf(STAT)], sem="v")
        P.emit("vector", lambda e: e.tensor_single_scalar(
                   STAT[:, 1:2], STAT[:, 0:1], 1e-5, ALU.max),
               reads=[bf(STAT)], writes=[bf(STAT)], sem="v")
        P.emit("vector", lambda e: e.reciprocal(STAT[:, 2:3], STAT[:, 1:2]),
               reads=[bf(STAT)], writes=[bf(STAT)], sem="v")
        P.emit("vector", lambda e, t=t: e.tensor_single_scalar(
                   S1S[:, t:t + 1], STAT[:, 2:3], 127.0, ALU.mult),
               reads=[bf(STAT)], writes=[bf(S1S)], sem="v")
        P.emit("vector", lambda e, t=t: e.tensor_single_scalar(
                   S1INV[:, t:t + 1], STAT[:, 1:2], 1.0 / 127.0, ALU.mult),
               reads=[bf(STAT)], writes=[bf(S1INV)], sem="v")
        P.emit("scalar", lambda e, t=t: e.activation(
                   ZSCR[:], XB[t][:], AF.Identity, bias=C1285[:],
                   scale=S1S[:, t:t + 1]),
               reads=[bf(XB[t]), bf(S1S), bf(C1285)], writes=[bf(ZSCR)], sem="s")
        P.emit("vector", lambda e: e.tensor_copy(ZI1[:], ZSCR[:]),
               reads=[bf(ZSCR)], writes=[bf(ZI1)], sem="v")
        P.emit("vector", lambda e: e.scalar_tensor_tensor(
                   M1[:], ZSCR[:], 0.0, ZI1[:], ALU.add, ALU.is_lt),
               reads=[bf(ZSCR), bf(ZI1)], writes=[bf(M1)], sem="v")
        P.emit("vector", lambda e: e.scalar_tensor_tensor(
                   ZSCR[:], ZI1[:], -128.0, M1[:], ALU.add, ALU.subtract),
               reads=[bf(ZI1), bf(M1)], writes=[bf(ZSCR)], sem="v")
        for i in range(NE):
            lo = ((t * NE + i) % 2) * 512
            pb = pbank(PSD, lo)
            P.emit("tensor", lambda e, lo=lo, i=i: e.transpose(
                       PSD[:, lo:lo + 128], ZSCR[:, i * 128:(i + 1) * 128], IDF[:]),
                   reads=[bf(ZSCR), bf(IDF)], writes=[pb], sem="t")
            P.emit("vector", lambda e, lo=lo, i=i, t=t: e.tensor_copy(
                       K1T[i][:, t * 128:(t + 1) * 128], PSD[:, lo:lo + 128]),
                   reads=[pb], writes=[bf(K1T[i])], sem="v")
        P.emit("vector", lambda e, t=t: e.tensor_copy(H1B[t][:], XB[t][:]),
               reads=[bf(XB[t])], writes=[bf(H1B[t])], sem="v")

    if debug:
        for i in range(NE):
            dma_st(dbg["d_k1T"][i * 128:(i + 1) * 128, :], K1T[i][:],
                   reads=[bf(K1T[i])])

    for k in range(TSB):
        for t in range(4 * k, 4 * k + 4):
            bf(ARAW[k]).inherit(bf(XB[t]))
    bf(ARAW[0]).inherit(bf(CF), bf(SF))
    bf(ARAW[3]).inherit(bf(XT[0]), bf(XT[1]), bf(XT[2]), bf(XT[3]))
    for k in range(TSB):
        bufs[id(KT2V[k])] = bf(ARAW[k])
    for zg in (ZGF, ZGI):
        bf(zg).inherit(bf(CBT), bf(SBT), bf(PT[0]), bf(PT[1]),
                       bf(PT[2]), bf(PT[3]), bf(SQS), bf(SQS2))
    for j in range(8):
        for t in range(0, 9):
            bf(RING[0][j]).inherit(bf(VTOK[t]))
        for t in range(7, NT):
            bf(RING[1][j]).inherit(bf(VTOK[t]))
        bf(RING[2][j]).inherit(bf(ZSCR), bf(ZI1), bf(M1), bf(WA_), bf(WB_),
                               bf(WST[0]), bf(WST[1]), bf(OBUF[0]), bf(OBUF[1]),
                               bf(CBQ), bf(CBK),
                               *[bf(XT[i]) for i in range(4, NE)])
    for k in range(TSB):
        bf(GB[k]).inherit(bf(VTOK[14]), bf(VTOK[15]), bf(XBF[0]),
                          bf(XBF[1]), bf(VSCR))
    bf(OUTB[0]).inherit(bf(XBF[1]), bf(TOUT[0]), bf(MST[0]))
    bf(OUTB[1]).inherit(bf(TOUT[0]), bf(TOUT[1]), bf(WAV), bf(MST[1]))

    sched = []
    for sb in range(NSB):
        sched.append(("g", sb, 0))
        if sb > 0:
            for q in range(4):
                sched.append(("dA", sb - 1, q))
            for q in range(4):
                sched.append(("dB", sb - 1, q))
        sched.append(("u", sb, 0))
        for q in range(1, 4):
            sched.append(("g", sb, q))
            sched.append(("u", sb, q))
    for q in range(4):
        sched.append(("dA", NSB - 1, q))
    for q in range(4):
        sched.append(("dB", NSB - 1, q))

    def load_chunk(ci):
        kind, sb, q = sched[ci]
        slot = RING[ci % 3]
        hb = hbuf({"g": "t0", "u": "t1", "dA": "t2", "dB": "t2"}[kind])
        for j in range(8):
            if kind in ("dA", "dB"):
                src = td_h[(q * 8 + j) * 128:(q * 8 + j + 1) * 128, :]
            else:
                wh = tg_h if kind == "g" else tu_h
                src = wh[j * 128:(j + 1) * 128, q * 1024:(q + 1) * 1024]
            dma_ld(slot[j][:], src, reads=[hb], writes=[bf(slot[j])])

    def emit_scales(sb, k):
        gti = sb * TSB + k
        P.emit("vector", lambda e, k=k: e.reduce_max(
                   STAT2[:, 0:1], ARAW[k][:], axis=mybir.AxisListType.X,
                   apply_absolute_value=True),
               reads=[bf(ARAW[k])], writes=[bf(STAT2)], sem="v")
        P.emit("vector", lambda e, gti=gti, k=k: e.tensor_tensor(
                   SUT[:, k:k + 1], ALUp[:], S1INV[:, gti:gti + 1], ALU.mult),
               reads=[bf(ALUp), bf(S1INV)], writes=[bf(SUT)], sem="v")
        P.emit("vector", lambda e, k=k: e.tensor_tensor(
                   STAT2[:, 1:2], STAT2[:, 0:1], SUT[:, k:k + 1], ALU.mult),
               reads=[bf(STAT2), bf(SUT)], writes=[bf(STAT2)], sem="v")
        P.emit("vector", lambda e: e.tensor_single_scalar(
                   STAT2[:, 2:3], STAT2[:, 1:2], 1e-5, ALU.max),
               reads=[bf(STAT2)], writes=[bf(STAT2)], sem="v")
        P.emit("vector", lambda e: e.reciprocal(STAT2[:, 3:4], STAT2[:, 2:3]),
               reads=[bf(STAT2)], writes=[bf(STAT2)], sem="v")
        P.emit("vector", lambda e: e.tensor_single_scalar(
                   STAT2[:, 4:5], STAT2[:, 3:4], 127.0, ALU.mult),
               reads=[bf(STAT2)], writes=[bf(STAT2)], sem="v")
        P.emit("vector", lambda e, k=k: e.tensor_tensor(
                   S2E[:, k:k + 1], SUT[:, k:k + 1], STAT2[:, 4:5], ALU.mult),
               reads=[bf(SUT), bf(STAT2)], writes=[bf(S2E)], sem="v")
        P.emit("vector", lambda e: e.tensor_single_scalar(
                   STAT2[:, 5:6], STAT2[:, 2:3], 1.0 / 127.0, ALU.mult),
               reads=[bf(STAT2)], writes=[bf(STAT2)], sem="v")
        P.emit("vector", lambda e, k=k: e.tensor_tensor(
                   SDT[:, k:k + 1], ALD[:], STAT2[:, 5:6], ALU.mult),
               reads=[bf(ALD), bf(STAT2)], writes=[bf(SDT)], sem="v")

    def emit_rounds(sb, k):
        ab = bf(ARAW[k])
        for c in range(4):
            ar = ARAW[k][:, c * 1024:(c + 1) * 1024]
            zf, zi = (ZVF, ZVI) if c % 2 == 0 else (ZGF, ZGI)
            P.emit("vector", lambda e, ar=ar, k=k, zf=zf: e.tensor_scalar(
                       zf[:], ar, S2E[:, k:k + 1], 128.5, ALU.mult, ALU.add),
                   reads=[bf(S2E), ab], writes=[bf(zf)], sem="v")
            P.emit("vector", lambda e, zf=zf, zi=zi: e.tensor_copy(
                       zi[:], zf[:]),
                   reads=[bf(zf)], writes=[bf(zi)], sem="v")
            P.emit("vector", lambda e, zf=zf, zi=zi: e.tensor_tensor(
                       zf[:], zf[:], zi[:], ALU.is_lt),
                   reads=[bf(zf), bf(zi)], writes=[bf(zf)], sem="v")
            P.emit("vector", lambda e, k=k, c=c, zf=zf, zi=zi:
                       e.scalar_tensor_tensor(
                           ARAW[k][:, c * 1024:(c + 1) * 1024], zi[:],
                           -128.0, zf[:], ALU.add, ALU.subtract),
                   reads=[bf(zi), bf(zf)], writes=[ab], sem="v")

    def emit_tr(sb, k):
        ab = bf(ARAW[k])
        for g in range(7, -1, -1):
            ps, lo = ((PSC, 0), (PSC, 512), (PSD, 0), (PSD, 512))[g % 4]
            pb = pbank(ps, lo)
            for w in range(4):
                oc = 4 * g + w
                P.emit("tensor",
                       lambda e, ps=ps, lo=lo, w=w, oc=oc, k=k: e.transpose(
                           ps[:, lo + w * 128:lo + (w + 1) * 128],
                           ARAW[k][:, oc * 128:(oc + 1) * 128], IDF[:]),
                       reads=[ab, bf(IDF)], writes=[pb], sem="t")
            P.emit("scalar", lambda e, ps=ps, lo=lo, g=g, k=k: e.copy(
                       KT2V[k][:, g * 512:(g + 1) * 512], ps[:, lo:lo + 512]),
                   reads=[pb], writes=[ab], sem="s")

    loaded = [False] * len(sched)

    def ensure_load(ci):
        if ci < len(sched) and not loaded[ci]:
            load_chunk(ci)
            loaded[ci] = True

    P.emit("vector", lambda e: e.tensor_scalar(
               SGT[:, 0:NT], S1INV[:, 0:NT], ALG[:], None, ALU.mult),
           reads=[bf(ALG), bf(S1INV)], writes=[bf(SGT)], sem="v")

    pend_tr = []

    for ci, (kind, sb, q) in enumerate(sched):
        for la in range(3):
            ensure_load(ci + la)
        slot = RING[ci % 3]
        if kind in ("g", "u"):
            def emit_gu_mm(k):
                gti = sb * TSB + k
                ps = (PSA, PSB2)[k % 2]
                for i in range(NE):
                    for oc2 in range(2):
                        lo = oc2 * 512
                        P.emit("tensor",
                               lambda e, ps=ps, lo=lo, i=i, gti=gti, slot=slot:
                                   e.matmul(ps[:, lo:lo + 512],
                                            K1T[i][:, gti * 128:(gti + 1) * 128],
                                            slot[i][:, lo:lo + 512],
                                            start=(i == 0), stop=(i == NE - 1)),
                               reads=[bf(K1T[i]), bf(slot[i])],
                               writes=[pbank(ps, lo)], sem="t")

            def emit_gu_epi(k):
                ps = (PSA, PSB2)[k % 2]
                if kind == "g":
                    gti2 = sb * TSB + k
                    P.emit("scalar", lambda e, ps=ps, k=k, gti2=gti2:
                               e.activation(GB[k][:], ps[:], AF.Silu,
                                            scale=SGT[:, gti2:gti2 + 1]),
                           reads=[pbank(ps, 0), pbank(ps, 512), bf(SGT)],
                           writes=[bf(GB[k])], sem="s")
                else:
                    P.emit("vector", lambda e, ps=ps, k=k, q=q: e.tensor_tensor(
                               ARAW[k][:, q * 1024:(q + 1) * 1024], GB[k][:],
                               ps[:], ALU.mult),
                           reads=[bf(GB[k]), pbank(ps, 0), pbank(ps, 512)],
                           writes=[bf(ARAW[k])], sem="v")

            if kind == "u" and q == 3:
                emit_gu_mm(0)
                emit_gu_mm(1)
                emit_gu_epi(0)
                emit_gu_epi(1)
                emit_gu_mm(2)
                emit_gu_mm(3)
                emit_gu_epi(2)
                emit_gu_epi(3)
                bf(ZVF).inherit(bf(OUTB[0]))
                bf(ZVI).inherit(bf(OUTB[1]))
                for k in range(TSB):
                    emit_scales(sb, k)
                    emit_rounds(sb, k)
                    pend_tr.append((sb, k))
            else:
                for k in range(TSB):
                    emit_gu_mm(k)
                    emit_gu_epi(k)
        else:
            if q == 0:
                for _ in range(2):
                    if pend_tr:
                        emit_tr(*pend_tr.pop(0))
            ks = (0, 1) if kind == "dA" else (2, 3)
            for k in ks:
                ps = (PSC, PSD)[k % 2]
                ab = bf(ARAW[k])
                for j in range(8):
                    oc = q * 8 + j
                    for c in range(2):
                        lo = c * 512
                        P.emit("tensor",
                               lambda e, ps=ps, lo=lo, oc=oc, k=k, slot=slot,
                                       j=j, q=q:
                                   e.matmul(ps[:, lo:lo + 512],
                                            KT2V[k][:, oc * 128:(oc + 1) * 128],
                                            slot[j][:, lo:lo + 512],
                                            start=(q == 0 and j == 0),
                                            stop=(q == 3 and j == 7)),
                               reads=[ab, bf(slot[j])],
                               writes=[pbank(ps, lo)], sem="t")
            if q == 3:
                for k in ks:
                    gti = sb * TSB + k
                    ps = (PSC, PSD)[k % 2]
                    ob_ = OUTB[k % 2]
                    if kind == "dA" and k == 0:
                        bf(OUTB[0]).inherit(bf(ZVF))
                        bf(OUTB[1]).inherit(bf(ZVI))
                    for c in range(2):
                        lo = c * 512
                        P.emit("vector",
                               lambda e, ps=ps, lo=lo, ob_=ob_, k=k, gti=gti:
                                   e.scalar_tensor_tensor(
                                       ob_[:, lo:lo + 512], ps[:, lo:lo + 512],
                                       SDT[:, k:k + 1],
                                       H1B[gti][:, lo:lo + 512],
                                       ALU.mult, ALU.add),
                               reads=[pbank(ps, lo), bf(SDT), bf(H1B[gti])],
                               writes=[bf(ob_)], sem="v")
                    emit_ln(ob_, bf(ob_), STAT2, bf(STAT2), ps[:, 0:1024],
                            [pbank(ps, 0), pbank(ps, 512)])
                    P.emit("gpsimd",
                           lambda e, ob_=ob_, gti=gti:
                               e.dma_start(out_h[gti * 128:(gti + 1) * 128, :],
                                           ob_[:]),
                           reads=[bf(ob_)], sem="st", step=16)

    P.q["sync"].append(lambda e, v=P.cnt["st"]: e.wait_ge(P.sem["st"], v))

    with nc.Block() as blk:
        @blk.tensor
        def _(eng):
            for f in P.q["tensor"]:
                f(eng)

        @blk.scalar
        def _(eng):
            for f in P.q["scalar"]:
                f(eng)

        @blk.vector
        def _(eng):
            for f in P.q["vector"]:
                f(eng)

        @blk.gpsimd
        def _(eng):
            for f in P.q["gpsimd"]:
                f(eng)

        @blk.sync
        def _(eng):
            for f in P.q["sync"]:
                f(eng)

    es.close()
    return nc


_CACHE = {}


def _get_nc(debug=False):
    if debug not in _CACHE:
        _CACHE[debug] = build(debug)
    return _CACHE[debug]


def make_inputs(L0_attn_x, L0_attn_W_qkv, L0_attn_b_qkv, L0_ffn_w_gate,
                L0_ffn_w_up, L0_ffn_w_down):
    C, S = _rope_tables()
    perm = np.zeros((128, 128), ml_dtypes.bfloat16)
    for i in range(128):
        perm[i ^ 1, i] = 1.0
    shared = {
        "wqT": np.ascontiguousarray(np.asarray(L0_attn_W_qkv).T),
        "b2": np.ascontiguousarray(np.asarray(L0_attn_b_qkv).reshape(24, 128).T),
        "wgT": np.ascontiguousarray(np.asarray(L0_ffn_w_gate).T),
        "wuT": np.ascontiguousarray(np.asarray(L0_ffn_w_up).T),
        "wdT": np.ascontiguousarray(np.asarray(L0_ffn_w_down).T),
        "Ct": C, "St": S,
        "Pm": perm,
        "Idb": np.eye(128, dtype=ml_dtypes.bfloat16),
        "Idf": np.eye(128, dtype=np.float32),
        "OnesRow": np.ones((1, 128), np.float32),
    }
    x = np.asarray(L0_attn_x)
    return [dict(shared, x=np.ascontiguousarray(x[c])) for c in range(B)]


def kernel(L0_attn_x, L0_attn_W_qkv, L0_attn_b_qkv, L0_ffn_w_gate,
           L0_ffn_w_up, L0_ffn_w_down):
    nc = _get_nc(False)
    in_maps = make_inputs(L0_attn_x, L0_attn_W_qkv, L0_attn_b_qkv,
                          L0_ffn_w_gate, L0_ffn_w_up, L0_ffn_w_down)
    res = run_bass_kernel_spmd(nc, in_maps, core_ids=list(range(B)))
    return np.stack([res.results[c]["out"] for c in range(B)], axis=0)


# version 21 (final)
# speedup vs baseline: 1.0468x; 1.0318x over previous
import numpy as np
import ml_dtypes
from contextlib import ExitStack
from functools import reduce as _reduce

import concourse.bass as bass
import concourse.mybir as mybir
from concourse.bass_utils import run_bass_kernel_spmd

F32 = mybir.dt.float32
BF16 = mybir.dt.bfloat16
I32 = mybir.dt.int32
AF = mybir.ActivationFunctionType
ALU = mybir.AluOpType

B, T, E, H, D, INNER = 8, 2048, 1024, 16, 64, 4096
NT = T // 128
NE = E // 128
NI = INNER // 128
TSB = 4
NSB = NT // TSB
KB = 1024
WQN = float(E * INNER)


def _rope_tables():
    import jax
    with jax.default_device(jax.devices("cpu")[0]):
        import jax.numpy as jnp
        freqs = 1.0 / 10000.0 ** (jnp.arange(0, D, 2, dtype=jnp.float32) / D)
        ang = jnp.arange(T, dtype=jnp.float32)[:, None] * freqs[None, :]
        c = np.asarray(jnp.cos(ang))
        s = np.asarray(jnp.sin(ang))
    C = np.empty((128, T), np.float32)
    S = np.empty((128, T), np.float32)
    for p in range(128):
        d = p % 64
        j = d // 2
        C[p] = c[:, j]
        S[p] = (s[:, j] if (d % 2 == 1) else -s[:, j])
    return C, S


class Buf:
    __slots__ = ("w", "readers", "war_extra")

    def __init__(self):
        self.w = None
        self.readers = {}
        self.war_extra = {}

    def inherit(self, *olds):
        for o in olds:
            if o.w:
                s, v = o.w
                self.war_extra[s] = max(self.war_extra.get(s, 0), v)
            for s, v in o.readers.items():
                self.war_extra[s] = max(self.war_extra.get(s, 0), v)
        return self


class Prog:
    ENGS = ("tensor", "scalar", "vector", "gpsimd", "sync")

    def __init__(self, nc, es):
        self.nc = nc
        self.q = {e: [] for e in self.ENGS}
        self.sem = {}
        self.cnt = {}
        self.wm = {}
        for s in ("t", "s", "v", "g", "ld", "lg", "st"):
            self.sem[s] = es.enter_context(nc.semaphore("sem_" + s))
            self.cnt[s] = 0

    def emit(self, eng, builder, reads=(), writes=(), sem=None, step=1,
             extra=()):
        waits = {}

        def add(dep):
            if dep:
                s, v = dep
                waits[s] = max(waits.get(s, 0), v)

        for b in reads:
            add(b.w)
        for b in writes:
            add(b.w)
            for s, v in b.readers.items():
                add((s, v))
            for s, v in b.war_extra.items():
                add((s, v))
        for d in extra:
            add(d)
        assert sem is not None
        self.cnt[sem] += step
        dep = (sem, self.cnt[sem])
        wlist = []
        for s, v in waits.items():
            if v > self.wm.get((eng, s), 0):
                self.wm[(eng, s)] = v
                wlist.append((self.sem[s], v))
        semh = self.sem[sem]

        def f(e, wlist=wlist, builder=builder, semh=semh, step=step):
            for h, v in wlist:
                e.wait_ge(h, v)
            builder(e).then_inc(semh, step)

        self.q[eng].append(f)
        for b in reads:
            s, v = dep
            b.readers[s] = max(b.readers.get(s, 0), v)
        for b in writes:
            b.w = dep
            b.readers = {}
            b.war_extra = {}
        return dep

    def wait(self, eng, dep):
        s, v = dep
        if v > self.wm.get((eng, s), 0):
            self.wm[(eng, s)] = v
            h = self.sem[s]
            self.q[eng].append(lambda e, h=h, v=v: e.wait_ge(h, v))


class Arena:
    def __init__(self, nc, size_bytes):
        self.nc = nc
        self.size = size_bytes
        self.tensors = []

    def at(self, name, off, shape, dtype):
        per = _reduce(lambda a, b: a * b, shape[1:]) * mybir.dt.size(dtype)
        assert off % 32 == 0, (name, off)
        assert off + per <= self.size, (name, off, per, self.size)
        h = self.nc.virtual_sbuf_tensor(name, shape, dtype)
        self.tensors.append((h, off))
        return h

    def allocate(self, es):
        self.handle = es.enter_context(
            self.nc.sbuf_tensor("arena", [128, self.size], mybir.dt.uint8))
        base = self.nc.lookup_mloc(self.handle).addr
        for h, off in self.tensors:
            self.nc._tensor(h.name, h.shape, h.dtype, type="SB")
            mloc = self.nc.lookup_mloc(h)
            mloc.allocated = not self.nc.target_bir_lowering
            mloc.addr = base + off


def build(debug=False):
    nc = bass.Bass("TRN2", target_bir_lowering=False)

    x_h = nc.dram_tensor("x", [T, E], F32, kind="ExternalInput").ap()
    wq_h = nc.dram_tensor("wqT", [E, 3 * E], F32, kind="ExternalInput").ap()
    b2_h = nc.dram_tensor("b2", [128, 24], F32, kind="ExternalInput").ap()
    wg_h = nc.dram_tensor("wgT", [E, INNER], F32, kind="ExternalInput").ap()
    wu_h = nc.dram_tensor("wuT", [E, INNER], F32, kind="ExternalInput").ap()
    wd_h = nc.dram_tensor("wdT", [INNER, E], F32, kind="ExternalInput").ap()
    ct_h = nc.dram_tensor("Ct", [128, T], F32, kind="ExternalInput").ap()
    st_h = nc.dram_tensor("St", [128, T], F32, kind="ExternalInput").ap()
    pm_h = nc.dram_tensor("Pm", [128, 128], BF16, kind="ExternalInput").ap()
    idb_h = nc.dram_tensor("Idb", [128, 128], BF16, kind="ExternalInput").ap()
    idf_h = nc.dram_tensor("Idf", [128, 128], F32, kind="ExternalInput").ap()
    or_h = nc.dram_tensor("OnesRow", [1, 128], F32, kind="ExternalInput").ap()
    out_h = nc.dram_tensor("out", [T, E], F32, kind="ExternalOutput").ap()
    tk = "ExternalOutput" if debug else "Internal"
    tg_h = nc.dram_tensor("ternG", [E, INNER], BF16, kind=tk).ap()
    tu_h = nc.dram_tensor("ternU", [E, INNER], BF16, kind=tk).ap()
    td_h = nc.dram_tensor("ternD", [INNER, E], BF16, kind=tk).ap()
    dbg = {}
    if debug:
        dbg["d_qT"] = nc.dram_tensor("d_qT", [E, T], BF16, kind="ExternalOutput").ap()
        dbg["d_kT"] = nc.dram_tensor("d_kT", [E, T], BF16, kind="ExternalOutput").ap()
        dbg["d_vtok"] = nc.dram_tensor("d_vtok", [T, 1040], BF16, kind="ExternalOutput").ap()
        dbg["d_h1"] = nc.dram_tensor("d_h1", [T, E], F32, kind="ExternalOutput").ap()
        dbg["d_k1T"] = nc.dram_tensor("d_k1T", [E, T], BF16, kind="ExternalOutput").ap()
        dbg["d_sca"] = nc.dram_tensor("d_sca", [1, 16], F32, kind="ExternalOutput").ap()
        for _m in range(3):
            dbg[f"d_wacc{_m}"] = nc.dram_tensor(f"d_wacc{_m}", [128, 1], F32,
                                                kind="ExternalOutput").ap()
        dbg["d_xatt"] = nc.dram_tensor("d_xatt", [T, E], F32, kind="ExternalOutput").ap()

    ar = Arena(nc, 206 * KB)
    A = ar.at

    _o = [0]
    def sm(name, shape, dtype):
        h = A(name, _o[0], shape, dtype)
        per = _reduce(lambda a, b: a * b, shape[1:]) * mybir.dt.size(dtype)
        _o[0] += (per + 31) & ~31
        return h
    PERM = sm("PERM", [128, 128], BF16)
    IDB = sm("IDB", [128, 128], BF16)
    IDF = sm("IDF", [128, 128], F32)
    ONESC = sm("ONESC", [128, 1], BF16)
    ONESR = sm("ONESR", [1, 128], F32)
    B2 = sm("B2", [128, 24], F32)
    CH2Q = sm("CH2Q", [1, 16], F32)
    CH2K = sm("CH2K", [1, 16], F32)
    CH2S = sm("CH2S", [1, 8], F32)
    CROW = sm("CROW", [1, 16], F32)
    CROWP = sm("CROWP", [1, 8], F32)
    CBIASP = sm("CBIASP", [128, 8], F32)
    S1INV = sm("S1INV", [128, NT], F32)
    S1S = sm("S1S", [128, NT], F32)
    ALG = sm("ALG", [128, 1], F32)
    ALUp = sm("ALUp", [128, 1], F32)
    ALD = sm("ALD", [128, 1], F32)
    RABG = sm("RABG", [128, 1], F32)
    RABU = sm("RABU", [128, 1], F32)
    RABD = sm("RABD", [128, 1], F32)
    WACC = sm("WACC", [128, 1], F32)
    SCA = sm("SCA", [1, 16], F32)
    C1285 = sm("C1285", [128, 1], F32)
    C05 = sm("C05", [128, 1], F32)
    CM05 = sm("CM05", [128, 1], F32)
    SGT = sm("SGT", [128, NT], F32)
    SUT = sm("SUT", [128, TSB], F32)
    S2E = sm("S2E", [128, TSB], F32)
    SDT = sm("SDT", [128, TSB], F32)
    STAT = sm("STAT", [128, 8], F32)
    STAT2 = sm("STAT2", [128, 8], F32)
    GSTAT = sm("GSTAT", [128, 2], F32)
    assert _o[0] <= 6 * KB, _o[0]

    CBT = A("CbT", 6 * KB, [128, T], BF16)
    SBT = A("SbT", 10 * KB, [128, T], BF16)
    PT = [A(f"PT{i}", 6 * KB + i * 2 * KB, [128, 1024], BF16) for i in range(4)]
    OBUF = [A(f"OB{i}", 86 * KB + i * 2 * KB, [65, 512], F32) for i in range(2)]

    CF = A("Cf", 14 * KB, [128, T], F32)
    SF = A("Sf", 22 * KB, [128, T], F32)
    WQ = [A(f"Wq{i}", 14 * KB + i * 6 * KB, [128, 3 * E], BF16) for i in range(NE)]
    XB = [A(f"xb{t}", 14 * KB + t * 4 * KB, [128, E], F32) for t in range(NT)]
    ARAW = [A(f"araw{k}", 14 * KB + k * 16 * KB, [128, INNER], F32)
            for k in range(TSB)]
    KT2V = [A(f"k2tv{k}", 14 * KB + k * 16 * KB + 8 * KB, [128, INNER], BF16)
            for k in range(TSB)]

    XT = [A(f"xT{i}", 62 * KB + i * 4 * KB, [128, T], BF16) for i in range(NE)]
    WST = [A(f"wst{i}", 78 * KB + i * 4 * KB, [128, 1024], F32) for i in range(2)]
    CBQ = A("CBQ", 78 * KB, [1, 2048], F32)
    CBK = A("CBK", 86 * KB, [1, 2048], F32)
    SQS = A("sqs", 6 * KB, [128, 1024], BF16)
    SQS2 = A("sqs2", 8 * KB, [128, 1024], BF16)
    MST = [A(f"mst{i}", 200 * KB + i * 2 * KB, [128, 1024], BF16) for i in range(2)]
    WA_ = A("wa", 90 * KB, [128, 1024], BF16)
    WB_ = A("wb", 92 * KB, [128, 1024], BF16)
    ZSCR = A("zscr", 78 * KB, [128, E], F32)
    ZI1 = A("zi1", 82 * KB, [128, E], I32)
    M1 = A("m1", 86 * KB, [128, E], BF16)

    QT = [A(f"qT{i}", 94 * KB + i * 4 * KB, [128, T], BF16) for i in range(NE)]
    K1T = [A(f"k1T{i}", 94 * KB + i * 4 * KB, [128, T], BF16) for i in range(NE)]
    KT = [A(f"kT{i}", 126 * KB + i * 4 * KB, [128, T], BF16) for i in range(NE)]
    H1B = [A(f"h1b{t}", 126 * KB + t * 2 * KB, [128, E], BF16) for t in range(NT)]
    VTOK = [A(f"vt{t}", 158 * KB + t * 2144, [128, 1040], BF16) for t in range(NT)]
    RING = [[A(f"rg{r}_{j}", (158 * KB + r * 16 * KB if r < 2 else 78 * KB)
               + j * 2 * KB, [128, 1024], BF16) for j in range(8)]
            for r in range(3)]
    XBF = [A(f"xbf{i}", 192 * KB + i * 4 * KB, [128, E], F32) for i in range(2)]
    VSCR = A("vscr", 192 * KB, [128, T], F32)
    GB = [A(f"gb{k}", 190 * KB + k * 2 * KB, [128, 1024], BF16)
          for k in range(TSB)]
    TOUT = [A(f"tout{i}", 200 * KB + i * 2 * KB, [128, 1024], BF16) for i in range(2)]
    WAV = A("wav", 204 * KB, [128, 1024], BF16)
    OUTB = [A(f"outb{o}", 198 * KB + o * 4 * KB, [128, E], F32) for o in range(2)]
    ZVF = A("zvf", 198 * KB, [128, 1024], F32)
    ZVI = A("zvi", 202 * KB, [128, 1024], I32)
    ZGF = A("zgf", 6 * KB, [128, 1024], F32)
    ZGI = A("zgi", 10 * KB, [128, 1024], I32)

    es = ExitStack()
    PSA = es.enter_context(nc.psum_tensor("psA", [128, 1024], F32))
    PSB2 = es.enter_context(nc.psum_tensor("psB", [128, 1024], F32))
    PSC = es.enter_context(nc.psum_tensor("psC", [128, 1024], F32))
    PSD = es.enter_context(nc.psum_tensor("psD", [128, 1024], F32))
    ar.allocate(es)
    P = Prog(nc, es)

    bufs = {}
    def bf(h):
        if id(h) not in bufs:
            bufs[id(h)] = Buf()
        return bufs[id(h)]

    PSBANK = {}
    def pbank(tensor, lo):
        key = (id(tensor), lo // 512)
        if key not in PSBANK:
            PSBANK[key] = Buf()
        return PSBANK[key]

    HB = {}
    def hbuf(name):
        if name not in HB:
            HB[name] = Buf()
        return HB[name]

    def dma_ld(dst_ap, src_ap, reads=(), writes=()):
        P.emit("sync", lambda e, d=dst_ap, s=src_ap: e.dma_start(d, s),
               reads=reads, writes=writes, sem="ld", step=16)

    def dma_st(dst_ap, src_ap, reads=(), writes=()):
        P.emit("sync", lambda e, d=dst_ap, s=src_ap: e.dma_start(d, s),
               reads=reads, writes=writes, sem="st", step=16)

    def gdma(dst_ap, src_ap, reads=(), writes=()):
        P.emit("gpsimd", lambda e, d=dst_ap, s=src_ap: e.dma_start(d, s),
               reads=reads, writes=writes, sem="lg", step=16)

    for dst, src in ((PERM, pm_h), (IDB, idb_h), (IDF, idf_h),
                     (ONESR, or_h), (B2, b2_h), (CF, ct_h), (SF, st_h)):
        dma_ld(dst[:], src[:], writes=[bf(dst)])
    P.emit("vector", lambda e: e.memset(ONESC[:], 1.0), writes=[bf(ONESC)], sem="v")
    P.emit("vector", lambda e: e.memset(C1285[:], 128.5), writes=[bf(C1285)], sem="v")
    P.emit("vector", lambda e: e.memset(C05[:], 0.5), writes=[bf(C05)], sem="v")
    P.emit("vector", lambda e: e.memset(CM05[:], -0.5), writes=[bf(CM05)], sem="v")
    P.emit("vector", lambda e: e.tensor_copy(CBT[:], CF[:]),
           reads=[bf(CF)], writes=[bf(CBT)], sem="v")
    P.emit("vector", lambda e: e.tensor_copy(SBT[:], SF[:]),
           reads=[bf(SF)], writes=[bf(SBT)], sem="v")
    for t in range(NT):
        P.emit("vector", lambda e, t=t: e.memset(VTOK[t][:], 1.0),
               writes=[bf(VTOK[t])], sem="v")

    for t in range(NT):
        xb = XBF[t % 2]
        dma_ld(xb[:], x_h[t * 128:(t + 1) * 128, :], writes=[bf(xb)])
        for i in range(NE):
            pp = (t * NE + i) % 2
            pslice = PSD[:, pp * 512: pp * 512 + 128]
            pb = pbank(PSD, pp * 512)
            P.emit("tensor",
                   lambda e, p=pslice, xb=xb, i=i:
                       e.transpose(p, xb[:, i * 128:(i + 1) * 128], IDF[:]),
                   reads=[bf(xb), bf(IDF)], writes=[pb], sem="t")
            P.emit("vector",
                   lambda e, p=pslice, i=i, t=t:
                       e.tensor_copy(XT[i][:, t * 128:(t + 1) * 128], p),
                   reads=[pb], writes=[bf(XT[i])], sem="v")

    NCH = 32

    def wslice(m, ch):
        wh = (wg_h, wu_h, wd_h)[m]
        if m < 2:
            i, q = ch // 4, ch % 4
            return wh[i * 128:(i + 1) * 128, q * 1024:(q + 1) * 1024]
        return wh[ch * 128:(ch + 1) * 128, :]

    def tslice(m, ch):
        th = (tg_h, tu_h, td_h)[m]
        if m < 2:
            i, q = ch // 4, ch % 4
            return th[i * 128:(i + 1) * 128, q * 1024:(q + 1) * 1024]
        return th[ch * 128:(ch + 1) * 128, :]

    mean_wl = []

    def emit_mean(m, ch):
        w = MST[ch % 2]
        gdma(w[:], wslice(m, ch), reads=[hbuf(f"w{m}")], writes=[bf(w)])
        if ch == 0:
            P.emit("vector", lambda e: e.memset(WACC[:], 0.0),
                   writes=[bf(WACC)], sem="v")
        P.emit("vector",
               lambda e, w=w: e.reduce_sum(GSTAT[:, 0:1], w[:],
                                           axis=mybir.AxisListType.X,
                                           apply_absolute_value=True),
               reads=[bf(w)], writes=[bf(GSTAT)], sem="v")
        P.emit("vector", lambda e: e.tensor_add(WACC[:], WACC[:], GSTAT[:, 0:1]),
               reads=[bf(GSTAT), bf(WACC)], writes=[bf(WACC)], sem="v")
        if ch == NCH - 1:
            pb = pbank(PSD, 0)
            P.emit("tensor", lambda e: e.transpose(PSD[0:1, 0:128], WACC[:], IDF[:]),
                   reads=[bf(WACC), bf(IDF)], writes=[pb], sem="t")
            P.emit("vector", lambda e: e.reduce_sum(SCA[0:1, 0:1], PSD[0:1, 0:128],
                                                    axis=mybir.AxisListType.X),
                   reads=[pb], writes=[bf(SCA)], sem="v")
            P.emit("vector", lambda e, m=m: e.tensor_copy(
                       SCA[0:1, 10 + m:11 + m], SCA[0:1, 0:1]),
                   reads=[bf(SCA)], writes=[bf(SCA)], sem="v")
            if debug:
                dma_st(dbg[f"d_wacc{m}"][:], WACC[:], reads=[bf(WACC)])
            P.emit("vector", lambda e, m=m: e.tensor_scalar(
                       SCA[0:1, 1 + m:2 + m], SCA[0:1, 0:1], 1.0 / WQN, 1e-5,
                       ALU.mult, ALU.max),
                   reads=[bf(SCA)], writes=[bf(SCA)], sem="v")
            P.emit("vector", lambda e, m=m: e.reciprocal(
                       SCA[0:1, 4 + m:5 + m], SCA[0:1, 1 + m:2 + m]),
                   reads=[bf(SCA)], writes=[bf(SCA)], sem="v")
            P.emit("vector", lambda e, m=m: e.tensor_single_scalar(
                       SCA[0:1, 7 + m:8 + m], SCA[0:1, 1 + m:2 + m], 0.5,
                       ALU.mult),
                   reads=[bf(SCA)], writes=[bf(SCA)], sem="v")
            for col, dst in ((7 + m, (ALG, ALUp, ALD)[m]),
                             (4 + m, (RABG, RABU, RABD)[m])):
                pb2 = pbank(PSD, 512)
                P.emit("tensor", lambda e, col=col: e.matmul(
                           PSD[0:128, 512:513], ONESR[:], SCA[0:1, col:col + 1],
                           start=True, stop=True),
                       reads=[bf(SCA), bf(ONESR)], writes=[pb2], sem="t")
                P.emit("vector", lambda e, dst=dst: e.tensor_copy(
                           dst[:], PSD[0:128, 512:513]),
                       reads=[pb2], writes=[bf(dst)], sem="v")

    for m in range(3):
        for ch in range(NCH):
            mean_wl.append((emit_mean, m, ch))

    def emit_quant(m, ch):
        w = WST[0]
        rab = (RABG, RABU, RABD)[m]
        to = TOUT[0]
        if m == 0 and ch == 0:
            bf(WST[0]).inherit(bf(XT[4]), bf(CBQ))
            bf(WST[1]).inherit(bf(XT[5]), bf(CBQ))
            bf(WA_).inherit(bf(XT[7]), bf(CBK))
            bf(WB_).inherit(bf(XT[7]), bf(CBK))
            bf(TOUT[0]).inherit(bf(MST[0]), bf(MST[1]))
            bf(TOUT[1]).inherit(bf(MST[1]))
        dma_ld(w[:], wslice(m, ch), reads=[hbuf(f"w{m}")], writes=[bf(w)])
        P.emit("scalar", lambda e, w=w, rab=rab: e.activation(
                   WA_[:], w[:], AF.Sign, bias=C05[:], scale=rab[:]),
               reads=[bf(w), bf(rab), bf(C05)], writes=[bf(WA_)], sem="s")
        P.emit("scalar", lambda e, w=w, rab=rab: e.activation(
                   WB_[:], w[:], AF.Sign, bias=CM05[:], scale=rab[:]),
               reads=[bf(w), bf(rab), bf(CM05)], writes=[bf(WB_)], sem="s")
        P.emit("vector", lambda e, to=to: e.tensor_tensor(
                   to[:], WA_[:], WB_[:], ALU.add),
               reads=[bf(WA_), bf(WB_)], writes=[bf(to)], sem="v")
        dma_st(tslice(m, ch), to[:], reads=[bf(to)], writes=[hbuf(f"t{m}")])

    def emit_quant_v(m, ch):
        w = WST[ch % 2]
        rab = (RABG, RABU, RABD)[m]
        to = TOUT[ch % 2]
        if m == 0 and ch == 0:
            bf(WST[0]).inherit(bf(XT[4]), bf(CBQ))
            bf(WST[1]).inherit(bf(XT[5]), bf(CBQ))
            bf(WA_).inherit(bf(XT[7]), bf(CBK))
            bf(WB_).inherit(bf(XT[7]), bf(CBK))
            bf(TOUT[0]).inherit(bf(MST[0]), bf(MST[1]))
            bf(TOUT[1]).inherit(bf(MST[1]))
        dma_ld(w[:], wslice(m, ch), reads=[hbuf(f"w{m}")], writes=[bf(w)])
        P.emit("vector", lambda e, w=w, rab=rab: e.tensor_scalar(
                   WAV[:], w[:], rab[:], 0.5, ALU.mult, ALU.is_ge),
               reads=[bf(w), bf(rab)], writes=[bf(WAV)], sem="v")
        P.emit("vector", lambda e, w=w, rab=rab, to=to: e.tensor_scalar(
                   to[:], w[:], rab[:], -0.5, ALU.mult, ALU.is_ge),
               reads=[bf(w), bf(rab)], writes=[bf(to)], sem="v")
        P.emit("vector", lambda e, to=to: e.tensor_tensor(
                   to[:], to[:], WAV[:], ALU.add),
               reads=[bf(to), bf(WAV)], writes=[bf(to)], sem="v")
        P.emit("vector", lambda e, to=to: e.tensor_scalar(
                   to[:], to[:], 2.0, -2.0, ALU.mult, ALU.add),
               reads=[bf(to)], writes=[bf(to)], sem="v")
        dma_st(tslice(m, ch), to[:], reads=[bf(to)], writes=[hbuf(f"t{m}")])

    wq_s = []
    wq_v = [(emit_quant_v, m, ch) for m in (0, 1, 2) for ch in range(NCH)]

    def drain(wl, n):
        for _ in range(min(n, len(wl))):
            fn, m, ch = wl.pop(0)
            fn(m, ch)

    bf(VSCR).inherit(bf(XBF[0]), bf(XBF[1]))
    for i in range(3):
        bf(WQ[i]).inherit(bf(CF), bf(SF))
    for i in range(NE):
        gdma(WQ[i][:], wq_h[i * 128:(i + 1) * 128, :], writes=[bf(WQ[i])])

    def emit_rope(tl):
        tb = bf(tl)
        for h in range(2):
            for lo in (0, 512):
                pb = pbank(PSC, lo)
                P.emit("tensor",
                       lambda e, lo=lo, h=h, tl=tl:
                           e.matmul(PSC[:, lo:lo + 512], PERM[:],
                                    tl[:, h * 1024 + lo:h * 1024 + lo + 512],
                                    start=True, stop=True),
                       reads=[bf(PERM), tb], writes=[pb], sem="t")
                P.emit("vector",
                       lambda e, lo=lo, h=h:
                           e.tensor_tensor(
                               WAV[:, lo:lo + 512], PSC[:, lo:lo + 512],
                               SBT[:, h * 1024 + lo:h * 1024 + lo + 512],
                               ALU.mult),
                       reads=[pb, bf(SBT)], writes=[bf(WAV)], sem="v")
            P.emit("vector",
                   lambda e, h=h, tl=tl:
                       e.tensor_tensor(tl[:, h * 1024:(h + 1) * 1024],
                                       tl[:, h * 1024:(h + 1) * 1024],
                                       CBT[:, h * 1024:(h + 1) * 1024],
                                       ALU.mult),
                   reads=[tb, bf(CBT)], writes=[tb], sem="v")
            P.emit("vector",
                   lambda e, h=h, tl=tl:
                       e.tensor_tensor(tl[:, h * 1024:(h + 1) * 1024],
                                       tl[:, h * 1024:(h + 1) * 1024],
                                       WAV[:], ALU.add),
                   reads=[tb, bf(WAV)], writes=[tb], sem="v")

    QKVPS = [(PSA, 0), (PSA, 512), (PSB2, 0), (PSB2, 512)]
    for ot in range(24):
        for i in range(NE):
            for tc in range(4):
                ps, lo = QKVPS[tc]
                P.emit("tensor",
                       lambda e, ps=ps, lo=lo, i=i, ot=ot, tc=tc:
                           e.matmul(ps[:, lo:lo + 512],
                                    WQ[i][:, ot * 128:(ot + 1) * 128],
                                    XT[i][:, tc * 512:(tc + 1) * 512],
                                    start=(i == 0), stop=(i == NE - 1)),
                       reads=[bf(WQ[i]), bf(XT[i])], writes=[pbank(ps, lo)],
                       sem="t")
        for tc in range(4):
            ps, lo = QKVPS[tc]
            if ot < 8:
                dst, db = QT[ot], bf(QT[ot])
            elif ot < 16:
                dst, db = KT[ot - 8], bf(KT[ot - 8])
            else:
                dst, db = VSCR, bf(VSCR)
            P.emit("scalar",
                   lambda e, ps=ps, lo=lo, dst=dst, ot=ot, tc=tc:
                       e.activation(dst[:, tc * 512:(tc + 1) * 512],
                                    ps[:, lo:lo + 512], AF.Identity,
                                    bias=B2[:, ot:ot + 1], scale=1.0),
                   reads=[pbank(ps, lo), bf(B2)], writes=[db], sem="s")
        if ot >= 16:
            h0 = 2 * (ot - 16)
            for kt in range(NT):
                pp = kt % 2
                pslice = PSD[:, pp * 512: pp * 512 + 128]
                pb = pbank(PSD, pp * 512)
                P.emit("tensor",
                       lambda e, p=pslice, kt=kt:
                           e.transpose(p, VSCR[:, kt * 128:(kt + 1) * 128], IDF[:]),
                       reads=[bf(VSCR), bf(IDF)], writes=[pb], sem="t")
                for hh in range(2):
                    P.emit("vector",
                           lambda e, p=pslice, kt=kt, h=h0 + hh, hh=hh:
                               e.tensor_copy(VTOK[kt][:, h * 65: h * 65 + 64],
                                             p[:, hh * 64:(hh + 1) * 64]),
                           reads=[pb], writes=[bf(VTOK[kt])], sem="v")
        if 1 <= ot <= 8:
            emit_rope(QT[ot - 1])
        elif 9 <= ot <= 16:
            emit_rope(KT[ot - 9])
        drain(mean_wl, 4)

    bf(CBQ).inherit(bf(XT[4]), bf(XT[5]))
    bf(SQS).inherit(bf(CBT))
    bf(SQS2).inherit(bf(CBT))
    CPS = [(PSC, 0), (PSC, 512), (PSD, 0), (PSD, 512)]
    for tiles, chrow in ((QT, CH2Q), (KT, CH2K)):
        for hp in range(NE):
            for hh in range(2):
                sq = (SQS, SQS2)[hh]
                P.emit("scalar",
                       lambda e, tl=tiles[hp], hh=hh, sq=sq:
                           e.activation(sq[:], tl[:, hh * 1024:(hh + 1) * 1024],
                                        AF.Square),
                       reads=[bf(tiles[hp])], writes=[bf(sq)], sem="s")
                for side in range(2):
                    for cc in range(2):
                        ps, lo = CPS[side * 2 + cc]
                        P.emit("tensor",
                               lambda e, ps=ps, lo=lo, cc=cc, side=side, sq=sq:
                                   e.matmul(ps[0:1, lo:lo + 512],
                                            ONESC[side * 64:(side + 1) * 64, 0:1],
                                            sq[side * 64:(side + 1) * 64,
                                                cc * 512:(cc + 1) * 512],
                                            start=True, stop=True),
                               reads=[bf(ONESC), bf(sq)],
                               writes=[pbank(ps, lo)], sem="t")
                for side in range(2):
                    for cc in range(2):
                        ps, lo = CPS[side * 2 + cc]
                        col = side * 4 + hh * 2 + cc
                        P.emit("vector",
                               lambda e, ps=ps, lo=lo, col=col:
                                   e.reduce_max(CH2S[0:1, col:col + 1],
                                                ps[0:1, lo:lo + 512],
                                                axis=mybir.AxisListType.X),
                               reads=[pbank(ps, lo)], writes=[bf(CH2S)],
                               sem="v")
            P.emit("vector", lambda e, hp=hp, chrow=chrow: e.reduce_max(
                       chrow[0:1, 2 * hp:2 * hp + 1], CH2S[0:1, 0:4],
                       axis=mybir.AxisListType.X),
                   reads=[bf(CH2S)], writes=[bf(chrow)], sem="v")
            P.emit("vector", lambda e, hp=hp, chrow=chrow: e.reduce_max(
                       chrow[0:1, 2 * hp + 1:2 * hp + 2], CH2S[0:1, 4:8],
                       axis=mybir.AxisListType.X),
                   reads=[bf(CH2S)], writes=[bf(chrow)], sem="v")
    P.emit("vector", lambda e: e.tensor_tensor(CROW[:], CH2Q[:], CH2K[:],
                                               ALU.mult),
           reads=[bf(CH2Q), bf(CH2K)], writes=[bf(CROW)], sem="v")
    P.emit("scalar", lambda e: e.activation(CROW[:], CROW[:], AF.Sqrt,
                                            scale=1.0 / 64.0),
           reads=[bf(CROW)], writes=[bf(CROW)], sem="s")
    P.emit("vector", lambda e: e.tensor_tensor(CROWP[:], CROW[0:1, 0:16:2],
                                               CROW[0:1, 1:16:2], ALU.max),
           reads=[bf(CROW)], writes=[bf(CROWP)], sem="v")
    P.emit("vector", lambda e: e.tensor_single_scalar(CROWP[:], CROWP[:], -1.0,
                                                      ALU.mult),
           reads=[bf(CROWP)], writes=[bf(CROWP)], sem="v")
    pbX = pbank(PSD, 0)
    P.emit("tensor", lambda e: e.matmul(PSD[0:128, 0:8], ONESR[:], CROWP[:],
                                        start=True, stop=True),
           reads=[bf(ONESR), bf(CROWP)], writes=[pbX], sem="t")
    P.emit("vector", lambda e: e.tensor_copy(CBIASP[:], PSD[0:128, 0:8]),
           reads=[pbX], writes=[bf(CBIASP)], sem="v")

    for t in range(NT):
        bf(XB[t]).inherit(*[bf(WQ[i]) for i in range(NE)],
                          *[bf(XT[i]) for i in range(4)])
        dma_ld(XB[t][:], x_h[t * 128:(t + 1) * 128, :], writes=[bf(XB[t])])
    for pt in PT:
        bf(pt).inherit(bf(CBT), bf(SBT), bf(SQS), bf(SQS2))
    for ob in OBUF:
        bf(ob).inherit(bf(XT[6]), bf(CBK))

    for hp in range(NE):
        h0, h1 = 2 * hp, 2 * hp + 1
        for qc in range(4):
            def emit_S(kt):
                ps = PSA if kt % 2 == 0 else PSB2
                for side in range(2):
                    lo = side * 512
                    P.emit("tensor",
                           lambda e, ps=ps, lo=lo, side=side, kt=kt, qc=qc, hp=hp:
                               e.matmul(ps[:, lo:lo + 512],
                                        KT[hp][side * 64:(side + 1) * 64,
                                               kt * 128:(kt + 1) * 128],
                                        QT[hp][side * 64:(side + 1) * 64,
                                               qc * 512:(qc + 1) * 512],
                                        start=True, stop=True,
                                        tile_position=(side * 64, 0)),
                           reads=[bf(KT[hp]), bf(QT[hp])],
                           writes=[pbank(ps, lo)], sem="t")

            def emit_exp(kt):
                ps = PSA if kt % 2 == 0 else PSB2
                pt = PT[kt % 4]
                P.emit("scalar",
                       lambda e, ps=ps, pt=pt, hp=hp:
                           e.activation(pt[:], ps[:], AF.Exp,
                                        bias=CBIASP[:, hp:hp + 1], scale=0.125),
                       reads=[pbank(ps, 0), pbank(ps, 512), bf(CBIASP)],
                       writes=[bf(pt)], sem="s")

            def emit_PV(kt):
                pt = PT[kt % 4]
                for side, h in ((0, h0), (1, h1)):
                    lo = side * 512
                    P.emit("tensor",
                           lambda e, pt=pt, lo=lo, h=h, kt=kt:
                               e.matmul(PSC[0:65, lo:lo + 512],
                                        VTOK[kt][:, h * 65: h * 65 + 65],
                                        pt[:, lo:lo + 512],
                                        start=(kt == 0), stop=(kt == NT - 1)),
                           reads=[bf(VTOK[kt]), bf(pt)],
                           writes=[pbank(PSC, lo)], sem="t")

            emit_S(0)
            emit_exp(0)
            for kt in range(1, NT):
                emit_S(kt)
                emit_PV(kt - 1)
                emit_exp(kt)
            emit_PV(NT - 1)

            for side, h in ((0, h0), (1, h1)):
                lo = side * 512
                ob = OBUF[side]
                P.emit("scalar",
                       lambda e, ob=ob, lo=lo:
                           e.copy(ob[:], PSC[0:65, lo:lo + 512]),
                       reads=[pbank(PSC, lo)], writes=[bf(ob)], sem="s")
                for j in range(4):
                    lo2 = (j % 2) * 512
                    pb = pbank(PSD, lo2)
                    P.emit("tensor",
                           lambda e, ob=ob, j=j, lo2=lo2:
                               e.transpose(PSD[:, lo2:lo2 + 65],
                                           ob[0:65, j * 128:(j + 1) * 128],
                                           IDF[0:65, 0:65]),
                           reads=[bf(ob), bf(IDF)], writes=[pb], sem="t")
                    P.emit("vector",
                           lambda e, lo2=lo2:
                               e.reciprocal(STAT[:, 0:1], PSD[:, lo2 + 64:lo2 + 65]),
                           reads=[pb], writes=[bf(STAT)], sem="v")
                    tb = XB[qc * 4 + j]
                    P.emit("vector",
                           lambda e, lo2=lo2, tb=tb, h=h:
                               e.scalar_tensor_tensor(
                                   tb[:, h * 64:(h + 1) * 64],
                                   PSD[:, lo2:lo2 + 64], STAT[:, 0:1],
                                   tb[:, h * 64:(h + 1) * 64],
                                   ALU.mult, ALU.add),
                           reads=[pb, bf(STAT), bf(tb)], writes=[bf(tb)],
                           sem="v")
            drain(wq_v, 3)
    drain(wq_s, len(wq_s))
    drain(wq_v, len(wq_v))

    if debug:
        for i in range(NE):
            dma_st(dbg["d_qT"][i * 128:(i + 1) * 128, :], QT[i][:], reads=[bf(QT[i])])
            dma_st(dbg["d_kT"][i * 128:(i + 1) * 128, :], KT[i][:], reads=[bf(KT[i])])
        for t in range(NT):
            dma_st(dbg["d_vtok"][t * 128:(t + 1) * 128, :], VTOK[t][:],
                   reads=[bf(VTOK[t])])
            dma_st(dbg["d_xatt"][t * 128:(t + 1) * 128, :], XB[t][:],
                   reads=[bf(XB[t])])
        dma_st(dbg["d_sca"][:], SCA[:], reads=[bf(SCA)])

    def emit_ln(tile, tbuf, st, stb, sq, sqb):
        P.emit("vector", lambda e: e.reduce_sum(st[:, 0:1], tile[:],
                                                axis=mybir.AxisListType.X),
               reads=[tbuf], writes=[stb], sem="v")
        P.emit("scalar", lambda e: e.activation(sq, tile[:], AF.Square,
                                                accum_out=st[:, 1:2]),
               reads=[tbuf], writes=[stb] + list(sqb), sem="s")
        P.emit("vector", lambda e: e.tensor_single_scalar(
                   st[:, 2:3], st[:, 0:1], 1.0 / float(E), ALU.mult),
               reads=[stb], writes=[stb], sem="v")
        P.emit("vector", lambda e: e.tensor_tensor(st[:, 3:4], st[:, 2:3],
                                                   st[:, 2:3], ALU.mult),
               reads=[stb], writes=[stb], sem="v")
        P.emit("vector", lambda e: e.scalar_tensor_tensor(
                   st[:, 4:5], st[:, 1:2], 1.0 / float(E), st[:, 3:4],
                   ALU.mult, ALU.subtract),
               reads=[stb], writes=[stb], sem="v")
        P.emit("vector", lambda e: e.tensor_single_scalar(
                   st[:, 5:6], st[:, 4:5], 1e-5, ALU.add),
               reads=[stb], writes=[stb], sem="v")
        P.emit("scalar", lambda e: e.activation(st[:, 6:7], st[:, 5:6], AF.Sqrt),
               reads=[stb], writes=[stb], sem="s")
        P.emit("vector", lambda e: e.reciprocal(st[:, 7:8], st[:, 6:7]),
               reads=[stb], writes=[stb], sem="v")
        P.emit("vector", lambda e: e.tensor_tensor(st[:, 3:4], st[:, 7:8],
                                                   st[:, 7:8], ALU.mult),
               reads=[stb], writes=[stb], sem="v")
        P.emit("vector", lambda e: e.scalar_tensor_tensor(
                   st[:, 4:5], st[:, 5:6], 0.5, st[:, 3:4],
                   ALU.mult, ALU.mult),
               reads=[stb], writes=[stb], sem="v")
        P.emit("vector", lambda e: e.tensor_scalar(
                   st[:, 3:4], st[:, 4:5], -1.0, 1.5, ALU.mult, ALU.add),
               reads=[stb], writes=[stb], sem="v")
        P.emit("vector", lambda e: e.tensor_tensor(st[:, 6:7], st[:, 7:8],
                                                   st[:, 3:4], ALU.mult),
               reads=[stb], writes=[stb], sem="v")
        P.emit("vector", lambda e: e.tensor_scalar(
                   tile[:], tile[:], st[:, 2:3], st[:, 6:7],
                   ALU.subtract, ALU.mult),
               reads=[stb, tbuf], writes=[tbuf], sem="v")

    for t in range(NT):
        emit_ln(XB[t], bf(XB[t]), STAT, bf(STAT),
                PSC[:, 0:1024], [pbank(PSC, 0), pbank(PSC, 512)])

    if debug:
        for t in range(NT):
            dma_st(dbg["d_h1"][t * 128:(t + 1) * 128, :], XB[t][:],
                   reads=[bf(XB[t])])

    bf(ZSCR).inherit(bf(WST[0]))
    bf(ZI1).inherit(bf(WST[1]))
    bf(M1).inherit(bf(CBK), bf(OBUF[0]), bf(OBUF[1]))
    for i in range(NE):
        bf(K1T[i]).inherit(bf(QT[i]))
    for t in range(NT):
        bf(H1B[t]).inherit(bf(KT[t // 2]))

    for t in range(NT):
        P.emit("vector", lambda e, t=t: e.reduce_max(
                   STAT[:, 0:1], XB[t][:], axis=mybir.AxisListType.X,
                   apply_absolute_value=True),
               reads=[bf(XB[t])], writes=[bf(STAT)], sem="v")
        P.emit("vector", lambda e: e.tensor_single_scalar(
                   STAT[:, 1:2], STAT[:, 0:1], 1e-5, ALU.max),
               reads=[bf(STAT)], writes=[bf(STAT)], sem="v")
        P.emit("vector", lambda e: e.reciprocal(STAT[:, 2:3], STAT[:, 1:2]),
               reads=[bf(STAT)], writes=[bf(STAT)], sem="v")
        P.emit("vector", lambda e, t=t: e.tensor_single_scalar(
                   S1S[:, t:t + 1], STAT[:, 2:3], 127.0, ALU.mult),
               reads=[bf(STAT)], writes=[bf(S1S)], sem="v")
        P.emit("vector", lambda e, t=t: e.tensor_single_scalar(
                   S1INV[:, t:t + 1], STAT[:, 1:2], 1.0 / 127.0, ALU.mult),
               reads=[bf(STAT)], writes=[bf(S1INV)], sem="v")
        P.emit("scalar", lambda e, t=t: e.activation(
                   ZSCR[:], XB[t][:], AF.Identity, bias=C1285[:],
                   scale=S1S[:, t:t + 1]),
               reads=[bf(XB[t]), bf(S1S), bf(C1285)], writes=[bf(ZSCR)], sem="s")
        P.emit("vector", lambda e: e.tensor_copy(ZI1[:], ZSCR[:]),
               reads=[bf(ZSCR)], writes=[bf(ZI1)], sem="v")
        P.emit("vector", lambda e: e.scalar_tensor_tensor(
                   M1[:], ZSCR[:], 0.0, ZI1[:], ALU.add, ALU.is_lt),
               reads=[bf(ZSCR), bf(ZI1)], writes=[bf(M1)], sem="v")
        P.emit("vector", lambda e: e.scalar_tensor_tensor(
                   ZSCR[:], ZI1[:], -128.0, M1[:], ALU.add, ALU.subtract),
               reads=[bf(ZI1), bf(M1)], writes=[bf(ZSCR)], sem="v")
        for i in range(NE):
            lo = ((t * NE + i) % 2) * 512
            pb = pbank(PSD, lo)
            P.emit("tensor", lambda e, lo=lo, i=i: e.transpose(
                       PSD[:, lo:lo + 128], ZSCR[:, i * 128:(i + 1) * 128], IDF[:]),
                   reads=[bf(ZSCR), bf(IDF)], writes=[pb], sem="t")
            P.emit("vector", lambda e, lo=lo, i=i, t=t: e.tensor_copy(
                       K1T[i][:, t * 128:(t + 1) * 128], PSD[:, lo:lo + 128]),
                   reads=[pb], writes=[bf(K1T[i])], sem="v")
        P.emit("vector", lambda e, t=t: e.tensor_copy(H1B[t][:], XB[t][:]),
               reads=[bf(XB[t])], writes=[bf(H1B[t])], sem="v")

    if debug:
        for i in range(NE):
            dma_st(dbg["d_k1T"][i * 128:(i + 1) * 128, :], K1T[i][:],
                   reads=[bf(K1T[i])])

    for k in range(TSB):
        for t in range(4 * k, 4 * k + 4):
            bf(ARAW[k]).inherit(bf(XB[t]))
    bf(ARAW[0]).inherit(bf(CF), bf(SF))
    bf(ARAW[3]).inherit(bf(XT[0]), bf(XT[1]), bf(XT[2]), bf(XT[3]))
    for k in range(TSB):
        bufs[id(KT2V[k])] = bf(ARAW[k])
    for zg in (ZGF, ZGI):
        bf(zg).inherit(bf(CBT), bf(SBT), bf(PT[0]), bf(PT[1]),
                       bf(PT[2]), bf(PT[3]), bf(SQS), bf(SQS2))
    for j in range(8):
        for t in range(0, 9):
            bf(RING[0][j]).inherit(bf(VTOK[t]))
        for t in range(7, NT):
            bf(RING[1][j]).inherit(bf(VTOK[t]))
        bf(RING[2][j]).inherit(bf(ZSCR), bf(ZI1), bf(M1), bf(WA_), bf(WB_),
                               bf(WST[0]), bf(WST[1]), bf(OBUF[0]), bf(OBUF[1]),
                               bf(CBQ), bf(CBK),
                               *[bf(XT[i]) for i in range(4, NE)])
    for k in range(TSB):
        bf(GB[k]).inherit(bf(VTOK[14]), bf(VTOK[15]), bf(XBF[0]),
                          bf(XBF[1]), bf(VSCR))
    bf(OUTB[0]).inherit(bf(XBF[1]), bf(TOUT[0]), bf(MST[0]))
    bf(OUTB[1]).inherit(bf(TOUT[0]), bf(TOUT[1]), bf(WAV), bf(MST[1]))

    sched = []
    for sb in range(NSB):
        sched.append(("g", sb, 0))
        if sb > 0:
            for q in range(4):
                sched.append(("dA", sb - 1, q))
            for q in range(4):
                sched.append(("dB", sb - 1, q))
        sched.append(("u", sb, 0))
        for q in range(1, 4):
            sched.append(("g", sb, q))
            sched.append(("u", sb, q))
    for q in range(4):
        sched.append(("dA", NSB - 1, q))
    for q in range(4):
        sched.append(("dB", NSB - 1, q))

    def load_chunk(ci):
        kind, sb, q = sched[ci]
        slot = RING[ci % 3]
        hb = hbuf({"g": "t0", "u": "t1", "dA": "t2", "dB": "t2"}[kind])
        for j in range(8):
            if kind in ("dA", "dB"):
                src = td_h[(q * 8 + j) * 128:(q * 8 + j + 1) * 128, :]
            else:
                wh = tg_h if kind == "g" else tu_h
                src = wh[j * 128:(j + 1) * 128, q * 1024:(q + 1) * 1024]
            dma_ld(slot[j][:], src, reads=[hb], writes=[bf(slot[j])])

    def emit_scales(sb, k):
        gti = sb * TSB + k
        P.emit("vector", lambda e, k=k: e.reduce_max(
                   STAT2[:, 0:1], ARAW[k][:], axis=mybir.AxisListType.X,
                   apply_absolute_value=True),
               reads=[bf(ARAW[k])], writes=[bf(STAT2)], sem="v")
        P.emit("vector", lambda e, gti=gti, k=k: e.tensor_tensor(
                   SUT[:, k:k + 1], ALUp[:], S1INV[:, gti:gti + 1], ALU.mult),
               reads=[bf(ALUp), bf(S1INV)], writes=[bf(SUT)], sem="v")
        P.emit("vector", lambda e, k=k: e.tensor_tensor(
                   STAT2[:, 1:2], STAT2[:, 0:1], SUT[:, k:k + 1], ALU.mult),
               reads=[bf(STAT2), bf(SUT)], writes=[bf(STAT2)], sem="v")
        P.emit("vector", lambda e: e.tensor_single_scalar(
                   STAT2[:, 2:3], STAT2[:, 1:2], 1e-5, ALU.max),
               reads=[bf(STAT2)], writes=[bf(STAT2)], sem="v")
        P.emit("vector", lambda e: e.reciprocal(STAT2[:, 3:4], STAT2[:, 2:3]),
               reads=[bf(STAT2)], writes=[bf(STAT2)], sem="v")
        P.emit("vector", lambda e: e.tensor_single_scalar(
                   STAT2[:, 4:5], STAT2[:, 3:4], 127.0, ALU.mult),
               reads=[bf(STAT2)], writes=[bf(STAT2)], sem="v")
        P.emit("vector", lambda e, k=k: e.tensor_tensor(
                   S2E[:, k:k + 1], SUT[:, k:k + 1], STAT2[:, 4:5], ALU.mult),
               reads=[bf(SUT), bf(STAT2)], writes=[bf(S2E)], sem="v")
        P.emit("vector", lambda e: e.tensor_single_scalar(
                   STAT2[:, 5:6], STAT2[:, 2:3], 1.0 / 127.0, ALU.mult),
               reads=[bf(STAT2)], writes=[bf(STAT2)], sem="v")
        P.emit("vector", lambda e, k=k: e.tensor_tensor(
                   SDT[:, k:k + 1], ALD[:], STAT2[:, 5:6], ALU.mult),
               reads=[bf(ALD), bf(STAT2)], writes=[bf(SDT)], sem="v")

    def emit_rounds(sb, k):
        ab = bf(ARAW[k])
        for c in range(4):
            ar = ARAW[k][:, c * 1024:(c + 1) * 1024]
            zf, zi = (ZVF, ZVI) if c % 2 == 0 else (ZGF, ZGI)
            P.emit("vector", lambda e, ar=ar, k=k, zf=zf: e.tensor_scalar(
                       zf[:], ar, S2E[:, k:k + 1], 128.5, ALU.mult, ALU.add),
                   reads=[bf(S2E), ab], writes=[bf(zf)], sem="v")
            P.emit("vector", lambda e, zf=zf, zi=zi: e.tensor_copy(
                       zi[:], zf[:]),
                   reads=[bf(zf)], writes=[bf(zi)], sem="v")
            P.emit("vector", lambda e, zf=zf, zi=zi: e.tensor_tensor(
                       zf[:], zf[:], zi[:], ALU.is_lt),
                   reads=[bf(zf), bf(zi)], writes=[bf(zf)], sem="v")
            P.emit("vector", lambda e, k=k, c=c, zf=zf, zi=zi:
                       e.scalar_tensor_tensor(
                           ARAW[k][:, c * 1024:(c + 1) * 1024], zi[:],
                           -128.0, zf[:], ALU.add, ALU.subtract),
                   reads=[bf(zi), bf(zf)], writes=[ab], sem="v")

    def emit_tr(sb, k):
        ps = (PSC, PSD)[k % 2]
        ab = bf(ARAW[k])
        for g in range(7, -1, -1):
            lo = (g % 2) * 512
            pb = pbank(ps, lo)
            for w in range(4):
                oc = 4 * g + w
                P.emit("tensor",
                       lambda e, ps=ps, lo=lo, w=w, oc=oc, k=k: e.transpose(
                           ps[:, lo + w * 128:lo + (w + 1) * 128],
                           ARAW[k][:, oc * 128:(oc + 1) * 128], IDF[:]),
                       reads=[ab, bf(IDF)], writes=[pb], sem="t")
            P.emit("scalar", lambda e, ps=ps, lo=lo, g=g, k=k: e.copy(
                       KT2V[k][:, g * 512:(g + 1) * 512], ps[:, lo:lo + 512]),
                   reads=[pb], writes=[ab], sem="s")

    loaded = [False] * len(sched)

    def ensure_load(ci):
        if ci < len(sched) and not loaded[ci]:
            load_chunk(ci)
            loaded[ci] = True

    P.emit("vector", lambda e: e.tensor_scalar(
               SGT[:, 0:NT], S1INV[:, 0:NT], ALG[:], None, ALU.mult),
           reads=[bf(ALG), bf(S1INV)], writes=[bf(SGT)], sem="v")

    pend_tr = []

    for ci, (kind, sb, q) in enumerate(sched):
        for la in range(3):
            ensure_load(ci + la)
        slot = RING[ci % 3]
        if kind in ("g", "u"):
            def emit_gu_mm(k):
                gti = sb * TSB + k
                ps = (PSA, PSB2)[k % 2]
                for i in range(NE):
                    for oc2 in range(2):
                        lo = oc2 * 512
                        P.emit("tensor",
                               lambda e, ps=ps, lo=lo, i=i, gti=gti, slot=slot:
                                   e.matmul(ps[:, lo:lo + 512],
                                            K1T[i][:, gti * 128:(gti + 1) * 128],
                                            slot[i][:, lo:lo + 512],
                                            start=(i == 0), stop=(i == NE - 1)),
                               reads=[bf(K1T[i]), bf(slot[i])],
                               writes=[pbank(ps, lo)], sem="t")

            def emit_gu_epi(k):
                ps = (PSA, PSB2)[k % 2]
                if kind == "g":
                    gti2 = sb * TSB + k
                    P.emit("scalar", lambda e, ps=ps, k=k, gti2=gti2:
                               e.activation(GB[k][:], ps[:], AF.Silu,
                                            scale=SGT[:, gti2:gti2 + 1]),
                           reads=[pbank(ps, 0), pbank(ps, 512), bf(SGT)],
                           writes=[bf(GB[k])], sem="s")
                else:
                    P.emit("vector", lambda e, ps=ps, k=k, q=q: e.tensor_tensor(
                               ARAW[k][:, q * 1024:(q + 1) * 1024], GB[k][:],
                               ps[:], ALU.mult),
                           reads=[bf(GB[k]), pbank(ps, 0), pbank(ps, 512)],
                           writes=[bf(ARAW[k])], sem="v")

            if kind == "u" and q == 3:
                emit_gu_mm(0)
                emit_gu_mm(1)
                emit_gu_epi(0)
                emit_gu_epi(1)
                emit_gu_mm(2)
                emit_gu_mm(3)
                emit_gu_epi(2)
                emit_gu_epi(3)
                bf(ZVF).inherit(bf(OUTB[0]))
                bf(ZVI).inherit(bf(OUTB[1]))
                for k in range(TSB):
                    emit_scales(sb, k)
                    emit_rounds(sb, k)
                    pend_tr.append((sb, k))
            else:
                for k in range(TSB):
                    emit_gu_mm(k)
                    emit_gu_epi(k)
        else:
            if q == 0:
                for _ in range(2):
                    if pend_tr:
                        emit_tr(*pend_tr.pop(0))
            ks = (0, 1) if kind == "dA" else (2, 3)
            for k in ks:
                ps = (PSC, PSD)[k % 2]
                ab = bf(ARAW[k])
                for j in range(8):
                    oc = q * 8 + j
                    for c in range(2):
                        lo = c * 512
                        P.emit("tensor",
                               lambda e, ps=ps, lo=lo, oc=oc, k=k, slot=slot,
                                       j=j, q=q:
                                   e.matmul(ps[:, lo:lo + 512],
                                            KT2V[k][:, oc * 128:(oc + 1) * 128],
                                            slot[j][:, lo:lo + 512],
                                            start=(q == 0 and j == 0),
                                            stop=(q == 3 and j == 7)),
                               reads=[ab, bf(slot[j])],
                               writes=[pbank(ps, lo)], sem="t")
            if q == 3:
                for k in ks:
                    gti = sb * TSB + k
                    ps = (PSC, PSD)[k % 2]
                    ob_ = OUTB[k % 2]
                    if kind == "dA" and k == 0:
                        bf(OUTB[0]).inherit(bf(ZVF))
                        bf(OUTB[1]).inherit(bf(ZVI))
                    for c in range(2):
                        lo = c * 512
                        P.emit("vector",
                               lambda e, ps=ps, lo=lo, ob_=ob_, k=k, gti=gti:
                                   e.scalar_tensor_tensor(
                                       ob_[:, lo:lo + 512], ps[:, lo:lo + 512],
                                       SDT[:, k:k + 1],
                                       H1B[gti][:, lo:lo + 512],
                                       ALU.mult, ALU.add),
                               reads=[pbank(ps, lo), bf(SDT), bf(H1B[gti])],
                               writes=[bf(ob_)], sem="v")
                    emit_ln(ob_, bf(ob_), STAT2, bf(STAT2), ps[:, 0:1024],
                            [pbank(ps, 0), pbank(ps, 512)])
                    P.emit("gpsimd",
                           lambda e, ob_=ob_, gti=gti:
                               e.dma_start(out_h[gti * 128:(gti + 1) * 128, :],
                                           ob_[:]),
                           reads=[bf(ob_)], sem="st", step=16)

    P.q["sync"].append(lambda e, v=P.cnt["st"]: e.wait_ge(P.sem["st"], v))

    with nc.Block() as blk:
        @blk.tensor
        def _(eng):
            for f in P.q["tensor"]:
                f(eng)

        @blk.scalar
        def _(eng):
            for f in P.q["scalar"]:
                f(eng)

        @blk.vector
        def _(eng):
            for f in P.q["vector"]:
                f(eng)

        @blk.gpsimd
        def _(eng):
            for f in P.q["gpsimd"]:
                f(eng)

        @blk.sync
        def _(eng):
            for f in P.q["sync"]:
                f(eng)

    es.close()
    return nc


_CACHE = {}


def _get_nc(debug=False):
    if debug not in _CACHE:
        _CACHE[debug] = build(debug)
    return _CACHE[debug]


def make_inputs(L0_attn_x, L0_attn_W_qkv, L0_attn_b_qkv, L0_ffn_w_gate,
                L0_ffn_w_up, L0_ffn_w_down):
    C, S = _rope_tables()
    perm = np.zeros((128, 128), ml_dtypes.bfloat16)
    for i in range(128):
        perm[i ^ 1, i] = 1.0
    shared = {
        "wqT": np.ascontiguousarray(np.asarray(L0_attn_W_qkv).T),
        "b2": np.ascontiguousarray(np.asarray(L0_attn_b_qkv).reshape(24, 128).T),
        "wgT": np.ascontiguousarray(np.asarray(L0_ffn_w_gate).T),
        "wuT": np.ascontiguousarray(np.asarray(L0_ffn_w_up).T),
        "wdT": np.ascontiguousarray(np.asarray(L0_ffn_w_down).T),
        "Ct": C, "St": S,
        "Pm": perm,
        "Idb": np.eye(128, dtype=ml_dtypes.bfloat16),
        "Idf": np.eye(128, dtype=np.float32),
        "OnesRow": np.ones((1, 128), np.float32),
    }
    x = np.asarray(L0_attn_x)
    return [dict(shared, x=np.ascontiguousarray(x[c])) for c in range(B)]


def kernel(L0_attn_x, L0_attn_W_qkv, L0_attn_b_qkv, L0_ffn_w_gate,
           L0_ffn_w_up, L0_ffn_w_down):
    nc = _get_nc(False)
    in_maps = make_inputs(L0_attn_x, L0_attn_W_qkv, L0_attn_b_qkv,
                          L0_ffn_w_gate, L0_ffn_w_up, L0_ffn_w_down)
    res = run_bass_kernel_spmd(nc, in_maps, core_ids=list(range(B)))
    return np.stack([res.results[c]["out"] for c in range(B)], axis=0)


# version 22
# speedup vs baseline: 1.0491x; 1.0022x over previous
import numpy as np
import ml_dtypes
from contextlib import ExitStack
from functools import reduce as _reduce

import concourse.bass as bass
import concourse.mybir as mybir
from concourse.bass_utils import run_bass_kernel_spmd

F32 = mybir.dt.float32
BF16 = mybir.dt.bfloat16
I32 = mybir.dt.int32
AF = mybir.ActivationFunctionType
ALU = mybir.AluOpType

B, T, E, H, D, INNER = 8, 2048, 1024, 16, 64, 4096
NT = T // 128
NE = E // 128
NI = INNER // 128
TSB = 4
NSB = NT // TSB
KB = 1024
WQN = float(E * INNER)


def _rope_tables():
    import jax
    with jax.default_device(jax.devices("cpu")[0]):
        import jax.numpy as jnp
        freqs = 1.0 / 10000.0 ** (jnp.arange(0, D, 2, dtype=jnp.float32) / D)
        ang = jnp.arange(T, dtype=jnp.float32)[:, None] * freqs[None, :]
        c = np.asarray(jnp.cos(ang))
        s = np.asarray(jnp.sin(ang))
    C = np.empty((128, T), np.float32)
    S = np.empty((128, T), np.float32)
    for p in range(128):
        d = p % 64
        j = d // 2
        C[p] = c[:, j]
        S[p] = (s[:, j] if (d % 2 == 1) else -s[:, j])
    return C, S


class Buf:
    __slots__ = ("w", "readers", "war_extra")

    def __init__(self):
        self.w = None
        self.readers = {}
        self.war_extra = {}

    def inherit(self, *olds):
        for o in olds:
            if o.w:
                s, v = o.w
                self.war_extra[s] = max(self.war_extra.get(s, 0), v)
            for s, v in o.readers.items():
                self.war_extra[s] = max(self.war_extra.get(s, 0), v)
        return self


class Prog:
    ENGS = ("tensor", "scalar", "vector", "gpsimd", "sync")

    def __init__(self, nc, es):
        self.nc = nc
        self.q = {e: [] for e in self.ENGS}
        self.sem = {}
        self.cnt = {}
        self.wm = {}
        for s in ("t", "s", "v", "g", "ld", "lg", "st"):
            self.sem[s] = es.enter_context(nc.semaphore("sem_" + s))
            self.cnt[s] = 0

    def emit(self, eng, builder, reads=(), writes=(), sem=None, step=1,
             extra=()):
        waits = {}

        def add(dep):
            if dep:
                s, v = dep
                waits[s] = max(waits.get(s, 0), v)

        for b in reads:
            add(b.w)
        for b in writes:
            add(b.w)
            for s, v in b.readers.items():
                add((s, v))
            for s, v in b.war_extra.items():
                add((s, v))
        for d in extra:
            add(d)
        assert sem is not None
        self.cnt[sem] += step
        dep = (sem, self.cnt[sem])
        wlist = []
        for s, v in waits.items():
            if v > self.wm.get((eng, s), 0):
                self.wm[(eng, s)] = v
                wlist.append((self.sem[s], v))
        semh = self.sem[sem]

        def f(e, wlist=wlist, builder=builder, semh=semh, step=step):
            for h, v in wlist:
                e.wait_ge(h, v)
            builder(e).then_inc(semh, step)

        self.q[eng].append(f)
        for b in reads:
            s, v = dep
            b.readers[s] = max(b.readers.get(s, 0), v)
        for b in writes:
            b.w = dep
            b.readers = {}
            b.war_extra = {}
        return dep

    def wait(self, eng, dep):
        s, v = dep
        if v > self.wm.get((eng, s), 0):
            self.wm[(eng, s)] = v
            h = self.sem[s]
            self.q[eng].append(lambda e, h=h, v=v: e.wait_ge(h, v))


class Arena:
    def __init__(self, nc, size_bytes):
        self.nc = nc
        self.size = size_bytes
        self.tensors = []

    def at(self, name, off, shape, dtype):
        per = _reduce(lambda a, b: a * b, shape[1:]) * mybir.dt.size(dtype)
        assert off % 32 == 0, (name, off)
        assert off + per <= self.size, (name, off, per, self.size)
        h = self.nc.virtual_sbuf_tensor(name, shape, dtype)
        self.tensors.append((h, off))
        return h

    def allocate(self, es):
        self.handle = es.enter_context(
            self.nc.sbuf_tensor("arena", [128, self.size], mybir.dt.uint8))
        base = self.nc.lookup_mloc(self.handle).addr
        for h, off in self.tensors:
            self.nc._tensor(h.name, h.shape, h.dtype, type="SB")
            mloc = self.nc.lookup_mloc(h)
            mloc.allocated = not self.nc.target_bir_lowering
            mloc.addr = base + off


def build(debug=False):
    nc = bass.Bass("TRN2", target_bir_lowering=False)

    x_h = nc.dram_tensor("x", [T, E], F32, kind="ExternalInput").ap()
    wq_h = nc.dram_tensor("wqT", [E, 3 * E], F32, kind="ExternalInput").ap()
    b2_h = nc.dram_tensor("b2", [128, 24], F32, kind="ExternalInput").ap()
    wg_h = nc.dram_tensor("wgT", [E, INNER], F32, kind="ExternalInput").ap()
    wu_h = nc.dram_tensor("wuT", [E, INNER], F32, kind="ExternalInput").ap()
    wd_h = nc.dram_tensor("wdT", [INNER, E], F32, kind="ExternalInput").ap()
    ct_h = nc.dram_tensor("Ct", [128, T], F32, kind="ExternalInput").ap()
    st_h = nc.dram_tensor("St", [128, T], F32, kind="ExternalInput").ap()
    pm_h = nc.dram_tensor("Pm", [128, 128], BF16, kind="ExternalInput").ap()
    idb_h = nc.dram_tensor("Idb", [128, 128], BF16, kind="ExternalInput").ap()
    idf_h = nc.dram_tensor("Idf", [128, 128], F32, kind="ExternalInput").ap()
    or_h = nc.dram_tensor("OnesRow", [1, 128], F32, kind="ExternalInput").ap()
    out_h = nc.dram_tensor("out", [T, E], F32, kind="ExternalOutput").ap()
    tk = "ExternalOutput" if debug else "Internal"
    tg_h = nc.dram_tensor("ternG", [E, INNER], BF16, kind=tk).ap()
    tu_h = nc.dram_tensor("ternU", [E, INNER], BF16, kind=tk).ap()
    td_h = nc.dram_tensor("ternD", [INNER, E], BF16, kind=tk).ap()
    dbg = {}
    if debug:
        dbg["d_qT"] = nc.dram_tensor("d_qT", [E, T], BF16, kind="ExternalOutput").ap()
        dbg["d_kT"] = nc.dram_tensor("d_kT", [E, T], BF16, kind="ExternalOutput").ap()
        dbg["d_vtok"] = nc.dram_tensor("d_vtok", [T, 1040], BF16, kind="ExternalOutput").ap()
        dbg["d_h1"] = nc.dram_tensor("d_h1", [T, E], F32, kind="ExternalOutput").ap()
        dbg["d_k1T"] = nc.dram_tensor("d_k1T", [E, T], BF16, kind="ExternalOutput").ap()
        dbg["d_sca"] = nc.dram_tensor("d_sca", [1, 16], F32, kind="ExternalOutput").ap()
        for _m in range(3):
            dbg[f"d_wacc{_m}"] = nc.dram_tensor(f"d_wacc{_m}", [128, 1], F32,
                                                kind="ExternalOutput").ap()
        dbg["d_xatt"] = nc.dram_tensor("d_xatt", [T, E], F32, kind="ExternalOutput").ap()

    ar = Arena(nc, 206 * KB)
    A = ar.at

    _o = [0]
    def sm(name, shape, dtype):
        h = A(name, _o[0], shape, dtype)
        per = _reduce(lambda a, b: a * b, shape[1:]) * mybir.dt.size(dtype)
        _o[0] += (per + 31) & ~31
        return h
    PERM = sm("PERM", [128, 128], BF16)
    IDB = sm("IDB", [128, 128], BF16)
    IDF = sm("IDF", [128, 128], F32)
    ONESC = sm("ONESC", [128, 1], BF16)
    ONESR = sm("ONESR", [1, 128], F32)
    B2 = sm("B2", [128, 24], F32)
    CH2Q = sm("CH2Q", [1, 16], F32)
    CH2K = sm("CH2K", [1, 16], F32)
    CH2S = sm("CH2S", [1, 8], F32)
    CROW = sm("CROW", [1, 16], F32)
    CROWP = sm("CROWP", [1, 8], F32)
    CBIASP = sm("CBIASP", [128, 8], F32)
    S1INV = sm("S1INV", [128, NT], F32)
    S1S = sm("S1S", [128, NT], F32)
    ALG = sm("ALG", [128, 1], F32)
    ALUp = sm("ALUp", [128, 1], F32)
    ALD = sm("ALD", [128, 1], F32)
    RABG = sm("RABG", [128, 1], F32)
    RABU = sm("RABU", [128, 1], F32)
    RABD = sm("RABD", [128, 1], F32)
    WACC = sm("WACC", [128, 1], F32)
    SCA = sm("SCA", [1, 16], F32)
    C1285 = sm("C1285", [128, 1], F32)
    C05 = sm("C05", [128, 1], F32)
    CM05 = sm("CM05", [128, 1], F32)
    SGT = sm("SGT", [128, NT], F32)
    SUT = sm("SUT", [128, TSB], F32)
    S2E = sm("S2E", [128, TSB], F32)
    SDT = sm("SDT", [128, TSB], F32)
    STAT = sm("STAT", [128, 8], F32)
    STAT2 = sm("STAT2", [128, 8], F32)
    GSTAT = sm("GSTAT", [128, 2], F32)
    assert _o[0] <= 6 * KB, _o[0]

    CBT = A("CbT", 6 * KB, [128, T], BF16)
    SBT = A("SbT", 10 * KB, [128, T], BF16)
    PT = [A(f"PT{i}", 6 * KB + i * 2 * KB, [128, 1024], BF16) for i in range(4)]
    OBUF = [A(f"OB{i}", 86 * KB + i * 2 * KB, [65, 512], F32) for i in range(2)]

    CF = A("Cf", 14 * KB, [128, T], F32)
    SF = A("Sf", 22 * KB, [128, T], F32)
    WQ = [A(f"Wq{i}", 14 * KB + i * 6 * KB, [128, 3 * E], BF16) for i in range(NE)]
    XB = [A(f"xb{t}", 14 * KB + t * 4 * KB, [128, E], F32) for t in range(NT)]
    ARAW = [A(f"araw{k}", 14 * KB + k * 16 * KB, [128, INNER], F32)
            for k in range(TSB)]
    KT2V = [A(f"k2tv{k}", 14 * KB + k * 16 * KB + 8 * KB, [128, INNER], BF16)
            for k in range(TSB)]

    XT = [A(f"xT{i}", 62 * KB + i * 4 * KB, [128, T], BF16) for i in range(NE)]
    WST = [A(f"wst{i}", 78 * KB + i * 4 * KB, [128, 1024], F32) for i in range(2)]
    CBQ = A("CBQ", 78 * KB, [1, 2048], F32)
    CBK = A("CBK", 86 * KB, [1, 2048], F32)
    SQS = A("sqs", 6 * KB, [128, 1024], BF16)
    SQS2 = A("sqs2", 8 * KB, [128, 1024], BF16)
    MST = [A(f"mst{i}", 200 * KB + i * 2 * KB, [128, 1024], BF16) for i in range(2)]
    WA_ = A("wa", 90 * KB, [128, 1024], BF16)
    WB_ = A("wb", 92 * KB, [128, 1024], BF16)
    ZSCR = A("zscr", 78 * KB, [128, E], F32)
    ZSCR2 = A("zscr2", 10 * KB, [128, E], F32)
    ZI1 = A("zi1", 82 * KB, [128, E], I32)
    M1 = A("m1", 86 * KB, [128, E], BF16)

    QT = [A(f"qT{i}", 94 * KB + i * 4 * KB, [128, T], BF16) for i in range(NE)]
    K1T = [A(f"k1T{i}", 94 * KB + i * 4 * KB, [128, T], BF16) for i in range(NE)]
    KT = [A(f"kT{i}", 126 * KB + i * 4 * KB, [128, T], BF16) for i in range(NE)]
    H1B = [A(f"h1b{t}", 126 * KB + t * 2 * KB, [128, E], BF16) for t in range(NT)]
    VTOK = [A(f"vt{t}", 158 * KB + t * 2144, [128, 1040], BF16) for t in range(NT)]
    RING = [[A(f"rg{r}_{j}", (158 * KB + r * 16 * KB if r < 2 else 78 * KB)
               + j * 2 * KB, [128, 1024], BF16) for j in range(8)]
            for r in range(3)]
    XBF = [A(f"xbf{i}", 192 * KB + i * 4 * KB, [128, E], F32) for i in range(2)]
    VSCR = A("vscr", 192 * KB, [128, T], F32)
    GB = [A(f"gb{k}", 190 * KB + k * 2 * KB, [128, 1024], BF16)
          for k in range(TSB)]
    TOUT = [A(f"tout{i}", 200 * KB + i * 2 * KB, [128, 1024], BF16) for i in range(2)]
    WAV = A("wav", 204 * KB, [128, 1024], BF16)
    OUTB = [A(f"outb{o}", 198 * KB + o * 4 * KB, [128, E], F32) for o in range(2)]
    ZVF = A("zvf", 198 * KB, [128, 1024], F32)
    ZVI = A("zvi", 202 * KB, [128, 1024], I32)
    ZGF = A("zgf", 6 * KB, [128, 1024], F32)
    ZGI = A("zgi", 10 * KB, [128, 1024], I32)

    es = ExitStack()
    PSA = es.enter_context(nc.psum_tensor("psA", [128, 1024], F32))
    PSB2 = es.enter_context(nc.psum_tensor("psB", [128, 1024], F32))
    PSC = es.enter_context(nc.psum_tensor("psC", [128, 1024], F32))
    PSD = es.enter_context(nc.psum_tensor("psD", [128, 1024], F32))
    ar.allocate(es)
    P = Prog(nc, es)

    bufs = {}
    def bf(h):
        if id(h) not in bufs:
            bufs[id(h)] = Buf()
        return bufs[id(h)]

    PSBANK = {}
    def pbank(tensor, lo):
        key = (id(tensor), lo // 512)
        if key not in PSBANK:
            PSBANK[key] = Buf()
        return PSBANK[key]

    HB = {}
    def hbuf(name):
        if name not in HB:
            HB[name] = Buf()
        return HB[name]

    def dma_ld(dst_ap, src_ap, reads=(), writes=()):
        P.emit("sync", lambda e, d=dst_ap, s=src_ap: e.dma_start(d, s),
               reads=reads, writes=writes, sem="ld", step=16)

    def dma_st(dst_ap, src_ap, reads=(), writes=()):
        P.emit("sync", lambda e, d=dst_ap, s=src_ap: e.dma_start(d, s),
               reads=reads, writes=writes, sem="st", step=16)

    def gdma(dst_ap, src_ap, reads=(), writes=()):
        P.emit("gpsimd", lambda e, d=dst_ap, s=src_ap: e.dma_start(d, s),
               reads=reads, writes=writes, sem="lg", step=16)

    for dst, src in ((PERM, pm_h), (IDB, idb_h), (IDF, idf_h),
                     (ONESR, or_h), (B2, b2_h), (CF, ct_h), (SF, st_h)):
        dma_ld(dst[:], src[:], writes=[bf(dst)])
    P.emit("vector", lambda e: e.memset(ONESC[:], 1.0), writes=[bf(ONESC)], sem="v")
    P.emit("vector", lambda e: e.memset(C1285[:], 128.5), writes=[bf(C1285)], sem="v")
    P.emit("vector", lambda e: e.memset(C05[:], 0.5), writes=[bf(C05)], sem="v")
    P.emit("vector", lambda e: e.memset(CM05[:], -0.5), writes=[bf(CM05)], sem="v")
    P.emit("vector", lambda e: e.tensor_copy(CBT[:], CF[:]),
           reads=[bf(CF)], writes=[bf(CBT)], sem="v")
    P.emit("vector", lambda e: e.tensor_copy(SBT[:], SF[:]),
           reads=[bf(SF)], writes=[bf(SBT)], sem="v")
    for t in range(NT):
        P.emit("vector", lambda e, t=t: e.memset(VTOK[t][:], 1.0),
               writes=[bf(VTOK[t])], sem="v")

    for t in range(NT):
        xb = XBF[t % 2]
        dma_ld(xb[:], x_h[t * 128:(t + 1) * 128, :], writes=[bf(xb)])
        for i in range(NE):
            pp = (t * NE + i) % 2
            pslice = PSD[:, pp * 512: pp * 512 + 128]
            pb = pbank(PSD, pp * 512)
            P.emit("tensor",
                   lambda e, p=pslice, xb=xb, i=i:
                       e.transpose(p, xb[:, i * 128:(i + 1) * 128], IDF[:]),
                   reads=[bf(xb), bf(IDF)], writes=[pb], sem="t")
            P.emit("vector",
                   lambda e, p=pslice, i=i, t=t:
                       e.tensor_copy(XT[i][:, t * 128:(t + 1) * 128], p),
                   reads=[pb], writes=[bf(XT[i])], sem="v")

    NCH = 32

    def wslice(m, ch):
        wh = (wg_h, wu_h, wd_h)[m]
        if m < 2:
            i, q = ch // 4, ch % 4
            return wh[i * 128:(i + 1) * 128, q * 1024:(q + 1) * 1024]
        return wh[ch * 128:(ch + 1) * 128, :]

    def tslice(m, ch):
        th = (tg_h, tu_h, td_h)[m]
        if m < 2:
            i, q = ch // 4, ch % 4
            return th[i * 128:(i + 1) * 128, q * 1024:(q + 1) * 1024]
        return th[ch * 128:(ch + 1) * 128, :]

    mean_wl = []

    def emit_mean(m, ch):
        w = MST[ch % 2]
        gdma(w[:], wslice(m, ch), reads=[hbuf(f"w{m}")], writes=[bf(w)])
        if ch == 0:
            P.emit("vector", lambda e: e.memset(WACC[:], 0.0),
                   writes=[bf(WACC)], sem="v")
        P.emit("vector",
               lambda e, w=w: e.reduce_sum(GSTAT[:, 0:1], w[:],
                                           axis=mybir.AxisListType.X,
                                           apply_absolute_value=True),
               reads=[bf(w)], writes=[bf(GSTAT)], sem="v")
        P.emit("vector", lambda e: e.tensor_add(WACC[:], WACC[:], GSTAT[:, 0:1]),
               reads=[bf(GSTAT), bf(WACC)], writes=[bf(WACC)], sem="v")
        if ch == NCH - 1:
            pb = pbank(PSD, 0)
            P.emit("tensor", lambda e: e.transpose(PSD[0:1, 0:128], WACC[:], IDF[:]),
                   reads=[bf(WACC), bf(IDF)], writes=[pb], sem="t")
            P.emit("vector", lambda e: e.reduce_sum(SCA[0:1, 0:1], PSD[0:1, 0:128],
                                                    axis=mybir.AxisListType.X),
                   reads=[pb], writes=[bf(SCA)], sem="v")
            P.emit("vector", lambda e, m=m: e.tensor_copy(
                       SCA[0:1, 10 + m:11 + m], SCA[0:1, 0:1]),
                   reads=[bf(SCA)], writes=[bf(SCA)], sem="v")
            if debug:
                dma_st(dbg[f"d_wacc{m}"][:], WACC[:], reads=[bf(WACC)])
            P.emit("vector", lambda e, m=m: e.tensor_scalar(
                       SCA[0:1, 1 + m:2 + m], SCA[0:1, 0:1], 1.0 / WQN, 1e-5,
                       ALU.mult, ALU.max),
                   reads=[bf(SCA)], writes=[bf(SCA)], sem="v")
            P.emit("vector", lambda e, m=m: e.reciprocal(
                       SCA[0:1, 4 + m:5 + m], SCA[0:1, 1 + m:2 + m]),
                   reads=[bf(SCA)], writes=[bf(SCA)], sem="v")
            P.emit("vector", lambda e, m=m: e.tensor_single_scalar(
                       SCA[0:1, 7 + m:8 + m], SCA[0:1, 1 + m:2 + m], 0.5,
                       ALU.mult),
                   reads=[bf(SCA)], writes=[bf(SCA)], sem="v")
            for col, dst in ((7 + m, (ALG, ALUp, ALD)[m]),
                             (4 + m, (RABG, RABU, RABD)[m])):
                pb2 = pbank(PSD, 512)
                P.emit("tensor", lambda e, col=col: e.matmul(
                           PSD[0:128, 512:513], ONESR[:], SCA[0:1, col:col + 1],
                           start=True, stop=True),
                       reads=[bf(SCA), bf(ONESR)], writes=[pb2], sem="t")
                P.emit("vector", lambda e, dst=dst: e.tensor_copy(
                           dst[:], PSD[0:128, 512:513]),
                       reads=[pb2], writes=[bf(dst)], sem="v")

    for m in range(3):
        for ch in range(NCH):
            mean_wl.append((emit_mean, m, ch))

    def emit_quant(m, ch):
        w = WST[0]
        rab = (RABG, RABU, RABD)[m]
        to = TOUT[0]
        if m == 0 and ch == 0:
            bf(WST[0]).inherit(bf(XT[4]), bf(CBQ))
            bf(WST[1]).inherit(bf(XT[5]), bf(CBQ))
            bf(WA_).inherit(bf(XT[7]), bf(CBK))
            bf(WB_).inherit(bf(XT[7]), bf(CBK))
            bf(TOUT[0]).inherit(bf(MST[0]), bf(MST[1]))
            bf(TOUT[1]).inherit(bf(MST[1]))
        dma_ld(w[:], wslice(m, ch), reads=[hbuf(f"w{m}")], writes=[bf(w)])
        P.emit("scalar", lambda e, w=w, rab=rab: e.activation(
                   WA_[:], w[:], AF.Sign, bias=C05[:], scale=rab[:]),
               reads=[bf(w), bf(rab), bf(C05)], writes=[bf(WA_)], sem="s")
        P.emit("scalar", lambda e, w=w, rab=rab: e.activation(
                   WB_[:], w[:], AF.Sign, bias=CM05[:], scale=rab[:]),
               reads=[bf(w), bf(rab), bf(CM05)], writes=[bf(WB_)], sem="s")
        P.emit("vector", lambda e, to=to: e.tensor_tensor(
                   to[:], WA_[:], WB_[:], ALU.add),
               reads=[bf(WA_), bf(WB_)], writes=[bf(to)], sem="v")
        dma_st(tslice(m, ch), to[:], reads=[bf(to)], writes=[hbuf(f"t{m}")])

    def emit_quant_v(m, ch):
        w = WST[ch % 2]
        rab = (RABG, RABU, RABD)[m]
        to = TOUT[ch % 2]
        if m == 0 and ch == 0:
            bf(WST[0]).inherit(bf(XT[4]), bf(CBQ))
            bf(WST[1]).inherit(bf(XT[5]), bf(CBQ))
            bf(WA_).inherit(bf(XT[7]), bf(CBK))
            bf(WB_).inherit(bf(XT[7]), bf(CBK))
            bf(TOUT[0]).inherit(bf(MST[0]), bf(MST[1]))
            bf(TOUT[1]).inherit(bf(MST[1]))
        dma_ld(w[:], wslice(m, ch), reads=[hbuf(f"w{m}")], writes=[bf(w)])
        P.emit("vector", lambda e, w=w, rab=rab: e.tensor_scalar(
                   WAV[:], w[:], rab[:], 0.5, ALU.mult, ALU.is_ge),
               reads=[bf(w), bf(rab)], writes=[bf(WAV)], sem="v")
        P.emit("vector", lambda e, w=w, rab=rab, to=to: e.tensor_scalar(
                   to[:], w[:], rab[:], -0.5, ALU.mult, ALU.is_ge),
               reads=[bf(w), bf(rab)], writes=[bf(to)], sem="v")
        P.emit("vector", lambda e, to=to: e.tensor_tensor(
                   to[:], to[:], WAV[:], ALU.add),
               reads=[bf(to), bf(WAV)], writes=[bf(to)], sem="v")
        P.emit("vector", lambda e, to=to: e.tensor_scalar(
                   to[:], to[:], 2.0, -2.0, ALU.mult, ALU.add),
               reads=[bf(to)], writes=[bf(to)], sem="v")
        dma_st(tslice(m, ch), to[:], reads=[bf(to)], writes=[hbuf(f"t{m}")])

    wq_s = []
    wq_v = [(emit_quant_v, m, ch) for m in (0, 1, 2) for ch in range(NCH)]

    def drain(wl, n):
        for _ in range(min(n, len(wl))):
            fn, m, ch = wl.pop(0)
            fn(m, ch)

    bf(VSCR).inherit(bf(XBF[0]), bf(XBF[1]))
    for i in range(3):
        bf(WQ[i]).inherit(bf(CF), bf(SF))
    for i in range(NE):
        gdma(WQ[i][:], wq_h[i * 128:(i + 1) * 128, :], writes=[bf(WQ[i])])

    def emit_rope(tl):
        tb = bf(tl)
        for h in range(2):
            for lo in (0, 512):
                pb = pbank(PSC, lo)
                P.emit("tensor",
                       lambda e, lo=lo, h=h, tl=tl:
                           e.matmul(PSC[:, lo:lo + 512], PERM[:],
                                    tl[:, h * 1024 + lo:h * 1024 + lo + 512],
                                    start=True, stop=True),
                       reads=[bf(PERM), tb], writes=[pb], sem="t")
                P.emit("vector",
                       lambda e, lo=lo, h=h:
                           e.tensor_tensor(
                               WAV[:, lo:lo + 512], PSC[:, lo:lo + 512],
                               SBT[:, h * 1024 + lo:h * 1024 + lo + 512],
                               ALU.mult),
                       reads=[pb, bf(SBT)], writes=[bf(WAV)], sem="v")
            P.emit("vector",
                   lambda e, h=h, tl=tl:
                       e.tensor_tensor(tl[:, h * 1024:(h + 1) * 1024],
                                       tl[:, h * 1024:(h + 1) * 1024],
                                       CBT[:, h * 1024:(h + 1) * 1024],
                                       ALU.mult),
                   reads=[tb, bf(CBT)], writes=[tb], sem="v")
            P.emit("vector",
                   lambda e, h=h, tl=tl:
                       e.tensor_tensor(tl[:, h * 1024:(h + 1) * 1024],
                                       tl[:, h * 1024:(h + 1) * 1024],
                                       WAV[:], ALU.add),
                   reads=[tb, bf(WAV)], writes=[tb], sem="v")

    QKVPS = [(PSA, 0), (PSA, 512), (PSB2, 0), (PSB2, 512)]
    for ot in range(24):
        for i in range(NE):
            for tc in range(4):
                ps, lo = QKVPS[tc]
                P.emit("tensor",
                       lambda e, ps=ps, lo=lo, i=i, ot=ot, tc=tc:
                           e.matmul(ps[:, lo:lo + 512],
                                    WQ[i][:, ot * 128:(ot + 1) * 128],
                                    XT[i][:, tc * 512:(tc + 1) * 512],
                                    start=(i == 0), stop=(i == NE - 1)),
                       reads=[bf(WQ[i]), bf(XT[i])], writes=[pbank(ps, lo)],
                       sem="t")
        for tc in range(4):
            ps, lo = QKVPS[tc]
            if ot < 8:
                dst, db = QT[ot], bf(QT[ot])
            elif ot < 16:
                dst, db = KT[ot - 8], bf(KT[ot - 8])
            else:
                dst, db = VSCR, bf(VSCR)
            P.emit("scalar",
                   lambda e, ps=ps, lo=lo, dst=dst, ot=ot, tc=tc:
                       e.activation(dst[:, tc * 512:(tc + 1) * 512],
                                    ps[:, lo:lo + 512], AF.Identity,
                                    bias=B2[:, ot:ot + 1], scale=1.0),
                   reads=[pbank(ps, lo), bf(B2)], writes=[db], sem="s")
        if ot >= 16:
            h0 = 2 * (ot - 16)
            for kt in range(NT):
                pp = kt % 2
                pslice = PSD[:, pp * 512: pp * 512 + 128]
                pb = pbank(PSD, pp * 512)
                P.emit("tensor",
                       lambda e, p=pslice, kt=kt:
                           e.transpose(p, VSCR[:, kt * 128:(kt + 1) * 128], IDF[:]),
                       reads=[bf(VSCR), bf(IDF)], writes=[pb], sem="t")
                for hh in range(2):
                    P.emit("vector",
                           lambda e, p=pslice, kt=kt, h=h0 + hh, hh=hh:
                               e.tensor_copy(VTOK[kt][:, h * 65: h * 65 + 64],
                                             p[:, hh * 64:(hh + 1) * 64]),
                           reads=[pb], writes=[bf(VTOK[kt])], sem="v")
        if 1 <= ot <= 8:
            emit_rope(QT[ot - 1])
        elif 9 <= ot <= 16:
            emit_rope(KT[ot - 9])
        drain(mean_wl, 4)

    bf(CBQ).inherit(bf(XT[4]), bf(XT[5]))
    bf(SQS).inherit(bf(CBT))
    bf(SQS2).inherit(bf(CBT))
    CPS = [(PSC, 0), (PSC, 512), (PSD, 0), (PSD, 512)]
    for tiles, chrow in ((QT, CH2Q), (KT, CH2K)):
        for hp in range(NE):
            for hh in range(2):
                sq = (SQS, SQS2)[hh]
                P.emit("scalar",
                       lambda e, tl=tiles[hp], hh=hh, sq=sq:
                           e.activation(sq[:], tl[:, hh * 1024:(hh + 1) * 1024],
                                        AF.Square),
                       reads=[bf(tiles[hp])], writes=[bf(sq)], sem="s")
                for side in range(2):
                    for cc in range(2):
                        ps, lo = CPS[side * 2 + cc]
                        P.emit("tensor",
                               lambda e, ps=ps, lo=lo, cc=cc, side=side, sq=sq:
                                   e.matmul(ps[0:1, lo:lo + 512],
                                            ONESC[side * 64:(side + 1) * 64, 0:1],
                                            sq[side * 64:(side + 1) * 64,
                                                cc * 512:(cc + 1) * 512],
                                            start=True, stop=True),
                               reads=[bf(ONESC), bf(sq)],
                               writes=[pbank(ps, lo)], sem="t")
                for side in range(2):
                    for cc in range(2):
                        ps, lo = CPS[side * 2 + cc]
                        col = side * 4 + hh * 2 + cc
                        P.emit("vector",
                               lambda e, ps=ps, lo=lo, col=col:
                                   e.reduce_max(CH2S[0:1, col:col + 1],
                                                ps[0:1, lo:lo + 512],
                                                axis=mybir.AxisListType.X),
                               reads=[pbank(ps, lo)], writes=[bf(CH2S)],
                               sem="v")
            P.emit("vector", lambda e, hp=hp, chrow=chrow: e.reduce_max(
                       chrow[0:1, 2 * hp:2 * hp + 1], CH2S[0:1, 0:4],
                       axis=mybir.AxisListType.X),
                   reads=[bf(CH2S)], writes=[bf(chrow)], sem="v")
            P.emit("vector", lambda e, hp=hp, chrow=chrow: e.reduce_max(
                       chrow[0:1, 2 * hp + 1:2 * hp + 2], CH2S[0:1, 4:8],
                       axis=mybir.AxisListType.X),
                   reads=[bf(CH2S)], writes=[bf(chrow)], sem="v")
    P.emit("vector", lambda e: e.tensor_tensor(CROW[:], CH2Q[:], CH2K[:],
                                               ALU.mult),
           reads=[bf(CH2Q), bf(CH2K)], writes=[bf(CROW)], sem="v")
    P.emit("scalar", lambda e: e.activation(CROW[:], CROW[:], AF.Sqrt,
                                            scale=1.0 / 64.0),
           reads=[bf(CROW)], writes=[bf(CROW)], sem="s")
    P.emit("vector", lambda e: e.tensor_tensor(CROWP[:], CROW[0:1, 0:16:2],
                                               CROW[0:1, 1:16:2], ALU.max),
           reads=[bf(CROW)], writes=[bf(CROWP)], sem="v")
    P.emit("vector", lambda e: e.tensor_single_scalar(CROWP[:], CROWP[:], -1.0,
                                                      ALU.mult),
           reads=[bf(CROWP)], writes=[bf(CROWP)], sem="v")
    pbX = pbank(PSD, 0)
    P.emit("tensor", lambda e: e.matmul(PSD[0:128, 0:8], ONESR[:], CROWP[:],
                                        start=True, stop=True),
           reads=[bf(ONESR), bf(CROWP)], writes=[pbX], sem="t")
    P.emit("vector", lambda e: e.tensor_copy(CBIASP[:], PSD[0:128, 0:8]),
           reads=[pbX], writes=[bf(CBIASP)], sem="v")

    for t in range(NT):
        bf(XB[t]).inherit(*[bf(WQ[i]) for i in range(NE)],
                          *[bf(XT[i]) for i in range(4)])
        dma_ld(XB[t][:], x_h[t * 128:(t + 1) * 128, :], writes=[bf(XB[t])])
    for pt in PT:
        bf(pt).inherit(bf(CBT), bf(SBT), bf(SQS), bf(SQS2))
    for ob in OBUF:
        bf(ob).inherit(bf(XT[6]), bf(CBK))

    for hp in range(NE):
        h0, h1 = 2 * hp, 2 * hp + 1
        for qc in range(4):
            def emit_S(kt):
                ps = PSA if kt % 2 == 0 else PSB2
                for side in range(2):
                    lo = side * 512
                    P.emit("tensor",
                           lambda e, ps=ps, lo=lo, side=side, kt=kt, qc=qc, hp=hp:
                               e.matmul(ps[:, lo:lo + 512],
                                        KT[hp][side * 64:(side + 1) * 64,
                                               kt * 128:(kt + 1) * 128],
                                        QT[hp][side * 64:(side + 1) * 64,
                                               qc * 512:(qc + 1) * 512],
                                        start=True, stop=True,
                                        tile_position=(side * 64, 0)),
                           reads=[bf(KT[hp]), bf(QT[hp])],
                           writes=[pbank(ps, lo)], sem="t")

            def emit_exp(kt):
                ps = PSA if kt % 2 == 0 else PSB2
                pt = PT[kt % 4]
                P.emit("scalar",
                       lambda e, ps=ps, pt=pt, hp=hp:
                           e.activation(pt[:], ps[:], AF.Exp,
                                        bias=CBIASP[:, hp:hp + 1], scale=0.125),
                       reads=[pbank(ps, 0), pbank(ps, 512), bf(CBIASP)],
                       writes=[bf(pt)], sem="s")

            def emit_PV(kt):
                pt = PT[kt % 4]
                for side, h in ((0, h0), (1, h1)):
                    lo = side * 512
                    P.emit("tensor",
                           lambda e, pt=pt, lo=lo, h=h, kt=kt:
                               e.matmul(PSC[0:65, lo:lo + 512],
                                        VTOK[kt][:, h * 65: h * 65 + 65],
                                        pt[:, lo:lo + 512],
                                        start=(kt == 0), stop=(kt == NT - 1)),
                           reads=[bf(VTOK[kt]), bf(pt)],
                           writes=[pbank(PSC, lo)], sem="t")

            emit_S(0)
            emit_exp(0)
            for kt in range(1, NT):
                emit_S(kt)
                emit_PV(kt - 1)
                emit_exp(kt)
            emit_PV(NT - 1)

            for side, h in ((0, h0), (1, h1)):
                lo = side * 512
                ob = OBUF[side]
                P.emit("scalar",
                       lambda e, ob=ob, lo=lo:
                           e.copy(ob[:], PSC[0:65, lo:lo + 512]),
                       reads=[pbank(PSC, lo)], writes=[bf(ob)], sem="s")
                for j in range(4):
                    lo2 = (j % 2) * 512
                    pb = pbank(PSD, lo2)
                    P.emit("tensor",
                           lambda e, ob=ob, j=j, lo2=lo2:
                               e.transpose(PSD[:, lo2:lo2 + 65],
                                           ob[0:65, j * 128:(j + 1) * 128],
                                           IDF[0:65, 0:65]),
                           reads=[bf(ob), bf(IDF)], writes=[pb], sem="t")
                    P.emit("vector",
                           lambda e, lo2=lo2:
                               e.reciprocal(STAT[:, 0:1], PSD[:, lo2 + 64:lo2 + 65]),
                           reads=[pb], writes=[bf(STAT)], sem="v")
                    tb = XB[qc * 4 + j]
                    P.emit("vector",
                           lambda e, lo2=lo2, tb=tb, h=h:
                               e.scalar_tensor_tensor(
                                   tb[:, h * 64:(h + 1) * 64],
                                   PSD[:, lo2:lo2 + 64], STAT[:, 0:1],
                                   tb[:, h * 64:(h + 1) * 64],
                                   ALU.mult, ALU.add),
                           reads=[pb, bf(STAT), bf(tb)], writes=[bf(tb)],
                           sem="v")
            drain(wq_v, 3)
    drain(wq_s, len(wq_s))
    drain(wq_v, len(wq_v))

    if debug:
        for i in range(NE):
            dma_st(dbg["d_qT"][i * 128:(i + 1) * 128, :], QT[i][:], reads=[bf(QT[i])])
            dma_st(dbg["d_kT"][i * 128:(i + 1) * 128, :], KT[i][:], reads=[bf(KT[i])])
        for t in range(NT):
            dma_st(dbg["d_vtok"][t * 128:(t + 1) * 128, :], VTOK[t][:],
                   reads=[bf(VTOK[t])])
            dma_st(dbg["d_xatt"][t * 128:(t + 1) * 128, :], XB[t][:],
                   reads=[bf(XB[t])])
        dma_st(dbg["d_sca"][:], SCA[:], reads=[bf(SCA)])

    def emit_ln(tile, tbuf, st, stb, sq, sqb):
        P.emit("vector", lambda e: e.reduce_sum(st[:, 0:1], tile[:],
                                                axis=mybir.AxisListType.X),
               reads=[tbuf], writes=[stb], sem="v")
        P.emit("scalar", lambda e: e.activation(sq, tile[:], AF.Square,
                                                accum_out=st[:, 1:2]),
               reads=[tbuf], writes=[stb] + list(sqb), sem="s")
        P.emit("vector", lambda e: e.tensor_single_scalar(
                   st[:, 2:3], st[:, 0:1], 1.0 / float(E), ALU.mult),
               reads=[stb], writes=[stb], sem="v")
        P.emit("vector", lambda e: e.tensor_tensor(st[:, 3:4], st[:, 2:3],
                                                   st[:, 2:3], ALU.mult),
               reads=[stb], writes=[stb], sem="v")
        P.emit("vector", lambda e: e.scalar_tensor_tensor(
                   st[:, 4:5], st[:, 1:2], 1.0 / float(E), st[:, 3:4],
                   ALU.mult, ALU.subtract),
               reads=[stb], writes=[stb], sem="v")
        P.emit("vector", lambda e: e.tensor_single_scalar(
                   st[:, 5:6], st[:, 4:5], 1e-5, ALU.add),
               reads=[stb], writes=[stb], sem="v")
        P.emit("scalar", lambda e: e.activation(st[:, 6:7], st[:, 5:6], AF.Sqrt),
               reads=[stb], writes=[stb], sem="s")
        P.emit("vector", lambda e: e.reciprocal(st[:, 7:8], st[:, 6:7]),
               reads=[stb], writes=[stb], sem="v")
        P.emit("vector", lambda e: e.tensor_tensor(st[:, 3:4], st[:, 7:8],
                                                   st[:, 7:8], ALU.mult),
               reads=[stb], writes=[stb], sem="v")
        P.emit("vector", lambda e: e.scalar_tensor_tensor(
                   st[:, 4:5], st[:, 5:6], 0.5, st[:, 3:4],
                   ALU.mult, ALU.mult),
               reads=[stb], writes=[stb], sem="v")
        P.emit("vector", lambda e: e.tensor_scalar(
                   st[:, 3:4], st[:, 4:5], -1.0, 1.5, ALU.mult, ALU.add),
               reads=[stb], writes=[stb], sem="v")
        P.emit("vector", lambda e: e.tensor_tensor(st[:, 6:7], st[:, 7:8],
                                                   st[:, 3:4], ALU.mult),
               reads=[stb], writes=[stb], sem="v")
        P.emit("vector", lambda e: e.tensor_scalar(
                   tile[:], tile[:], st[:, 2:3], st[:, 6:7],
                   ALU.subtract, ALU.mult),
               reads=[stb, tbuf], writes=[tbuf], sem="v")

    if debug:
        for t in range(NT):
            dma_st(dbg["d_h1"][t * 128:(t + 1) * 128, :], XB[t][:],
                   reads=[bf(XB[t])])

    bf(ZSCR).inherit(bf(WST[0]))
    bf(ZSCR2).inherit(bf(PT[2]), bf(PT[3]))
    bf(ZI1).inherit(bf(WST[1]))
    bf(M1).inherit(bf(CBK), bf(OBUF[0]), bf(OBUF[1]))
    for i in range(NE):
        bf(K1T[i]).inherit(bf(QT[i]))
    for t in range(NT):
        bf(H1B[t]).inherit(bf(KT[t // 2]))

    for t in range(NT):
        st = (STAT, STAT2)[t % 2]
        zs = (ZSCR, ZSCR2)[t % 2]
        emit_ln(XB[t], bf(XB[t]), st, bf(st),
                PSC[:, 0:1024], [pbank(PSC, 0), pbank(PSC, 512)])
        P.emit("vector", lambda e, t=t, st=st: e.reduce_max(
                   st[:, 0:1], XB[t][:], axis=mybir.AxisListType.X,
                   apply_absolute_value=True),
               reads=[bf(XB[t])], writes=[bf(st)], sem="v")
        P.emit("vector", lambda e, st=st: e.tensor_single_scalar(
                   st[:, 1:2], st[:, 0:1], 1e-5, ALU.max),
               reads=[bf(st)], writes=[bf(st)], sem="v")
        P.emit("vector", lambda e, st=st: e.reciprocal(st[:, 2:3], st[:, 1:2]),
               reads=[bf(st)], writes=[bf(st)], sem="v")
        P.emit("vector", lambda e, t=t, st=st: e.tensor_single_scalar(
                   S1S[:, t:t + 1], st[:, 2:3], 127.0, ALU.mult),
               reads=[bf(st)], writes=[bf(S1S)], sem="v")
        P.emit("vector", lambda e, t=t, st=st: e.tensor_single_scalar(
                   S1INV[:, t:t + 1], st[:, 1:2], 1.0 / 127.0, ALU.mult),
               reads=[bf(st)], writes=[bf(S1INV)], sem="v")
        P.emit("scalar", lambda e, t=t, zs=zs: e.activation(
                   zs[:], XB[t][:], AF.Identity, bias=C1285[:],
                   scale=S1S[:, t:t + 1]),
               reads=[bf(XB[t]), bf(S1S), bf(C1285)], writes=[bf(zs)], sem="s")
        P.emit("vector", lambda e, zs=zs: e.tensor_copy(ZI1[:], zs[:]),
               reads=[bf(zs)], writes=[bf(ZI1)], sem="v")
        P.emit("vector", lambda e, zs=zs: e.scalar_tensor_tensor(
                   M1[:], zs[:], 0.0, ZI1[:], ALU.add, ALU.is_lt),
               reads=[bf(zs), bf(ZI1)], writes=[bf(M1)], sem="v")
        P.emit("vector", lambda e, zs=zs: e.scalar_tensor_tensor(
                   zs[:], ZI1[:], -128.0, M1[:], ALU.add, ALU.subtract),
               reads=[bf(ZI1), bf(M1)], writes=[bf(zs)], sem="v")
        for i in range(NE):
            lo = ((t * NE + i) % 2) * 512
            pb = pbank(PSD, lo)
            P.emit("tensor", lambda e, lo=lo, i=i, zs=zs: e.transpose(
                       PSD[:, lo:lo + 128], zs[:, i * 128:(i + 1) * 128], IDF[:]),
                   reads=[bf(zs), bf(IDF)], writes=[pb], sem="t")
            P.emit("vector", lambda e, lo=lo, i=i, t=t: e.tensor_copy(
                       K1T[i][:, t * 128:(t + 1) * 128], PSD[:, lo:lo + 128]),
                   reads=[pb], writes=[bf(K1T[i])], sem="v")
        P.emit("vector", lambda e, t=t: e.tensor_copy(H1B[t][:], XB[t][:]),
               reads=[bf(XB[t])], writes=[bf(H1B[t])], sem="v")

    if debug:
        for i in range(NE):
            dma_st(dbg["d_k1T"][i * 128:(i + 1) * 128, :], K1T[i][:],
                   reads=[bf(K1T[i])])

    for k in range(TSB):
        for t in range(4 * k, 4 * k + 4):
            bf(ARAW[k]).inherit(bf(XB[t]))
    bf(ARAW[0]).inherit(bf(CF), bf(SF))
    bf(ARAW[3]).inherit(bf(XT[0]), bf(XT[1]), bf(XT[2]), bf(XT[3]))
    for k in range(TSB):
        bufs[id(KT2V[k])] = bf(ARAW[k])
    for zg in (ZGF, ZGI):
        bf(zg).inherit(bf(CBT), bf(SBT), bf(PT[0]), bf(PT[1]),
                       bf(PT[2]), bf(PT[3]), bf(SQS), bf(SQS2), bf(ZSCR2))
    for j in range(8):
        for t in range(0, 9):
            bf(RING[0][j]).inherit(bf(VTOK[t]))
        for t in range(7, NT):
            bf(RING[1][j]).inherit(bf(VTOK[t]))
        bf(RING[2][j]).inherit(bf(ZSCR), bf(ZI1), bf(M1), bf(WA_), bf(WB_),
                               bf(WST[0]), bf(WST[1]), bf(OBUF[0]), bf(OBUF[1]),
                               bf(CBQ), bf(CBK),
                               *[bf(XT[i]) for i in range(4, NE)])
    for k in range(TSB):
        bf(GB[k]).inherit(bf(VTOK[14]), bf(VTOK[15]), bf(XBF[0]),
                          bf(XBF[1]), bf(VSCR))
    bf(OUTB[0]).inherit(bf(XBF[1]), bf(TOUT[0]), bf(MST[0]))
    bf(OUTB[1]).inherit(bf(TOUT[0]), bf(TOUT[1]), bf(WAV), bf(MST[1]))

    sched = []
    for sb in range(NSB):
        sched.append(("g", sb, 0))
        if sb > 0:
            for q in range(4):
                sched.append(("dA", sb - 1, q))
            for q in range(4):
                sched.append(("dB", sb - 1, q))
        sched.append(("u", sb, 0))
        for q in range(1, 4):
            sched.append(("g", sb, q))
            sched.append(("u", sb, q))
    for q in range(4):
        sched.append(("dA", NSB - 1, q))
    for q in range(4):
        sched.append(("dB", NSB - 1, q))

    def load_chunk(ci):
        kind, sb, q = sched[ci]
        slot = RING[ci % 3]
        hb = hbuf({"g": "t0", "u": "t1", "dA": "t2", "dB": "t2"}[kind])
        for j in range(8):
            if kind in ("dA", "dB"):
                src = td_h[(q * 8 + j) * 128:(q * 8 + j + 1) * 128, :]
            else:
                wh = tg_h if kind == "g" else tu_h
                src = wh[j * 128:(j + 1) * 128, q * 1024:(q + 1) * 1024]
            dma_ld(slot[j][:], src, reads=[hb], writes=[bf(slot[j])])

    def emit_scales(sb, k):
        gti = sb * TSB + k
        P.emit("vector", lambda e, k=k: e.reduce_max(
                   STAT2[:, 0:1], ARAW[k][:], axis=mybir.AxisListType.X,
                   apply_absolute_value=True),
               reads=[bf(ARAW[k])], writes=[bf(STAT2)], sem="v")
        P.emit("vector", lambda e, gti=gti, k=k: e.tensor_tensor(
                   SUT[:, k:k + 1], ALUp[:], S1INV[:, gti:gti + 1], ALU.mult),
               reads=[bf(ALUp), bf(S1INV)], writes=[bf(SUT)], sem="v")
        P.emit("vector", lambda e, k=k: e.tensor_tensor(
                   STAT2[:, 1:2], STAT2[:, 0:1], SUT[:, k:k + 1], ALU.mult),
               reads=[bf(STAT2), bf(SUT)], writes=[bf(STAT2)], sem="v")
        P.emit("vector", lambda e: e.tensor_single_scalar(
                   STAT2[:, 2:3], STAT2[:, 1:2], 1e-5, ALU.max),
               reads=[bf(STAT2)], writes=[bf(STAT2)], sem="v")
        P.emit("vector", lambda e: e.reciprocal(STAT2[:, 3:4], STAT2[:, 2:3]),
               reads=[bf(STAT2)], writes=[bf(STAT2)], sem="v")
        P.emit("vector", lambda e: e.tensor_single_scalar(
                   STAT2[:, 4:5], STAT2[:, 3:4], 127.0, ALU.mult),
               reads=[bf(STAT2)], writes=[bf(STAT2)], sem="v")
        P.emit("vector", lambda e, k=k: e.tensor_tensor(
                   S2E[:, k:k + 1], SUT[:, k:k + 1], STAT2[:, 4:5], ALU.mult),
               reads=[bf(SUT), bf(STAT2)], writes=[bf(S2E)], sem="v")
        P.emit("vector", lambda e: e.tensor_single_scalar(
                   STAT2[:, 5:6], STAT2[:, 2:3], 1.0 / 127.0, ALU.mult),
               reads=[bf(STAT2)], writes=[bf(STAT2)], sem="v")
        P.emit("vector", lambda e, k=k: e.tensor_tensor(
                   SDT[:, k:k + 1], ALD[:], STAT2[:, 5:6], ALU.mult),
               reads=[bf(ALD), bf(STAT2)], writes=[bf(SDT)], sem="v")

    def emit_rounds(sb, k):
        ab = bf(ARAW[k])
        for c in range(4):
            ar = ARAW[k][:, c * 1024:(c + 1) * 1024]
            zf, zi = (ZVF, ZVI) if c % 2 == 0 else (ZGF, ZGI)
            P.emit("vector", lambda e, ar=ar, k=k, zf=zf: e.tensor_scalar(
                       zf[:], ar, S2E[:, k:k + 1], 128.5, ALU.mult, ALU.add),
                   reads=[bf(S2E), ab], writes=[bf(zf)], sem="v")
            P.emit("vector", lambda e, zf=zf, zi=zi: e.tensor_copy(
                       zi[:], zf[:]),
                   reads=[bf(zf)], writes=[bf(zi)], sem="v")
            P.emit("vector", lambda e, zf=zf, zi=zi: e.tensor_tensor(
                       zf[:], zf[:], zi[:], ALU.is_lt),
                   reads=[bf(zf), bf(zi)], writes=[bf(zf)], sem="v")
            P.emit("vector", lambda e, k=k, c=c, zf=zf, zi=zi:
                       e.scalar_tensor_tensor(
                           ARAW[k][:, c * 1024:(c + 1) * 1024], zi[:],
                           -128.0, zf[:], ALU.add, ALU.subtract),
                   reads=[bf(zi), bf(zf)], writes=[ab], sem="v")

    def emit_tr(sb, k):
        ps = (PSC, PSD)[k % 2]
        ab = bf(ARAW[k])
        for g in range(7, -1, -1):
            lo = (g % 2) * 512
            pb = pbank(ps, lo)
            for w in range(4):
                oc = 4 * g + w
                P.emit("tensor",
                       lambda e, ps=ps, lo=lo, w=w, oc=oc, k=k: e.transpose(
                           ps[:, lo + w * 128:lo + (w + 1) * 128],
                           ARAW[k][:, oc * 128:(oc + 1) * 128], IDF[:]),
                       reads=[ab, bf(IDF)], writes=[pb], sem="t")
            P.emit("scalar", lambda e, ps=ps, lo=lo, g=g, k=k: e.copy(
                       KT2V[k][:, g * 512:(g + 1) * 512], ps[:, lo:lo + 512]),
                   reads=[pb], writes=[ab], sem="s")

    loaded = [False] * len(sched)

    def ensure_load(ci):
        if ci < len(sched) and not loaded[ci]:
            load_chunk(ci)
            loaded[ci] = True

    P.emit("vector", lambda e: e.tensor_scalar(
               SGT[:, 0:NT], S1INV[:, 0:NT], ALG[:], None, ALU.mult),
           reads=[bf(ALG), bf(S1INV)], writes=[bf(SGT)], sem="v")

    pend_tr = []

    for ci, (kind, sb, q) in enumerate(sched):
        for la in range(3):
            ensure_load(ci + la)
        slot = RING[ci % 3]
        if kind in ("g", "u"):
            def emit_gu_mm(k):
                gti = sb * TSB + k
                ps = (PSA, PSB2)[k % 2]
                for i in range(NE):
                    for oc2 in range(2):
                        lo = oc2 * 512
                        P.emit("tensor",
                               lambda e, ps=ps, lo=lo, i=i, gti=gti, slot=slot:
                                   e.matmul(ps[:, lo:lo + 512],
                                            K1T[i][:, gti * 128:(gti + 1) * 128],
                                            slot[i][:, lo:lo + 512],
                                            start=(i == 0), stop=(i == NE - 1)),
                               reads=[bf(K1T[i]), bf(slot[i])],
                               writes=[pbank(ps, lo)], sem="t")

            def emit_gu_epi(k):
                ps = (PSA, PSB2)[k % 2]
                if kind == "g":
                    gti2 = sb * TSB + k
                    P.emit("scalar", lambda e, ps=ps, k=k, gti2=gti2:
                               e.activation(GB[k][:], ps[:], AF.Silu,
                                            scale=SGT[:, gti2:gti2 + 1]),
                           reads=[pbank(ps, 0), pbank(ps, 512), bf(SGT)],
                           writes=[bf(GB[k])], sem="s")
                else:
                    P.emit("vector", lambda e, ps=ps, k=k, q=q: e.tensor_tensor(
                               ARAW[k][:, q * 1024:(q + 1) * 1024], GB[k][:],
                               ps[:], ALU.mult),
                           reads=[bf(GB[k]), pbank(ps, 0), pbank(ps, 512)],
                           writes=[bf(ARAW[k])], sem="v")

            if kind == "u" and q == 3:
                emit_gu_mm(0)
                emit_gu_mm(1)
                emit_gu_epi(0)
                emit_gu_epi(1)
                emit_gu_mm(2)
                emit_gu_mm(3)
                emit_gu_epi(2)
                emit_gu_epi(3)
                bf(ZVF).inherit(bf(OUTB[0]))
                bf(ZVI).inherit(bf(OUTB[1]))
                for k in range(TSB):
                    emit_scales(sb, k)
                    emit_rounds(sb, k)
                    pend_tr.append((sb, k))
            else:
                for k in range(TSB):
                    emit_gu_mm(k)
                    emit_gu_epi(k)
        else:
            if q == 0:
                for _ in range(2):
                    if pend_tr:
                        emit_tr(*pend_tr.pop(0))
            ks = (0, 1) if kind == "dA" else (2, 3)
            for k in ks:
                ps = (PSC, PSD)[k % 2]
                ab = bf(ARAW[k])
                for j in range(8):
                    oc = q * 8 + j
                    for c in range(2):
                        lo = c * 512
                        P.emit("tensor",
                               lambda e, ps=ps, lo=lo, oc=oc, k=k, slot=slot,
                                       j=j, q=q:
                                   e.matmul(ps[:, lo:lo + 512],
                                            KT2V[k][:, oc * 128:(oc + 1) * 128],
                                            slot[j][:, lo:lo + 512],
                                            start=(q == 0 and j == 0),
                                            stop=(q == 3 and j == 7)),
                               reads=[ab, bf(slot[j])],
                               writes=[pbank(ps, lo)], sem="t")
            if q == 3:
                for k in ks:
                    gti = sb * TSB + k
                    ps = (PSC, PSD)[k % 2]
                    ob_ = OUTB[k % 2]
                    if kind == "dA" and k == 0:
                        bf(OUTB[0]).inherit(bf(ZVF))
                        bf(OUTB[1]).inherit(bf(ZVI))
                    for c in range(2):
                        lo = c * 512
                        P.emit("vector",
                               lambda e, ps=ps, lo=lo, ob_=ob_, k=k, gti=gti:
                                   e.scalar_tensor_tensor(
                                       ob_[:, lo:lo + 512], ps[:, lo:lo + 512],
                                       SDT[:, k:k + 1],
                                       H1B[gti][:, lo:lo + 512],
                                       ALU.mult, ALU.add),
                               reads=[pbank(ps, lo), bf(SDT), bf(H1B[gti])],
                               writes=[bf(ob_)], sem="v")
                    emit_ln(ob_, bf(ob_), STAT2, bf(STAT2), ps[:, 0:1024],
                            [pbank(ps, 0), pbank(ps, 512)])
                    P.emit("gpsimd",
                           lambda e, ob_=ob_, gti=gti:
                               e.dma_start(out_h[gti * 128:(gti + 1) * 128, :],
                                           ob_[:]),
                           reads=[bf(ob_)], sem="st", step=16)

    P.q["sync"].append(lambda e, v=P.cnt["st"]: e.wait_ge(P.sem["st"], v))

    with nc.Block() as blk:
        @blk.tensor
        def _(eng):
            for f in P.q["tensor"]:
                f(eng)

        @blk.scalar
        def _(eng):
            for f in P.q["scalar"]:
                f(eng)

        @blk.vector
        def _(eng):
            for f in P.q["vector"]:
                f(eng)

        @blk.gpsimd
        def _(eng):
            for f in P.q["gpsimd"]:
                f(eng)

        @blk.sync
        def _(eng):
            for f in P.q["sync"]:
                f(eng)

    es.close()
    return nc


_CACHE = {}


def _get_nc(debug=False):
    if debug not in _CACHE:
        _CACHE[debug] = build(debug)
    return _CACHE[debug]


def make_inputs(L0_attn_x, L0_attn_W_qkv, L0_attn_b_qkv, L0_ffn_w_gate,
                L0_ffn_w_up, L0_ffn_w_down):
    C, S = _rope_tables()
    perm = np.zeros((128, 128), ml_dtypes.bfloat16)
    for i in range(128):
        perm[i ^ 1, i] = 1.0
    shared = {
        "wqT": np.ascontiguousarray(np.asarray(L0_attn_W_qkv).T),
        "b2": np.ascontiguousarray(np.asarray(L0_attn_b_qkv).reshape(24, 128).T),
        "wgT": np.ascontiguousarray(np.asarray(L0_ffn_w_gate).T),
        "wuT": np.ascontiguousarray(np.asarray(L0_ffn_w_up).T),
        "wdT": np.ascontiguousarray(np.asarray(L0_ffn_w_down).T),
        "Ct": C, "St": S,
        "Pm": perm,
        "Idb": np.eye(128, dtype=ml_dtypes.bfloat16),
        "Idf": np.eye(128, dtype=np.float32),
        "OnesRow": np.ones((1, 128), np.float32),
    }
    x = np.asarray(L0_attn_x)
    return [dict(shared, x=np.ascontiguousarray(x[c])) for c in range(B)]


def kernel(L0_attn_x, L0_attn_W_qkv, L0_attn_b_qkv, L0_ffn_w_gate,
           L0_ffn_w_up, L0_ffn_w_down):
    nc = _get_nc(False)
    in_maps = make_inputs(L0_attn_x, L0_attn_W_qkv, L0_attn_b_qkv,
                          L0_ffn_w_gate, L0_ffn_w_up, L0_ffn_w_down)
    res = run_bass_kernel_spmd(nc, in_maps, core_ids=list(range(B)))
    return np.stack([res.results[c]["out"] for c in range(B)], axis=0)
